# version 23
# baseline (speedup 1.0000x reference)
import contextlib
from collections import defaultdict

import numpy as np
import ml_dtypes
import concourse.bass as bass
import concourse.mybir as mybir
from concourse.bass_utils import run_bass_kernel_spmd

F32 = mybir.dt.float32
BF16 = mybir.dt.bfloat16
AF = mybir.ActivationFunctionType
ALU = mybir.AluOpType
AX = mybir.AxisListType

NEG = -1.0e30
P = 128


def step_chunks(s, CW=1024):
    keys = 256 * (s + 1)
    out = []
    off = 0
    while off < keys:
        w = min(CW, keys - off)
        out.append((off, w))
        off += w
    return out


def build_program(S=4096, D=1024, dbg=False):
    Dc = D // P
    DKV = 2 * D
    T_OWN = S // 256
    SCH = 256
    NCH = S // SCH
    TPC = SCH // P
    CW = 1024
    ST = S // P
    F_TPC = 4
    NFC = T_OWN // F_TPC
    NH = D // 512
    M = T_OWN

    nc = bass.Bass("TRN2", target_bir_lowering=False, detect_race_conditions=False)

    x = nc.dram_tensor("x", [S, D], F32, kind="ExternalInput")
    xq_in = nc.dram_tensor("xq_in", [T_OWN * P, D], F32, kind="ExternalInput")
    wkv_t = nc.dram_tensor("wkv_t", [D, DKV], F32, kind="ExternalInput")
    wr_t = nc.dram_tensor("wr_t", [D, D], F32, kind="ExternalInput")
    wk_t = nc.dram_tensor("wk_t", [D, D], F32, kind="ExternalInput")
    wv_t = nc.dram_tensor("wv_t", [D, D], F32, kind="ExternalInput")
    mu_in = nc.dram_tensor("mu_in", [P, 2 * Dc], F32, kind="ExternalInput")
    masks_in = nc.dram_tensor("masks_in", [P, 256], BF16, kind="ExternalInput")
    tmask_in = nc.dram_tensor("tmask_in", [M, S], BF16, kind="ExternalInput")
    x1bnd_in = nc.dram_tensor("x1bnd_in", [M, D], F32, kind="ExternalInput")
    zmask_in = nc.dram_tensor("zmask_in", [M, 1], F32, kind="ExternalInput")
    out = nc.dram_tensor("out", [T_OWN * P, D], F32, kind="ExternalOutput")
    rb_scr = nc.dram_tensor("rb_scr", [P, 128], BF16, kind="Internal")
    x2_spill = nc.dram_tensor("x2_spill", [T_OWN * P, D], F32,
                              kind="ExternalOutput" if dbg else "Internal")
    if dbg:
        dbg_xnt = nc.dram_tensor("dbg_xnt", [P, 2048], BF16, kind="ExternalOutput")
        dbg_xn1t = nc.dram_tensor("dbg_xn1t", [P, D], BF16, kind="ExternalOutput")
        dbg_KT = nc.dram_tensor("dbg_KT", [P, S], BF16, kind="ExternalOutput")
        dbg_V = nc.dram_tensor("dbg_V", [P, D], BF16, kind="ExternalOutput")
        dbg_Pb = nc.dram_tensor("dbg_Pb", [P, S], BF16, kind="ExternalOutput")
        dbg_st = nc.dram_tensor("dbg_st", [P, 16], F32, kind="ExternalOutput")
        dbg_x2bnd = nc.dram_tensor("dbg_x2bnd", [M, D], F32, kind="ExternalOutput")
        dbg_xn1q = nc.dram_tensor("dbg_xn1q", [P, D], BF16, kind="ExternalOutput")
    ident_dram = nc.inline_tensor(np.eye(P, dtype=ml_dtypes.bfloat16), "ident_const")

    cnt = defaultdict(int)
    sched = {}

    def bump(sem, key, n=1):
        cnt[sem] += n
        if key is not None:
            sched[key] = cnt[sem]
        return cnt[sem]

    bump("c_cst", ("cst",), 16 * 3)
    bump("c_dbg", ("dbg",), 16 * 999)
    bump("c_w1", ("wkv",), 32)

    for ch in range(NCH):
        bump("c_x" + str(ch % 3), ("x", ch), 16)
        for t in range(TPC):
            bump("c_lnst", ("p1st", ch, t))
            bump("c_xn", ("p1xn", ch, t))
            bump("c_xtp", ("p1xtp", ch, t))
            bump("c_xte", ("p1xte", ch, t))
        for dkc in range(Dc):
            bump("c_kmm", ("p1kmm", ch, dkc))
            bump("c_kev", ("p1kev", ch, dkc))
        for t in range(TPC):
            bump("c_vmm", ("p1vmm", ch, t))
            bump("c_vev", ("p1vev", ch, t))

    bump("c_bnd", ("mb",), 16)
    for s in range(T_OWN):
        bump("c_xq" + str(s % 4), ("xq", s), 16)
        bump("c_xnq", ("xnq", s))
        bump("c_qtp", ("qtp", s))
        bump("c_qte", ("qte", s))
        chunks = step_chunks(s, CW)
        for j in range(len(chunks)):
            bump("c_qk", ("qk", s, j))
            bump("c_max", ("max", s, j))
            bump("c_exp", ("exp", s, j))
        if len(chunks) > 1:
            bump("c_fx1", ("fx1", s))
        bump("c_cv", ("cv", s))
        bump("c_pfx", ("pfx", s))
        nkb = 256 * (s + 1) // P
        for g in range((nkb + 7) // 8):
            bump("c_ptp", ("ptp", s, g))
            bump("c_pte", ("pte", s, g))
        bump("c_av", ("av", s))
        bump("c_attb", ("attb", s))
        bump("c_ep2", ("ep2", s))
        bump("c_x2", ("x2", s))
        bump("c_sp" + str(s % 2), ("spill", s), 16)

    NTJ = S // CW
    bump("c_bnd", ("bnd",), 32)
    bump("c_xnq", ("bxn",))
    bump("c_qtp", ("bqtp",))
    bump("c_qte", ("bqte",))
    for j in range(NTJ):
        bump("c_qk", ("bqk", j))
        bump("c_max", ("bmax", j))
        bump("c_exp", ("bexp", j))
    bump("c_fx1", ("bfx1",))
    bump("c_cv", ("bcv",))
    bump("c_pfx", ("bpfx",))
    NTG = (ST + 7) // 8
    for g in range(NTG):
        bump("c_ptp", ("bptp", g))
        bump("c_pte", ("bpte", g))
    bump("c_av", ("bav",))
    bump("c_attb", ("battb",))
    bump("c_ep2", ("bep2",))
    bump("c_x2", ("bx2",))

    bump("c_w2", ("wr",), 32)
    bump("c_w2", ("wk2",), 32)
    bump("c_w2", ("wv2",), 32)
    bump("c_xnq", ("b2xn",))
    bump("c_qtp", ("b2tp",))
    bump("c_qte", ("b2te",))
    for u in range(NFC):
        bump("c_x2r" + str(u % 2), ("x2r", u), 16)
        for t in range(F_TPC):
            bump("c_lnst", ("f_st", u, t))
            bump("c_xn", ("f_xn", u, t))
            bump("c_xtp", ("f_xtp", u, t))
            bump("c_xte", ("f_xte", u, t))
            bump("c_mix", ("f_mix", u, t))
        for t in range(F_TPC):
            bump("c_rmm", ("f_rmm", u, t))
            bump("c_rev", ("f_rev", u, t))
        for dkc in range(Dc):
            bump("c_kamm", ("f_kamm", u, dkc))
            bump("c_kaev", ("f_kaev", u, dkc))
        for t in range(F_TPC):
            bump("c_omm", ("f_omm", u, t))
            bump("c_ffob", ("ffob", u, t))
            bump("c_est", ("f_et", u, t))
            bump("c_oe2", ("f_oe2", u, t))
            bump("c_oe3", ("f_oe3", u, t))
            bump("c_out" + str(t % 2), ("f_out", u, t), 16)

    C = dict(cnt)

    def prev_ps_key(s, j):
        if j >= 2:
            return ("exp", s, j - 2)
        ss = s - 1
        while ss >= 0:
            cand = [jj for jj in range(len(step_chunks(ss, CW))) if jj % 2 == j % 2]
            if cand:
                return ("exp", ss, cand[-1])
            ss -= 1
        return None

    with contextlib.ExitStack() as ctx:
        E = ctx.enter_context
        sems = {name: E(nc.semaphore(name)) for name in C}

        ident = E(nc.sbuf_tensor("ident", [P, P], BF16))
        epsv = E(nc.sbuf_tensor("epsv", [P, 1], F32))
        mu = E(nc.sbuf_tensor("mu", [P, 2 * Dc], F32))
        zmask = E(nc.sbuf_tensor("zmask", [M, 1], F32))
        x2bnd = E(nc.sbuf_tensor("x2bnd", [M, D], F32))
        ssum = E(nc.sbuf_tensor("ssum", [P, 1], F32))
        ssq = E(nc.sbuf_tensor("ssq", [P, 1], F32))
        varep = E(nc.sbuf_tensor("varep", [P, 1], F32))
        lnv = E(nc.sbuf_tensor("lnv", [P, 1], F32))
        negmean = E(nc.sbuf_tensor("negmean", [P, 1], F32))
        rstd = E(nc.sbuf_tensor("rstd", [P, 1], F32))
        nmr = E(nc.sbuf_tensor("nmr", [P, 1], F32))
        spcv = E(nc.sbuf_tensor("spcv", [P, 4], F32))
        spca = E(nc.sbuf_tensor("spca", [P, 1], F32))
        wup = E(nc.sbuf_tensor("wup", [P, 1], F32))
        sqscr = E(nc.sbuf_tensor("sqscr", [P, D], BF16))

        def warm_sc(sc):
            sc.wait_ge(sems["c_cst"], sched[("cst",)])
            sc.activation(sqscr[:, 0:P], ident[:, 0:P], AF.Square, accum_out=wup[:, 0:1])
            sc.activation(sqscr[:, 0:P], ident[:, 0:P], AF.Square, accum_out=wup[:, 0:1])
            sc.activation(spca[:, 0:1], wup[:, 0:1], AF.Ln)
            sc.activation(spca[:, 0:1], spca[:, 0:1], AF.Exp, scale=-0.5)
            sc.activation(spca[:, 0:1], ident[:, 0:1], AF.Identity,
                          bias=wup[:, 0:1], scale=wup[:, 0:1])

        def warm_v(v):
            v.memset(spcv[:, :], 0.0)
            v.wait_ge(sems["c_cst"], sched[("cst",)])
            v.reduce_sum(wup[:, 0:1], ident[:, 0:64], axis=AX.X)
            v.tensor_copy(spcv[:, 0:1], spcv[:, 0:1])
            v.reduce_max(wup[:, 0:1], ident[:, 0:64], axis=AX.X, negate=True)
            v.tensor_copy(spcv[:, 0:1], spcv[:, 0:1])

        RD = 1.0 / D

        def ln_act(sc, src, dst, sem=None, key=None, bufs=None, zsc=None, stats=None):
            st, qq, vp, lv, nme, rs, nr = bufs if bufs is not None else (
                ssum, ssq, varep, lnv, negmean, rstd, nmr)
            n = src.shape[0]

            def spc():
                sc.activation(spca[0:n, 0:1], epsv[0:n, 0:1], AF.Identity)

            if stats is None:
                sc.activation(sqscr[0:n, :], src, AF.Identity, accum_out=st[:, 0:1])
                sc.activation(sqscr[0:n, :], src, AF.Square, accum_out=qq[:, 0:1])
            else:
                st, qq = stats
            sc.activation(vp[:, :], st[:, :], AF.Square, scale=RD)
            sc.activation(lv[:, :], vp[:, :], AF.Identity, scale=-1.0,
                          bias=epsv[0:n, 0:1])
            sc.activation(nme[:, :], st[:, :], AF.Identity, scale=-RD)
            sc.activation(vp[:, :], qq[:, :], AF.Ln, scale=RD, bias=lv[:, 0:1])
            sc.activation(rs[:, :], vp[:, :], AF.Exp, scale=-0.5)
            if zsc is not None:
                spc()
                sc.activation(lv[:, :], rs[:, :], AF.Identity, scale=zsc)
                spc()
                rs = lv
            sc.activation(nr[:, :], nme[:, :], AF.Identity, scale=rs[:, 0:1])
            spc()
            sc.activation(dst, src, AF.Identity, bias=nr[:, 0:1],
                          scale=rs[:, 0:1])
            spc()
            ins = sc.activation(spca[0:n, 0:1], epsv[0:n, 0:1], AF.Identity)
            if sem is not None:
                ins.then_inc(sems[sem], 1)

        actx = contextlib.ExitStack()
        EA = actx.enter_context
        KT = EA(nc.sbuf_tensor("KT", [P, Dc * S], BF16))
        V = EA(nc.sbuf_tensor("V", [P, ST * D], BF16))

        with contextlib.ExitStack() as p1:
            E1 = p1.enter_context
            WKV = E1(nc.sbuf_tensor("WKV", [P, Dc * DKV], BF16))
            xbuf = E1(nc.sbuf_tensor("xbuf", [P, 3 * TPC * D], F32))
            sqv1 = E1(nc.sbuf_tensor("sqv1", [P, D], F32))
            lnS1 = E1(nc.sbuf_tensor("lnS1", [P, 8], F32))
            xnt_roll = E1(nc.sbuf_tensor("xnt_roll", [P, 2 * Dc * SCH], BF16))
            xn1q_t = E1(nc.sbuf_tensor("xn1q_t", [P, D], BF16))
            pXT = [E1(nc.psum_tensor(f"pXT{i}", [P, Dc * P], BF16)) for i in range(2)]
            pKT = [E1(nc.psum_tensor(f"pKT{i}", [P, SCH], F32)) for i in range(2)]
            pV = [E1(nc.psum_tensor(f"pV{i}", [P, D], F32)) for i in range(2)]
            blk = E1(nc.Block())
            @blk.sync
            def _(sync):
                sync.dma_start(ident[:, :], ident_dram.ap()).then_inc(sems["c_cst"], 16)
                sync.dma_start(mu[:, :], mu_in.ap()).then_inc(sems["c_cst"], 16)
                sync.dma_start(zmask[:, :], zmask_in.ap()).then_inc(sems["c_cst"], 16)
                for ch in range(NCH):
                    if ch >= 3:
                        sync.wait_ge(sems["c_xn"], sched[("p1xn", ch - 3, TPC - 1)])
                    src = x.ap()[ch * SCH:(ch + 1) * SCH, :].rearrange(
                        "(t p) d -> p t d", p=P)
                    dst = xbuf[:, (ch % 3) * TPC * D:((ch % 3) + 1) * TPC * D
                               ].rearrange("p (t d) -> p t d", t=TPC)
                    sync.dma_start(dst, src).then_inc(sems["c_x" + str(ch % 3)], 16)

            @blk.gpsimd
            def _(g):
                g.dma_start(WKV[:, :].rearrange("p (c w) -> p c w", c=Dc),
                            wkv_t.ap().rearrange("(c p) w -> p c w", p=P)
                            ).then_inc(sems["c_w1"], 16)
                g.dma_start(rb_scr.ap()[:, 0:32], WKV[:, 0:32]
                            ).then_inc(sems["c_w1"], 16)
                if dbg:
                    g.wait_ge(sems["c_xn"], sched[("p1xn", 0, 0)])
                    g.dma_start(dbg_xn1t.ap(), xn1q_t[:, :]).then_inc(sems["c_dbg"], 16)
                    g.wait_ge(sems["c_xte"], sched[("p1xte", 0, TPC - 1)])
                    g.dma_start(dbg_xnt.ap(), xnt_roll[:, 0:Dc * SCH]
                                ).then_inc(sems["c_dbg"], 16)

            @blk.vector
            def _(v):
                v.memset(epsv[:, :], 1e-5)
                warm_v(v)
                def p1_part1(ch, t):
                    idx = ch * TPC + t
                    if idx >= 4:
                        pc, pt = divmod(idx - 4, TPC)
                        v.wait_ge(sems["c_xn"], sched[("p1xn", pc, pt)])
                    v.wait_ge(sems["c_x" + str(ch % 3)], sched[("x", ch)])
                    xt = xbuf[:, (ch % 3) * TPC * D + t * D:
                              (ch % 3) * TPC * D + (t + 1) * D]
                    c0 = (idx % 4) * 2
                    v.tensor_mul(sqv1[:, :], xt, xt)
                    v.reduce_sum(lnS1[:, c0:c0 + 1], xt, axis=AX.X)
                    v.reduce_sum(lnS1[:, c0 + 1:c0 + 2], sqv1[:, :], axis=AX.X)
                    v.tensor_copy(spcv[:, 1:3], spcv[:, 1:3]
                                  ).then_inc(sems["c_lnst"], 1)

                for t in range(TPC):
                    p1_part1(0, t)
                for ch in range(NCH):
                    for t in range(TPC):
                        if t == 0 and ch >= 2:
                            v.wait_ge(sems["c_vmm"], sched[("p1vmm", ch - 2, TPC - 1)])
                        if dbg and ch == 2 and t == 0:
                            v.wait_ge(sems["c_dbg"], 32)
                        v.wait_ge(sems["c_xtp"], sched[("p1xtp", ch, t)])
                        base = (ch % 2) * Dc * SCH
                        dst = xnt_roll[:, base:base + Dc * SCH].rearrange(
                            "p (c s) -> p c s", c=Dc)[:, :, t * P:(t + 1) * P]
                        src = pXT[t % 2][:, :].rearrange("p (c q) -> p c q", c=Dc)
                        v.tensor_copy(dst, src)
                        v.tensor_copy(spcv[:, 3:4], epsv[:, 0:1]
                                      ).then_inc(sems["c_xte"], 1)
                    if ch + 1 < NCH:
                        for t in range(TPC):
                            p1_part1(ch + 1, t)
                    for t in range(TPC):
                        v.wait_ge(sems["c_vmm"], sched[("p1vmm", ch, t)])
                        stile = ch * TPC + t
                        v.tensor_copy(V[:, stile * D:(stile + 1) * D],
                                      pV[t % 2][:, :]).then_inc(sems["c_vev"], 1)

            @blk.scalar
            def _(sc):
                warm_sc(sc)
                for ch in range(NCH):
                    for t in range(TPC):
                        idx = ch * TPC + t
                        if idx > 0:
                            pch, pt = divmod(idx - 1, TPC)
                            sc.wait_ge(sems["c_xtp"], sched[("p1xtp", pch, pt)])
                        xt = xbuf[:, (ch % 3) * TPC * D + t * D:
                                  (ch % 3) * TPC * D + (t + 1) * D]
                        if dbg and ch == 0 and t == 1:
                            sc.wait_ge(sems["c_dbg"], 16)
                        sc.wait_ge(sems["c_lnst"], sched[("p1st", ch, t)])
                        c0 = ((ch * TPC + t) % 4) * 2
                        ln_act(sc, xt, xn1q_t[:, :], "c_xn", ("p1xn", ch, t),
                               stats=(lnS1[:, c0:c0 + 1], lnS1[:, c0 + 1:c0 + 2]))
                    for dkc in range(Dc):
                        sc.wait_ge(sems["c_kmm"], sched[("p1kmm", ch, dkc)])
                        sc.copy(KT[:, dkc * S + ch * SCH: dkc * S + (ch + 1) * SCH],
                                pKT[dkc % 2][:, :]).then_inc(sems["c_kev"], 1)

            @blk.tensor
            def _(te):
                te.wait_ge(sems["c_cst"], sched[("cst",)])
                te.wait_ge(sems["c_w1"], sched[("wkv",)])
                for ch in range(NCH):
                    for t in range(TPC):
                        idx = ch * TPC + t
                        te.wait_ge(sems["c_xn"], sched[("p1xn", ch, t)])
                        if idx >= 2:
                            pch, pt = divmod(idx - 2, TPC)
                            te.wait_ge(sems["c_xte"], sched[("p1xte", pch, pt)])
                        for cc in range(Dc):
                            ins = te.transpose(
                                pXT[t % 2][:, cc * P:(cc + 1) * P],
                                xn1q_t[:, cc * P:(cc + 1) * P], ident[:, :])
                        ins.then_inc(sems["c_xtp"], 1)
                    te.wait_ge(sems["c_xte"], sched[("p1xte", ch, TPC - 1)])
                    rbase = (ch % 2) * Dc * SCH
                    for dkc in range(Dc):
                        kidx = ch * Dc + dkc
                        if kidx >= 2:
                            pch, pd = divmod(kidx - 2, Dc)
                            te.wait_ge(sems["c_kev"], sched[("p1kev", pch, pd)])
                        for cc in range(Dc):
                            ins = te.matmul(
                                pKT[dkc % 2][:, :],
                                WKV[:, cc * DKV + dkc * P: cc * DKV + (dkc + 1) * P],
                                xnt_roll[:, rbase + cc * SCH: rbase + (cc + 1) * SCH],
                                start=(cc == 0), stop=(cc == Dc - 1))
                        ins.then_inc(sems["c_kmm"], 1)
                    for t in range(TPC):
                        idx = ch * TPC + t
                        if idx >= 2:
                            pch, pt = divmod(idx - 2, TPC)
                            te.wait_ge(sems["c_vev"], sched[("p1vev", pch, pt)])
                        for cc in range(Dc):
                            for h in range(NH):
                                ins = te.matmul(
                                    pV[t % 2][:, h * 512:(h + 1) * 512],
                                    xnt_roll[:, rbase + cc * SCH + t * P:
                                             rbase + cc * SCH + (t + 1) * P],
                                    WKV[:, cc * DKV + D + h * 512:
                                        cc * DKV + D + (h + 1) * 512],
                                    start=(cc == 0), stop=(cc == Dc - 1))
                        ins.then_inc(sems["c_vmm"], 1)

        NTJ = S // CW
        NTG = (ST + 7) // 8
        with contextlib.ExitStack() as p2:
            E2 = p2.enter_context
            maskb = E2(nc.sbuf_tensor("maskb", [P, 256], BF16))
            tmaskb = E2(nc.sbuf_tensor("tmaskb", [M, S], BF16))
            x1bnd = E2(nc.sbuf_tensor("x1bnd", [M, D], F32))
            xn1bnd = E2(nc.sbuf_tensor("xn1bnd", [M, D], BF16))
            Pbt = E2(nc.sbuf_tensor("Pbt", [M, S], BF16))
            negmaxt = E2(nc.sbuf_tensor("negmaxt", [M, 4], F32))
            lvect = E2(nc.sbuf_tensor("lvect", [M, 4], F32))
            cvect = E2(nc.sbuf_tensor("cvect", [M, 4], F32))
            cvect2 = E2(nc.sbuf_tensor("cvect2", [M, 4], F32))
            negMt = E2(nc.sbuf_tensor("negMt", [M, 1], F32))
            lsumt = E2(nc.sbuf_tensor("lsumt", [M, 1], F32))
            rinvt = E2(nc.sbuf_tensor("rinvt", [M, 1], F32))
            ssumt = E2(nc.sbuf_tensor("ssumt", [M, 1], F32))
            ssqt = E2(nc.sbuf_tensor("ssqt", [M, 1], F32))
            varept = E2(nc.sbuf_tensor("varept", [M, 1], F32))
            lnvt = E2(nc.sbuf_tensor("lnvt", [M, 1], F32))
            negmeant = E2(nc.sbuf_tensor("negmeant", [M, 1], F32))
            rstdt = E2(nc.sbuf_tensor("rstdt", [M, 1], F32))
            nmrt = E2(nc.sbuf_tensor("nmrt", [M, 1], F32))
            tbufs = (ssumt, ssqt, varept, lnvt, negmeant, rstdt, nmrt)

            x1q = E2(nc.sbuf_tensor("x1q", [P, 4 * D], F32))
            xn1q = E2(nc.sbuf_tensor("xn1q", [P, 3 * D], BF16))
            QT = E2(nc.sbuf_tensor("QT", [P, 2 * Dc * P], BF16))
            Pb = E2(nc.sbuf_tensor("Pb", [P, S + 2 * CW], BF16))
            PT = E2(nc.sbuf_tensor("PT", [P, 2 * 8 * P], BF16))
            negmax = E2(nc.sbuf_tensor("negmax", [P, 8], F32))
            lvec = E2(nc.sbuf_tensor("lvec", [P, 8], F32))
            cvec = E2(nc.sbuf_tensor("cvec", [P, 8], F32))
            cvec2 = E2(nc.sbuf_tensor("cvec2", [P, 4], F32))
            negM = E2(nc.sbuf_tensor("negM", [P, 1], F32))
            lsum = E2(nc.sbuf_tensor("lsum", [P, 2], F32))
            rinv = E2(nc.sbuf_tensor("rinv", [P, 2], F32))
            attb = E2(nc.sbuf_tensor("attb", [P, D], F32))
            lnb = E2(nc.sbuf_tensor("lnb", [P, D], F32))
            pS = [E2(nc.psum_tensor(f"pS{i}", [P, CW], F32)) for i in range(2)]
            pPT = [E2(nc.psum_tensor(f"pPT{i}", [P, CW], BF16)) for i in range(2)]
            pAV = E2(nc.psum_tensor("pAV", [P, D], F32))

            def pb_col(ss, off):
                return (ss % 2) * 2 * CW + off if off < 2 * CW else 2 * CW + off

            blk = E2(nc.Block())
            @blk.sync
            def _(sync):
                sync.dma_start(maskb[:, :], masks_in.ap()).then_inc(sems["c_bnd"], 16)
                sync.dma_start(tmaskb[:, :], tmask_in.ap()).then_inc(sems["c_bnd"], 16)
                sync.dma_start(x1bnd[:, :], x1bnd_in.ap()).then_inc(sems["c_bnd"], 16)
                for s in range(T_OWN):
                    if s >= 4:
                        sync.wait_ge(sems["c_x2"], sched[("x2", s - 4)])
                        sync.wait_ge(sems["c_sp" + str((s - 4) % 2)],
                                     sched[("spill", s - 4)])
                    sync.dma_start(x1q[:, (s % 4) * D:((s % 4) + 1) * D],
                                   xq_in.ap()[s * P:(s + 1) * P, :]
                                   ).then_inc(sems["c_xq" + str(s % 4)], 16)

            @blk.gpsimd
            def _(g):
                for s in range(T_OWN):
                    g.wait_ge(sems["c_x2"], sched[("x2", s)])
                    g.dma_start(x2_spill.ap()[s * P:(s + 1) * P, :],
                                x1q[:, (s % 4) * D:((s % 4) + 1) * D]
                                ).then_inc(sems["c_sp" + str(s % 2)], 16)

            @blk.vector
            def _(v):
                warm_v(v)
                v.wait_ge(sems["c_bnd"], sched[("mb",)])

                def mask_j(ss, j, off, w):
                    v.wait_ge(sems["c_qk"], sched[("qk", ss, j)])
                    if j == len(step_chunks(ss, CW)) - 1:
                        v.tensor_add(pS[j % 2][:, w - 256:w],
                                     pS[j % 2][:, w - 256:w],
                                     maskb[:, 0:256])
                    nb = (ss % 2) * 4
                    v.reduce_max(negmax[:, nb + j:nb + j + 1],
                                 pS[j % 2][:, 0:w], axis=AX.X, negate=True)
                    v.tensor_copy(spcv[:, 1:3], spcv[:, 1:3]
                                  ).then_inc(sems["c_max"], 1)

                def qt_evac(ss):
                    v.wait_ge(sems["c_qtp"], sched[("qtp", ss)])
                    v.tensor_copy(QT[:, (ss % 2) * Dc * P:(ss % 2 + 1) * Dc * P],
                                  pPT[0][:, 0:Dc * P])
                    v.tensor_copy(spcv[:, 1:3], spcv[:, 1:3])
                    v.tensor_copy(spcv[:, 1:3], spcv[:, 1:3]
                                  ).then_inc(sems["c_qte"], 1)

                def x2add(k):
                    v.wait_ge(sems["c_ep2"], sched[("ep2", k)])
                    v.tensor_add(x1q[:, (k % 4) * D:(k % 4 + 1) * D],
                                 x1q[:, (k % 4) * D:(k % 4 + 1) * D],
                                 lnb[:, :]).then_inc(sems["c_x2"], 1)

                def fixup(ss):
                    chunks = step_chunks(ss, CW)
                    n = len(chunks)
                    nb = (ss % 2) * 4
                    if n > 1:
                        v.wait_ge(sems["c_max"], sched[("max", ss, n - 1)])
                        v.tensor_copy(spcv[:, 3:4], epsv[:, 0:1])
                        v.tensor_reduce(negM[:, :], negmax[:, nb:nb + n],
                                        AX.X, ALU.min)
                        v.tensor_copy(spcv[:, 1:3], spcv[:, 1:3]
                                      ).then_inc(sems["c_fx1"], 1)

                def pscale(ss):
                    chunks = step_chunks(ss, CW)
                    n = len(chunks)
                    nb = (ss % 2) * 4
                    v.wait_ge(sems["c_cv"], sched[("cv", ss)])
                    if n > 1:
                        for j, (off, w) in enumerate(chunks):
                            c0 = pb_col(ss, off)
                            v.tensor_scalar_mul(Pb[:, c0:c0 + w],
                                                Pb[:, c0:c0 + w],
                                                cvec[:, nb + j:nb + j + 1])
                    v.reciprocal(rinv[:, (ss % 2):(ss % 2) + 1],
                                 lsum[:, (ss % 2):(ss % 2) + 1])
                    v.tensor_copy(spcv[:, 1:3], spcv[:, 1:3]
                                  ).then_inc(sems["c_pfx"], 1)

                qt_evac(0)
                for j, (off, w) in enumerate(step_chunks(0, CW)):
                    mask_j(0, j, off, w)
                fixup(0)
                pscale(0)
                for s in range(T_OWN):
                    nxt = step_chunks(s + 1, CW) if s + 1 < T_OWN else None
                    if nxt is not None:
                        qt_evac(s + 1)
                        for j in range(len(nxt)):
                            mask_j(s + 1, j, *nxt[j])
                        fixup(s + 1)
                    else:
                        v.wait_ge(sems["c_bnd"], sched[("bnd",)])
                        v.wait_ge(sems["c_qtp"], sched[("bqtp",)])
                        v.tensor_copy(QT[:, 0:Dc * M], pPT[0][:, 0:Dc * M])
                        v.tensor_copy(spcv[:, 1:3], spcv[:, 1:3])
                        v.tensor_copy(spcv[:, 1:3], spcv[:, 1:3]
                                      ).then_inc(sems["c_qte"], 1)
                        for j in range(min(2, NTJ)):
                            v.wait_ge(sems["c_qk"], sched[("bqk", j)])
                            v.tensor_add(pS[j % 2][0:M, :], pS[j % 2][0:M, :],
                                         tmaskb[:, j * CW:(j + 1) * CW])
                            v.reduce_max(negmaxt[:, j:j + 1], pS[j % 2][0:M, :],
                                         axis=AX.X, negate=True)
                            v.tensor_copy(spcv[:, 1:3], spcv[:, 1:3]
                                          ).then_inc(sems["c_max"], 1)
                    nkb = 256 * (s + 1) // P
                    for g in range((nkb + 7) // 8):
                        kb0, kbn = g * 8, min(8, nkb - g * 8)
                        v.wait_ge(sems["c_ptp"], sched[("ptp", s, g)])
                        v.tensor_copy(PT[:, (g % 2) * 8 * P:(g % 2) * 8 * P + kbn * P],
                                      pPT[g % 2][:, 0:kbn * P])
                        v.tensor_copy(spcv[:, 1:3], spcv[:, 1:3]
                                      ).then_inc(sems["c_pte"], 1)
                    if nxt is not None:
                        pscale(s + 1)
                    else:
                        for j in range(2, NTJ):
                            v.wait_ge(sems["c_qk"], sched[("bqk", j)])
                            v.tensor_add(pS[j % 2][0:M, :], pS[j % 2][0:M, :],
                                         tmaskb[:, j * CW:(j + 1) * CW])
                            v.reduce_max(negmaxt[:, j:j + 1], pS[j % 2][0:M, :],
                                         axis=AX.X, negate=True)
                            v.tensor_copy(spcv[:, 1:3], spcv[:, 1:3]
                                          ).then_inc(sems["c_max"], 1)
                        v.wait_ge(sems["c_max"], sched[("bmax", NTJ - 1)])
                        v.tensor_copy(spcv[0:M, 3:4], epsv[0:M, 0:1])
                        v.tensor_reduce(negMt[:, :], negmaxt[:, 0:NTJ], AX.X, ALU.min)
                        v.tensor_copy(spcv[:, 1:3], spcv[:, 1:3]
                                      ).then_inc(sems["c_fx1"], 1)
                        v.wait_ge(sems["c_cv"], sched[("bcv",)])
                        for j in range(NTJ):
                            v.tensor_scalar_mul(Pbt[:, j * CW:(j + 1) * CW],
                                                Pbt[:, j * CW:(j + 1) * CW],
                                                cvect[:, j:j + 1])
                        v.reciprocal(rinvt[:, :], lsumt[:, :])
                        v.tensor_copy(spcv[:, 1:3], spcv[:, 1:3]
                                      ).then_inc(sems["c_pfx"], 1)
                    if s >= 1:
                        x2add(s - 1)
                    v.wait_ge(sems["c_av"], sched[("av", s)])
                    v.scalar_tensor_tensor(attb[:, :], pAV[:, :],
                                           rinv[:, (s % 2):(s % 2) + 1],
                                           xn1q[:, (s % 3) * D:(s % 3 + 1) * D],
                                           ALU.mult, ALU.add
                                           ).then_inc(sems["c_attb"], 1)
                for g in range(NTG):
                    kb0, kbn = g * 8, min(8, ST - g * 8)
                    v.wait_ge(sems["c_ptp"], sched[("bptp", g)])
                    v.tensor_copy(PT[:, kb0 * M:(kb0 + kbn) * M],
                                  pPT[g % 2][:, 0:kbn * M])
                    v.tensor_copy(spcv[:, 1:3], spcv[:, 1:3]
                                  ).then_inc(sems["c_pte"], 1)
                x2add(T_OWN - 1)
                v.wait_ge(sems["c_av"], sched[("bav",)])
                v.scalar_tensor_tensor(attb[0:M, 0:D], pAV[0:M, :], rinvt[:, 0:1],
                                       xn1bnd[:, :], ALU.mult, ALU.add
                                       ).then_inc(sems["c_attb"], 1)
                v.wait_ge(sems["c_ep2"], sched[("bep2",)])
                v.tensor_add(x2bnd[:, :], x1bnd[:, :], lnb[0:M, 0:D]
                             ).then_inc(sems["c_x2"], 1)

            @blk.scalar
            def _(sc):
                warm_sc(sc)

                def emit_exps(ss):
                    chunks = step_chunks(ss, CW)
                    nb = (ss % 2) * 4
                    for j, (off, w) in enumerate(chunks):
                        if j == 2:
                            png = (256 * ss // P + 7) // 8
                            sc.wait_ge(sems["c_ptp"], sched[("ptp", ss - 1, png - 1)])
                        sc.wait_ge(sems["c_max"], sched[("max", ss, j)])
                        c0 = pb_col(ss, off)
                        sc.activation(Pb[:, c0:c0 + w],
                                      pS[j % 2][:, 0:w], AF.Exp,
                                      bias=negmax[:, nb + j:nb + j + 1],
                                      accum_out=lvec[:, nb + j:nb + j + 1]
                                      ).then_inc(sems["c_exp"], 1)

                def emit_cv(ss):
                    chunks = step_chunks(ss, CW)
                    n = len(chunks)
                    nb = (ss % 2) * 4
                    if n > 1:
                        sc.wait_ge(sems["c_fx1"], sched[("fx1", ss)])
                        sc.activation(cvec[:, nb:nb + n], negmax[:, nb:nb + n],
                                      AF.Exp, scale=-1.0, bias=negM[:, 0:1])
                        sc.activation(spca[:, 0:1], epsv[:, 0:1], AF.Identity)
                        for j in range(n):
                            sc.activation(cvec2[:, j:j + 1],
                                          lvec[:, nb + j:nb + j + 1],
                                          AF.Identity, scale=cvec[:, nb + j:nb + j + 1])
                        sc.activation(spca[:, 0:1], epsv[:, 0:1], AF.Identity)
                        sc.activation(spca[:, 0:1], epsv[:, 0:1], AF.Identity)
                        sc.activation(lsum[:, (ss % 2):(ss % 2) + 1], cvec2[:, 0:1],
                                      AF.Identity, bias=cvec2[:, 1:2])
                        for j in range(2, n):
                            sc.activation(lsum[:, (ss % 2):(ss % 2) + 1],
                                          lsum[:, (ss % 2):(ss % 2) + 1],
                                          AF.Identity, bias=cvec2[:, j:j + 1])
                        sc.activation(spca[:, 0:1], epsv[:, 0:1], AF.Identity
                                      ).then_inc(sems["c_cv"], 1)
                    else:
                        sc.activation(lsum[:, (ss % 2):(ss % 2) + 1],
                                      lvec[:, nb:nb + 1], AF.Identity)
                        sc.activation(spca[:, 0:1], epsv[:, 0:1], AF.Identity
                                      ).then_inc(sems["c_cv"], 1)

                def emit_xnq(ss):
                    if ss >= 3:
                        sc.wait_ge(sems["c_attb"], sched[("attb", ss - 3)])
                    sc.wait_ge(sems["c_xq" + str(ss % 4)], sched[("xq", ss)])
                    ln_act(sc, x1q[:, (ss % 4) * D:(ss % 4 + 1) * D],
                           xn1q[:, (ss % 3) * D:(ss % 3 + 1) * D],
                           "c_xnq", ("xnq", ss))

                emit_xnq(0)
                emit_exps(0)
                emit_cv(0)
                emit_xnq(1)
                for s in range(T_OWN):
                    if s + 1 < T_OWN:
                        emit_exps(s + 1)
                        emit_cv(s + 1)
                    else:
                        for j in range(NTJ):
                            sc.wait_ge(sems["c_max"], sched[("bmax", j)])
                            sc.activation(Pbt[:, j * CW:(j + 1) * CW],
                                          pS[j % 2][0:M, :], AF.Exp,
                                          bias=negmaxt[:, j:j + 1],
                                          accum_out=lvect[:, j:j + 1]
                                          ).then_inc(sems["c_exp"], 1)
                        sc.wait_ge(sems["c_fx1"], sched[("bfx1",)])
                        sc.activation(cvect[:, 0:NTJ], negmaxt[:, 0:NTJ], AF.Exp,
                                      scale=-1.0, bias=negMt[:, 0:1])
                        sc.activation(spca[0:M, 0:1], epsv[0:M, 0:1], AF.Identity)
                        for j in range(NTJ):
                            sc.activation(cvect2[:, j:j + 1], lvect[:, j:j + 1],
                                          AF.Identity, scale=cvect[:, j:j + 1])
                        sc.activation(spca[0:M, 0:1], epsv[0:M, 0:1], AF.Identity)
                        sc.activation(spca[0:M, 0:1], epsv[0:M, 0:1], AF.Identity)
                        sc.activation(lsumt[:, :], cvect2[:, 0:1], AF.Identity,
                                      bias=cvect2[:, 1:2])
                        for j in range(2, NTJ):
                            sc.activation(lsumt[:, :], lsumt[:, :], AF.Identity,
                                          bias=cvect2[:, j:j + 1])
                        sc.activation(spca[0:M, 0:1], epsv[0:M, 0:1], AF.Identity
                                      ).then_inc(sems["c_cv"], 1)
                    if s + 2 < T_OWN:
                        emit_xnq(s + 2)
                    elif s + 2 == T_OWN:
                        sc.wait_ge(sems["c_bnd"], sched[("bnd",)])
                        ln_act(sc, x1bnd[:, :], xn1bnd[:, :], "c_xnq", ("bxn",),
                               bufs=tbufs)
                    sc.wait_ge(sems["c_attb"], sched[("attb", s)])
                    if s >= 1:
                        sc.wait_ge(sems["c_x2"], sched[("x2", s - 1)])
                    ln_act(sc, attb[:, :], lnb[:, :], "c_ep2", ("ep2", s))
                sc.wait_ge(sems["c_attb"], sched[("battb",)])
                sc.wait_ge(sems["c_x2"], sched[("x2", T_OWN - 1)])
                ln_act(sc, attb[0:M, 0:D], lnb[0:M, 0:D], "c_ep2", ("bep2",),
                       bufs=tbufs)

            @blk.tensor
            def _(te):
                def emit_qt_t(ss):
                    te.wait_ge(sems["c_xnq"], sched[("xnq", ss)])
                    for cc in range(Dc):
                        ins = te.transpose(pPT[0][:, cc * P:(cc + 1) * P],
                                           xn1q[:, (ss % 3) * D + cc * P:
                                                (ss % 3) * D + (cc + 1) * P],
                                           ident[:, :])
                    ins.then_inc(sems["c_qtp"], 1)

                def emit_bqk(j):
                    if j >= 2:
                        te.wait_ge(sems["c_exp"], sched[("bexp", j - 2)])
                    else:
                        pk = prev_ps_key(T_OWN, j)
                        if pk is not None:
                            te.wait_ge(sems["c_exp"], sched[pk])
                    for cc in range(Dc):
                        for nn in range(0, CW, 512):
                            ins = te.matmul(
                                pS[j % 2][0:M, nn:nn + 512],
                                QT[:, cc * M:(cc + 1) * M],
                                KT[:, cc * S + j * CW + nn:
                                   cc * S + j * CW + nn + 512],
                                start=(cc == 0), stop=(cc == Dc - 1),
                                skip_group_check=True)
                    ins.then_inc(sems["c_qk"], 1)

                def emit_qk(ss, j, off, w):
                    pk = prev_ps_key(ss, j)
                    if pk is not None:
                        te.wait_ge(sems["c_exp"], sched[pk])
                    for cc in range(Dc):
                        for nn in range(0, w, 512):
                            nw = min(512, w - nn)
                            ins = te.matmul(
                                pS[j % 2][:, nn:nn + nw],
                                QT[:, (ss % 2) * Dc * P + cc * P:
                                   (ss % 2) * Dc * P + (cc + 1) * P],
                                KT[:, cc * S + off + nn: cc * S + off + nn + nw],
                                start=(cc == 0), stop=(cc == Dc - 1))
                    ins.then_inc(sems["c_qk"], 1)

                emit_qt_t(0)
                te.wait_ge(sems["c_qte"], sched[("qte", 0)])
                for j, (off, w) in enumerate(step_chunks(0, CW)):
                    emit_qk(0, j, off, w)
                for s in range(T_OWN):
                    if s + 1 < T_OWN:
                        emit_qt_t(s + 1)
                        te.wait_ge(sems["c_qte"], sched[("qte", s + 1)])
                        for j, (off, w) in enumerate(step_chunks(s + 1, CW)):
                            emit_qk(s + 1, j, off, w)
                    else:
                        te.wait_ge(sems["c_xnq"], sched[("bxn",)])
                        for cc in range(Dc):
                            ins = te.transpose(pPT[0][:, cc * M:(cc + 1) * M],
                                               xn1bnd[:, cc * P:(cc + 1) * P],
                                               ident[0:M, 0:M])
                        ins.then_inc(sems["c_qtp"], 1)
                        te.wait_ge(sems["c_qte"], sched[("bqte",)])
                        for j in range(min(2, NTJ)):
                            emit_bqk(j)
                    te.wait_ge(sems["c_pfx"], sched[("pfx", s)])
                    nkb = 256 * (s + 1) // P
                    ngr = (nkb + 7) // 8

                    def tp_group(g):
                        kb0, kbn = g * 8, min(8, nkb - g * 8)
                        if g >= 2:
                            te.wait_ge(sems["c_pte"], sched[("pte", s, g - 2)])
                        for b in range(kbn):
                            kb = kb0 + b
                            c0 = pb_col(s, kb * P)
                            ins = te.transpose(
                                pPT[g % 2][:, b * P:(b + 1) * P],
                                Pb[:, c0:c0 + P], ident[:, :])
                        ins.then_inc(sems["c_ptp"], 1)

                    tp_group(0)
                    if ngr > 1:
                        tp_group(1)
                    for g in range(ngr):
                        kb0, kbn = g * 8, min(8, nkb - g * 8)
                        te.wait_ge(sems["c_pte"], sched[("pte", s, g)])
                        if g == 0 and s >= 1:
                            te.wait_ge(sems["c_attb"], sched[("attb", s - 1)])
                        for b in range(kbn):
                            kb = kb0 + b
                            for h in range(NH):
                                ins = te.matmul(
                                    pAV[:, h * 512:(h + 1) * 512],
                                    PT[:, (g % 2) * 8 * P + b * P:
                                       (g % 2) * 8 * P + (b + 1) * P],
                                    V[:, kb * D + h * 512: kb * D + (h + 1) * 512],
                                    start=(kb == 0), stop=(kb == nkb - 1),
                                    skip_group_check=True)
                        if g + 2 < ngr:
                            tp_group(g + 2)
                    ins.then_inc(sems["c_av"], 1)

                for j in range(2, NTJ):
                    emit_bqk(j)
                te.wait_ge(sems["c_pfx"], sched[("bpfx",)])

                def ttp(g):
                    kb0, kbn = g * 8, min(8, ST - g * 8)
                    if g >= 2:
                        te.wait_ge(sems["c_pte"], sched[("bpte", g - 2)])
                    for b in range(kbn):
                        kb = kb0 + b
                        ins = te.transpose(pPT[g % 2][:, b * M:(b + 1) * M],
                                           Pbt[:, kb * P:(kb + 1) * P],
                                           ident[0:M, 0:M])
                    ins.then_inc(sems["c_ptp"], 1)

                ttp(0)
                if NTG > 1:
                    ttp(1)
                for g in range(NTG):
                    kb0, kbn = g * 8, min(8, ST - g * 8)
                    te.wait_ge(sems["c_pte"], sched[("bpte", g)])
                    if g == 0:
                        te.wait_ge(sems["c_attb"], sched[("attb", T_OWN - 1)])
                    for b in range(kbn):
                        kb = kb0 + b
                        for h in range(NH):
                            ins = te.matmul(
                                pAV[0:M, h * 512:(h + 1) * 512],
                                PT[:, kb * M:(kb + 1) * M],
                                V[:, kb * D + h * 512: kb * D + (h + 1) * 512],
                                start=(kb == 0), stop=(kb == ST - 1),
                                skip_group_check=True)
                    if g + 2 < NTG:
                        ttp(g + 2)
                ins.then_inc(sems["c_av"], 1)

            if dbg:
                @blk.gpsimd
                def _(g):
                    g.wait_ge(sems["c_x2"], sched[("x2", T_OWN - 1)])
                    g.dma_start(dbg_KT.ap(), KT[:, 0:S]).then_inc(sems["c_dbg"], 16)
                    g.dma_start(dbg_V.ap(), V[:, 0:D]).then_inc(sems["c_dbg"], 16)
                    g.dma_start(dbg_Pb.ap(), Pb[:, 0:S]).then_inc(sems["c_dbg"], 16)
                    g.dma_start(dbg_xn1q.ap(), xn1q[:, D:2 * D]).then_inc(sems["c_dbg"], 16)
                    g.dma_start(dbg_st.ap()[:, 0:4], negmax[:, 0:4]).then_inc(sems["c_dbg"], 16)
                    g.dma_start(dbg_st.ap()[:, 4:8], lvec[:, 0:4]).then_inc(sems["c_dbg"], 16)
                    g.dma_start(dbg_st.ap()[:, 8:12], cvec[:, 0:4]).then_inc(sems["c_dbg"], 16)
                    g.dma_start(dbg_st.ap()[:, 12:13], rinv[:, 0:1], allow_slow_non_contiguous=True).then_inc(sems["c_dbg"], 16)
                    g.dma_start(dbg_st.ap()[:, 13:14], lsum[:, 0:1], allow_slow_non_contiguous=True).then_inc(sems["c_dbg"], 16)
                    g.wait_ge(sems["c_dbg"], 16 * 9)

        actx.close()

        FW = F_TPC * P
        with contextlib.ExitStack() as p3:
            E4 = p3.enter_context
            WR = E4(nc.sbuf_tensor("WR", [P, Dc * D], BF16))
            WK2 = E4(nc.sbuf_tensor("WK2", [P, Dc * D], BF16))
            WV2 = E4(nc.sbuf_tensor("WV2", [P, Dc * D], BF16))
            x2r = E4(nc.sbuf_tensor("x2r", [P, 2 * F_TPC * D], F32))
            xn2t = E4(nc.sbuf_tensor("xn2t", [P, 2 * D], BF16))
            xn2bndT = E4(nc.sbuf_tensor("xn2bndT", [P, Dc * M], BF16))
            XN2T = E4(nc.sbuf_tensor("XN2T", [P, Dc * FW], BF16))
            XN2ST = E4(nc.sbuf_tensor("XN2ST", [P, Dc * FW], BF16))
            XRT = E4(nc.sbuf_tensor("XRT", [P, Dc * FW], BF16))
            XKT = E4(nc.sbuf_tensor("XKT", [P, Dc * FW], BF16))
            rbuf = E4(nc.sbuf_tensor("rbuf", [P, F_TPC * D], BF16))
            kactT = E4(nc.sbuf_tensor("kactT", [P, Dc * FW], BF16))
            ffob = E4(nc.sbuf_tensor("ffob", [P, 4 * D], F32))
            lnb3 = E4(nc.sbuf_tensor("lnb3", [P, 2 * D], F32))
            outb = E4(nc.sbuf_tensor("outb", [P, 2 * D], F32))
            xn2bnd = E4(nc.sbuf_tensor("xn2bnd", [M, D], BF16))
            ssumt2 = E4(nc.sbuf_tensor("ssumt2", [M, 1], F32))
            ssqt2 = E4(nc.sbuf_tensor("ssqt2", [M, 1], F32))
            varept2 = E4(nc.sbuf_tensor("varept2", [M, 1], F32))
            lnvt2 = E4(nc.sbuf_tensor("lnvt2", [M, 1], F32))
            negmeant2 = E4(nc.sbuf_tensor("negmeant2", [M, 1], F32))
            rstdt2 = E4(nc.sbuf_tensor("rstdt2", [M, 1], F32))
            nmrt2 = E4(nc.sbuf_tensor("nmrt2", [M, 1], F32))
            sqv = E4(nc.sbuf_tensor("sqv", [P, D], F32))
            sqg = E4(nc.sbuf_tensor("sqg", [P, D], F32))
            lnS = E4(nc.sbuf_tensor("lnS", [P, 8], F32))
            eS = E4(nc.sbuf_tensor("eS", [P, 8], F32))
            zerov = E4(nc.sbuf_tensor("zerov", [P, 1], F32))
            pscr = E4(nc.sbuf_tensor("pscr", [P, 32], F32))
            pX2T = [E4(nc.psum_tensor(f"pX2T{i}", [P, Dc * P], BF16)) for i in range(2)]
            pRV = [E4(nc.psum_tensor(f"pRV{i}", [P, D], F32)) for i in range(2)]
            pKA = [E4(nc.psum_tensor(f"pKA{i}", [P, 512], F32)) for i in range(2)]
            blk = E4(nc.Block())
            @blk.gpsimd
            def _(g):
                g.memset(pscr[:, :], 0.0)
                g.dma_start(WR[:, :].rearrange("p (c w) -> p c w", c=Dc),
                            wr_t.ap().rearrange("(c p) w -> p c w", p=P)
                            ).then_inc(sems["c_w2"], 16)
                g.dma_start(rb_scr.ap()[:, 96:128], WR[:, 0:32]
                            ).then_inc(sems["c_w2"], 16)
                g.dma_start(WK2[:, :].rearrange("p (c w) -> p c w", c=Dc),
                            wk_t.ap().rearrange("(c p) w -> p c w", p=P)
                            ).then_inc(sems["c_w2"], 16)
                g.dma_start(rb_scr.ap()[:, 64:96], WK2[:, 0:32]
                            ).then_inc(sems["c_w2"], 16)
                g.dma_start(WV2[:, :].rearrange("p (c w) -> p c w", c=Dc),
                            wv_t.ap().rearrange("(c p) w -> p c w", p=P)
                            ).then_inc(sems["c_w2"], 16)
                g.dma_start(rb_scr.ap()[:, 32:64], WV2[:, 0:32]
                            ).then_inc(sems["c_w2"], 16)
                def emit_outb(u, t):
                    idx = u * F_TPC + t
                    g.wait_ge(sems["c_oe2"], sched[("f_oe2", u, t)])
                    if idx >= 2:
                        pu, pt = divmod(idx - 2, F_TPC)
                        g.wait_ge(sems["c_out" + str(pt % 2)],
                                  sched[("f_out", pu, pt)])
                    g.tensor_add(outb[:, (t % 2) * D:((t % 2) + 1) * D],
                                 x2r[:, (u % 2) * F_TPC * D + t * D:
                                     (u % 2) * F_TPC * D + (t + 1) * D],
                                 lnb3[:, (idx % 2) * D:((idx % 2) + 1) * D]
                                 ).then_inc(sems["c_oe3"], 1)
                    g.tensor_copy(pscr[:, 0:16], pscr[:, 16:32])
                    s_idx = u * F_TPC + t
                    g.dma_start(out.ap()[s_idx * P:(s_idx + 1) * P, :],
                                outb[:, (t % 2) * D:((t % 2) + 1) * D]
                                ).then_inc(sems["c_out" + str(t % 2)], 16)

                for u in range(NFC):
                    for t in range(F_TPC):
                        if u >= 1:
                            emit_outb(u - 1, t)
                for t in range(F_TPC):
                    emit_outb(NFC - 1, t)
                g.wait_ge(sems["c_out0"], C["c_out0"])
                g.wait_ge(sems["c_out1"], C["c_out1"])

            @blk.sync
            def _(sync):
                sync.wait_ge(sems["c_sp0"], C["c_sp0"])
                sync.wait_ge(sems["c_sp1"], C["c_sp1"])
                for u in range(NFC):
                    if u >= 2:
                        sync.wait_ge(sems["c_oe3"], sched[("f_oe3", u - 2, F_TPC - 1)])
                    src = x2_spill.ap()[u * FW:(u + 1) * FW, :].rearrange(
                        "(t p) d -> p t d", p=P)
                    dst = x2r[:, (u % 2) * F_TPC * D:((u % 2) + 1) * F_TPC * D
                              ].rearrange("p (t d) -> p t d", t=F_TPC)
                    sync.dma_start(dst, src).then_inc(sems["c_x2r" + str(u % 2)], 16)

            @blk.vector
            def _(v):
                warm_v(v)
                v.memset(zerov[:, :], 0.0)
                v.wait_ge(sems["c_qtp"], sched[("b2tp",)])
                v.tensor_copy(xn2bndT[:, :], pX2T[0][:, 0:Dc * M])
                v.tensor_copy(spcv[:, 1:3], spcv[:, 1:3]).then_inc(sems["c_qte"], 1)

                def part1(u, t):
                    idx = u * F_TPC + t
                    if idx >= 4:
                        pu, pt = divmod(idx - 4, F_TPC)
                        v.wait_ge(sems["c_xn"], sched[("f_xn", pu, pt)])
                    v.wait_ge(sems["c_x2r" + str(u % 2)], sched[("x2r", u)])
                    xt = x2r[:, (u % 2) * F_TPC * D + t * D:
                             (u % 2) * F_TPC * D + (t + 1) * D]
                    c0 = (idx % 4) * 2
                    v.tensor_mul(sqv[:, :], xt, xt)
                    v.reduce_sum(lnS[:, c0:c0 + 1], xt, axis=AX.X)
                    v.reduce_sum(lnS[:, c0 + 1:c0 + 2], sqv[:, :], axis=AX.X)
                    v.tensor_copy(spcv[:, 1:3], spcv[:, 1:3]
                                  ).then_inc(sems["c_lnst"], 1)

                def xevac(u, t):
                    idx = u * F_TPC + t
                    v.wait_ge(sems["c_xtp"], sched[("f_xtp", u, t)])
                    dst = XN2T.rearrange("p (c w) -> p c w", c=Dc)[:, :, t * P:(t + 1) * P]
                    src = pX2T[t % 2][:, :].rearrange("p (c q) -> p c q", c=Dc)
                    v.tensor_copy(dst, src)
                    v.tensor_copy(spcv[:, 1:3], spcv[:, 1:3]
                                  ).then_inc(sems["c_xte"], 1)
                    if t == 0 and u >= 1:
                        v.wait_ge(sems["c_rmm"], sched[("f_rmm", u - 1, F_TPC - 1)])
                        v.wait_ge(sems["c_kamm"], sched[("f_kamm", u - 1, Dc - 1)])
                    s4 = XN2T.rearrange("p (c t q) -> p c t q", c=Dc, t=F_TPC)
                    v4 = XN2ST.rearrange("p (c t q) -> p c t q", c=Dc, t=F_TPC)
                    if t == 0:
                        b4 = xn2bndT.rearrange("p (c m o) -> p c m o", c=Dc, o=1)
                        v.tensor_copy(v4[:, :, :, 0:1],
                                      b4[:, :, u * F_TPC:(u + 1) * F_TPC, :])
                    v.tensor_copy(v4[:, :, t:t + 1, 1:P], s4[:, :, t:t + 1, 0:P - 1])
                    for cc in range(Dc):
                        sl = slice(cc * FW + t * P, cc * FW + (t + 1) * P)
                        v.tensor_sub(XRT[:, sl], XN2T[:, sl], XN2ST[:, sl])
                    for cc in range(Dc):
                        sl = slice(cc * FW + t * P, cc * FW + (t + 1) * P)
                        v.scalar_tensor_tensor(XKT[:, sl], XRT[:, sl],
                                               mu[:, Dc + cc:Dc + cc + 1],
                                               XN2ST[:, sl], ALU.mult, ALU.add)
                    for cc in range(Dc):
                        sl = slice(cc * FW + t * P, cc * FW + (t + 1) * P)
                        v.scalar_tensor_tensor(XRT[:, sl], XRT[:, sl],
                                               mu[:, cc:cc + 1],
                                               XN2ST[:, sl], ALU.mult, ALU.add)
                    v.tensor_copy(spcv[:, 1:3], spcv[:, 1:3])
                    v.tensor_copy(spcv[:, 1:3], spcv[:, 1:3]).then_inc(sems["c_mix"], 1)

                def ffob_mul(u, t):
                    idx = u * F_TPC + t
                    v.wait_ge(sems["c_omm"], sched[("f_omm", u, t)])
                    if idx >= 4:
                        pu, pt = divmod(idx - 4, F_TPC)
                        v.wait_ge(sems["c_oe2"], sched[("f_oe2", pu, pt)])
                    fsl = ffob[:, (idx % 4) * D:((idx % 4) + 1) * D]
                    v.tensor_mul(fsl, pRV[t % 2][:, :], rbuf[:, t * D:(t + 1) * D])
                    v.tensor_copy(spcv[:, 1:3], spcv[:, 1:3]
                                  ).then_inc(sems["c_ffob"], 1)

                def est(u, t):
                    idx = u * F_TPC + t
                    fsl = ffob[:, (idx % 4) * D:((idx % 4) + 1) * D]
                    v.tensor_mul(sqg[:, :], fsl, fsl)
                    v.reduce_sum(eS[:, (idx % 4) * 2:(idx % 4) * 2 + 1], fsl,
                                 axis=AX.X)
                    v.reduce_sum(eS[:, (idx % 4) * 2 + 1:(idx % 4) * 2 + 2],
                                 sqg[:, :], axis=AX.X)
                    v.tensor_copy(spcv[:, 1:3], spcv[:, 1:3])
                    v.tensor_copy(spcv[:, 1:3], spcv[:, 1:3]
                                  ).then_inc(sems["c_est"], 1)

                for t in range(F_TPC):
                    part1(0, t)
                for u in range(NFC):
                    if u >= 1:
                        ffob_mul(u - 1, 0)
                        est(u - 1, 0)
                        ffob_mul(u - 1, 1)
                        est(u - 1, 1)
                    xevac(u, 0)
                    if u >= 1:
                        ffob_mul(u - 1, 2)
                        est(u - 1, 2)
                    xevac(u, 1)
                    if u >= 1:
                        ffob_mul(u - 1, 3)
                        est(u - 1, 3)
                    xevac(u, 2)
                    xevac(u, 3)
                    if u + 1 < NFC:
                        for t in range(F_TPC):
                            part1(u + 1, t)
                for t in range(F_TPC):
                    ffob_mul(NFC - 1, t)
                    est(NFC - 1, t)

            @blk.scalar
            def _(sc):
                warm_sc(sc)
                sc.wait_ge(sems["c_x2"], sched[("bx2",)])
                ln_act(sc, x2bnd[:, :], xn2bnd[:, :], "c_xnq", ("b2xn",),
                       bufs=(ssumt2, ssqt2, varept2, lnvt2, negmeant2, rstdt2,
                             nmrt2), zsc=zmask[:, 0:1])
                for u in range(NFC):
                    for t in range(F_TPC):
                        idx = u * F_TPC + t
                        if idx > 1:
                            pu, pt = divmod(idx - 2, F_TPC)
                            sc.wait_ge(sems["c_xtp"], sched[("f_xtp", pu, pt)])
                        sc.wait_ge(sems["c_lnst"], sched[("f_st", u, t)])
                        xt = x2r[:, (u % 2) * F_TPC * D + t * D:
                                 (u % 2) * F_TPC * D + (t + 1) * D]
                        c0 = (idx % 4) * 2
                        ln_act(sc, xt, xn2t[:, (idx % 2) * D:((idx % 2) + 1) * D],
                               "c_xn", ("f_xn", u, t),
                               stats=(lnS[:, c0:c0 + 1], lnS[:, c0 + 1:c0 + 2]))
                    def emit_epi(uu, t0, t1):
                        for t in range(t0, t1):
                            idx = uu * F_TPC + t
                            sc.wait_ge(sems["c_est"], sched[("f_et", uu, t)])
                            if idx >= 2:
                                pu, pt = divmod(idx - 2, F_TPC)
                                sc.wait_ge(sems["c_oe3"], sched[("f_oe3", pu, pt)])
                            ln_act(sc, ffob[:, (idx % 4) * D:((idx % 4) + 1) * D],
                                   lnb3[:, (idx % 2) * D:((idx % 2) + 1) * D],
                                   "c_oe2", ("f_oe2", uu, t),
                                   stats=(eS[:, (idx % 4) * 2:(idx % 4) * 2 + 1],
                                          eS[:, (idx % 4) * 2 + 1:(idx % 4) * 2 + 2]))

                    if u >= 1:
                        emit_epi(u - 1, 0, 2)
                    for t in range(F_TPC):
                        sc.wait_ge(sems["c_rmm"], sched[("f_rmm", u, t)])
                        if u >= 1:
                            sc.wait_ge(sems["c_ffob"], sched[("ffob", u - 1, t)])
                        sc.activation(rbuf[:, t * D:(t + 1) * D], pRV[t % 2][:, :],
                                      AF.Sigmoid)
                        sc.activation(spca[:, 0:1], epsv[:, 0:1], AF.Identity
                                      ).then_inc(sems["c_rev"], 1)
                    for dkc in range(Dc):
                        sc.wait_ge(sems["c_kamm"], sched[("f_kamm", u, dkc)])
                        sc.activation(pKA[dkc % 2][:, :], pKA[dkc % 2][:, :], AF.Relu)
                        sc.activation(kactT[:, dkc * FW:(dkc + 1) * FW],
                                      pKA[dkc % 2][:, :], AF.Square)
                        sc.activation(spca[:, 0:1], epsv[:, 0:1], AF.Identity)
                        sc.activation(spca[:, 0:1], epsv[:, 0:1], AF.Identity
                                      ).then_inc(sems["c_kaev"], 1)
                    if u >= 1:
                        emit_epi(u - 1, 2, 4)
                    if u == NFC - 1:
                        emit_epi(u, 0, 4)

            @blk.tensor
            def _(te):
                te.wait_ge(sems["c_xnq"], sched[("b2xn",)])
                for cc in range(Dc):
                    ins = te.transpose(pX2T[0][:, cc * M:(cc + 1) * M],
                                       xn2bnd[:, cc * P:(cc + 1) * P], ident[0:M, 0:M])
                ins.then_inc(sems["c_qtp"], 1)
                te.wait_ge(sems["c_qte"], sched[("b2te",)])
                for u in range(NFC):

                    def tp_tile(t):
                        idx = u * F_TPC + t
                        te.wait_ge(sems["c_xn"], sched[("f_xn", u, t)])
                        if idx >= 2:
                            pu, pt = divmod(idx - 2, F_TPC)
                            te.wait_ge(sems["c_xte"], sched[("f_xte", pu, pt)])
                        for cc in range(Dc):
                            ins = te.transpose(
                                pX2T[t % 2][:, cc * P:(cc + 1) * P],
                                xn2t[:, (idx % 2) * D + cc * P:
                                     (idx % 2) * D + (cc + 1) * P], ident[:, :])
                        ins.then_inc(sems["c_xtp"], 1)

                    def rmm(t):
                        seq = u * 8 + t
                        if u == 0 and t == 0:
                            te.wait_ge(sems["c_w2"], sched[("wr",)])
                        te.wait_ge(sems["c_mix"], sched[("f_mix", u, t)])
                        if seq >= 2:
                            pu2, sl = divmod(seq - 2, 8)
                            if sl < F_TPC:
                                te.wait_ge(sems["c_rev"], sched[("f_rev", pu2, sl)])
                            else:
                                te.wait_ge(sems["c_ffob"],
                                           sched[("ffob", pu2, sl - F_TPC)])
                        for h in range(NH):
                            for cc in range(Dc):
                                ins = te.matmul(
                                    pRV[t % 2][:, h * 512:(h + 1) * 512],
                                    XRT[:, cc * FW + t * P: cc * FW + (t + 1) * P],
                                    WR[:, cc * D + h * 512: cc * D + (h + 1) * 512],
                                    start=(cc == 0), stop=(cc == Dc - 1))
                        ins.then_inc(sems["c_rmm"], 1)

                    tp_tile(0)
                    tp_tile(1)
                    rmm(0)
                    tp_tile(2)
                    rmm(1)
                    tp_tile(3)
                    rmm(2)
                    rmm(3)
                    te.wait_ge(sems["c_mix"], sched[("f_mix", u, F_TPC - 1)])
                    if u == 0:
                        te.wait_ge(sems["c_w2"], sched[("wk2",)])
                    for dkc in range(Dc):
                        idx = u * Dc + dkc
                        if idx >= 2:
                            pu, pd = divmod(idx - 2, Dc)
                            te.wait_ge(sems["c_kaev"], sched[("f_kaev", pu, pd)])
                        for cc in range(Dc):
                            ins = te.matmul(
                                pKA[dkc % 2][:, :],
                                WK2[:, cc * D + dkc * P: cc * D + (dkc + 1) * P],
                                XKT[:, cc * FW:(cc + 1) * FW],
                                start=(cc == 0), stop=(cc == Dc - 1))
                        ins.then_inc(sems["c_kamm"], 1)
                    te.wait_ge(sems["c_kaev"], sched[("f_kaev", u, Dc - 1)])
                    if u == 0:
                        te.wait_ge(sems["c_w2"], sched[("wv2",)])
                    for t in range(F_TPC):
                        te.wait_ge(sems["c_rev"], sched[("f_rev", u, t)])
                        pu2, sl = divmod(u * 8 + 4 + t - 2, 8)
                        if sl < F_TPC:
                            te.wait_ge(sems["c_rev"], sched[("f_rev", pu2, sl)])
                        else:
                            te.wait_ge(sems["c_ffob"], sched[("ffob", pu2, sl - F_TPC)])
                        for h in range(NH):
                            for dkc in range(Dc):
                                ins = te.matmul(
                                    pRV[t % 2][:, h * 512:(h + 1) * 512],
                                    kactT[:, dkc * FW + t * P: dkc * FW + (t + 1) * P],
                                    WV2[:, dkc * D + h * 512: dkc * D + (h + 1) * 512],
                                    start=(dkc == 0), stop=(dkc == Dc - 1))
                        ins.then_inc(sems["c_omm"], 1)

    nc.finalize()
    return nc


def make_in_maps(x_full, W, Wr, Wk, Wv, mur, muk, S=4096, D=1024, n_cores=8):
    T_OWN = S // 256
    wkv_t = np.ascontiguousarray(W.T)
    wr_t = np.ascontiguousarray(Wr.T)
    wk_t = np.ascontiguousarray(Wk.T)
    wv_t = np.ascontiguousarray(Wv.T)
    Dc = D // P
    mu = np.ascontiguousarray(
        np.concatenate([mur.reshape(Dc, P).T, muk.reshape(Dc, P).T], axis=1))

    tri = np.where(np.arange(P)[:, None] >= np.arange(P)[None, :], 0.0, NEG
                   ).astype(ml_dtypes.bfloat16)

    in_maps = []
    for c in range(n_cores):
        b, r = divmod(c, 2)
        xb = np.ascontiguousarray(x_full[b])
        masks = np.zeros((P, 256), ml_dtypes.bfloat16)
        if r == 0:
            masks[:, 0:128] = tri
            masks[:, 128:256] = ml_dtypes.bfloat16(NEG)
        else:
            masks[:, 128:256] = tri
        xq = np.empty((T_OWN * P, D), np.float32)
        x1bnd = np.zeros((T_OWN, D), np.float32)
        zmask = np.zeros((T_OWN, 1), np.float32)
        tmask = np.full((T_OWN, S), NEG, ml_dtypes.bfloat16)
        for s in range(T_OWN):
            p = 2 * s + r
            xq[s * P:(s + 1) * P] = xb[p * P:(p + 1) * P]
            bnd = p * P - 1
            if bnd >= 0:
                x1bnd[s] = xb[bnd]
                zmask[s] = 1.0
                tmask[s, :bnd + 1] = 0.0
            else:
                tmask[s, 0] = 0.0
        in_maps.append({
            "x": xb, "xq_in": xq,
            "wkv_t": wkv_t, "wr_t": wr_t, "wk_t": wk_t, "wv_t": wv_t,
            "mu_in": mu, "masks_in": masks, "tmask_in": tmask,
            "x1bnd_in": x1bnd, "zmask_in": zmask,
        })
    return in_maps


def assemble_out(results, S=4096, D=1024, n_cores=8):
    T_OWN = S // 256
    out_full = np.empty((n_cores // 2, S, D), np.float32)
    for c in range(n_cores):
        b, r = divmod(c, 2)
        o = results[c]["out"]
        for s in range(T_OWN):
            p = 2 * s + r
            out_full[b, p * P:(p + 1) * P, :] = o[s * P:(s + 1) * P, :]
    return out_full


def kernel(L0_attn_x, L0_attn_W, L0_ffn_W_r, L0_ffn_W_k, L0_ffn_W_v,
           L0_ffn_mu_r, L0_ffn_mu_k):
    S, D = 4096, 1024
    x_full = np.ascontiguousarray(np.asarray(L0_attn_x, dtype=np.float32))
    in_maps = make_in_maps(
        x_full,
        np.asarray(L0_attn_W, np.float32),
        np.asarray(L0_ffn_W_r, np.float32),
        np.asarray(L0_ffn_W_k, np.float32),
        np.asarray(L0_ffn_W_v, np.float32),
        np.asarray(L0_ffn_mu_r, np.float32),
        np.asarray(L0_ffn_mu_k, np.float32),
        S=S, D=D)
    nc = build_program(S, D)
    res = run_bass_kernel_spmd(nc, in_maps, core_ids=list(range(8)))
    return assemble_out(res.results, S=S, D=D)


# version 24
# speedup vs baseline: 1.0285x; 1.0285x over previous
import contextlib
from collections import defaultdict

import numpy as np
import ml_dtypes
import concourse.bass as bass
import concourse.mybir as mybir
from concourse.bass_utils import run_bass_kernel_spmd

F32 = mybir.dt.float32
BF16 = mybir.dt.bfloat16
AF = mybir.ActivationFunctionType
ALU = mybir.AluOpType
AX = mybir.AxisListType

NEG = -1.0e30
P = 128


def step_chunks(s, CW=1024):
    keys = 256 * (s + 1)
    out = []
    off = 0
    while off < keys:
        w = min(CW, keys - off)
        out.append((off, w))
        off += w
    return out


def build_program(S=4096, D=1024, dbg=False):
    Dc = D // P
    DKV = 2 * D
    T_OWN = S // 256
    SCH = 256
    NCH = S // SCH
    TPC = SCH // P
    CW = 1024
    ST = S // P
    F_TPC = 4
    NFC = T_OWN // F_TPC
    NH = D // 512
    M = T_OWN

    nc = bass.Bass("TRN2", target_bir_lowering=False, detect_race_conditions=False)

    x = nc.dram_tensor("x", [S, D], F32, kind="ExternalInput")
    xq_in = nc.dram_tensor("xq_in", [T_OWN * P, D], F32, kind="ExternalInput")
    wkv_t = nc.dram_tensor("wkv_t", [D, DKV], F32, kind="ExternalInput")
    wr_t = nc.dram_tensor("wr_t", [D, D], F32, kind="ExternalInput")
    wk_t = nc.dram_tensor("wk_t", [D, D], F32, kind="ExternalInput")
    wv_t = nc.dram_tensor("wv_t", [D, D], F32, kind="ExternalInput")
    mu_in = nc.dram_tensor("mu_in", [P, 2 * Dc], F32, kind="ExternalInput")
    masks_in = nc.dram_tensor("masks_in", [P, 256], BF16, kind="ExternalInput")
    tmask_in = nc.dram_tensor("tmask_in", [M, S], BF16, kind="ExternalInput")
    x1bnd_in = nc.dram_tensor("x1bnd_in", [M, D], F32, kind="ExternalInput")
    zmask_in = nc.dram_tensor("zmask_in", [M, 1], F32, kind="ExternalInput")
    out = nc.dram_tensor("out", [T_OWN * P, D], F32, kind="ExternalOutput")
    rb_scr = nc.dram_tensor("rb_scr", [P, 128], BF16, kind="Internal")
    x2_spill = nc.dram_tensor("x2_spill", [T_OWN * P, D], F32,
                              kind="ExternalOutput" if dbg else "Internal")
    if dbg:
        dbg_xnt = nc.dram_tensor("dbg_xnt", [P, 2048], BF16, kind="ExternalOutput")
        dbg_xn1t = nc.dram_tensor("dbg_xn1t", [P, D], BF16, kind="ExternalOutput")
        dbg_KT = nc.dram_tensor("dbg_KT", [P, S], BF16, kind="ExternalOutput")
        dbg_V = nc.dram_tensor("dbg_V", [P, D], BF16, kind="ExternalOutput")
        dbg_Pb = nc.dram_tensor("dbg_Pb", [P, S], BF16, kind="ExternalOutput")
        dbg_st = nc.dram_tensor("dbg_st", [P, 16], F32, kind="ExternalOutput")
        dbg_x2bnd = nc.dram_tensor("dbg_x2bnd", [M, D], F32, kind="ExternalOutput")
        dbg_xn1q = nc.dram_tensor("dbg_xn1q", [P, D], BF16, kind="ExternalOutput")
    ident_dram = nc.inline_tensor(np.eye(P, dtype=ml_dtypes.bfloat16), "ident_const")

    cnt = defaultdict(int)
    sched = {}

    def bump(sem, key, n=1):
        cnt[sem] += n
        if key is not None:
            sched[key] = cnt[sem]
        return cnt[sem]

    bump("c_cst", ("cst",), 16 * 3)
    bump("c_dbg", ("dbg",), 16 * 999)
    bump("c_w1", ("wkv",), 32)

    for ch in range(NCH):
        bump("c_x" + str(ch % 3), ("x", ch), 16)
        for t in range(TPC):
            bump("c_lnst", ("p1st", ch, t))
            bump("c_xn", ("p1xn", ch, t))
            bump("c_xtp", ("p1xtp", ch, t))
            bump("c_xte", ("p1xte", ch, t))
        for dkc in range(Dc):
            bump("c_kmm", ("p1kmm", ch, dkc))
            bump("c_kev", ("p1kev", ch, dkc))
        for t in range(TPC):
            bump("c_vmm", ("p1vmm", ch, t))
            bump("c_vev", ("p1vev", ch, t))

    bump("c_bnd", ("mb",), 16)
    for s in range(T_OWN):
        bump("c_xq" + str(s % 4), ("xq", s), 16)
        bump("c_xnq", ("xnq", s))
        bump("c_qtp", ("qtp", s))
        bump("c_qte", ("qte", s))
        chunks = step_chunks(s, CW)
        for j in range(len(chunks)):
            bump("c_qk", ("qk", s, j))
            bump("c_max", ("max", s, j))
            bump("c_exp", ("exp", s, j))
        if len(chunks) > 1:
            bump("c_fx1", ("fx1", s))
        bump("c_cv", ("cv", s))
        bump("c_pfx", ("pfx", s))
        nkb = 256 * (s + 1) // P
        for g in range((nkb + 7) // 8):
            bump("c_ptp", ("ptp", s, g))
            bump("c_pte", ("pte", s, g))
        bump("c_av", ("av", s))
        bump("c_attb", ("attb", s))
        bump("c_ep2", ("ep2", s))
        bump("c_x2", ("x2", s))
        bump("c_sp" + str(s % 2), ("spill", s), 16)

    NTJ = S // CW
    bump("c_bnd", ("bnd",), 32)
    bump("c_xnq", ("bxn",))
    bump("c_qtp", ("bqtp",))
    bump("c_qte", ("bqte",))
    for j in range(NTJ):
        bump("c_qk", ("bqk", j))
        bump("c_max", ("bmax", j))
        bump("c_exp", ("bexp", j))
    bump("c_fx1", ("bfx1",))
    bump("c_cv", ("bcv",))
    bump("c_pfx", ("bpfx",))
    NTG = (ST + 7) // 8
    for g in range(NTG):
        bump("c_ptp", ("bptp", g))
        bump("c_pte", ("bpte", g))
    bump("c_av", ("bav",))
    bump("c_attb", ("battb",))
    bump("c_ep2", ("bep2",))
    bump("c_x2", ("bx2",))

    bump("c_w2", ("wr",), 32)
    bump("c_w2", ("wk2",), 32)
    bump("c_w2", ("wv2",), 32)
    bump("c_xnq", ("b2xn",))
    bump("c_qtp", ("b2tp",))
    bump("c_qte", ("b2te",))
    for u in range(NFC):
        bump("c_x2r" + str(u % 2), ("x2r", u), 16)
        for t in range(F_TPC):
            bump("c_lnst", ("f_st", u, t))
            bump("c_xn", ("f_xn", u, t))
            bump("c_xtp", ("f_xtp", u, t))
            bump("c_xte", ("f_xte", u, t))
            bump("c_mix", ("f_mix", u, t))
        for t in range(F_TPC):
            bump("c_rmm", ("f_rmm", u, t))
            bump("c_rev", ("f_rev", u, t))
        for dkc in range(Dc):
            bump("c_kamm", ("f_kamm", u, dkc))
            bump("c_kaev", ("f_kaev", u, dkc))
        for t in range(F_TPC):
            bump("c_omm", ("f_omm", u, t))
            bump("c_ffob", ("ffob", u, t))
            bump("c_est", ("f_et", u, t))
            bump("c_oe2", ("f_oe2", u, t))
            bump("c_oe3", ("f_oe3", u, t))
            bump("c_out" + str(t % 2), ("f_out", u, t), 16)

    C = dict(cnt)

    def prev_ps_key(s, j):
        if j >= 2:
            return ("exp", s, j - 2)
        ss = s - 1
        while ss >= 0:
            cand = [jj for jj in range(len(step_chunks(ss, CW))) if jj % 2 == j % 2]
            if cand:
                return ("exp", ss, cand[-1])
            ss -= 1
        return None

    with contextlib.ExitStack() as ctx:
        E = ctx.enter_context
        sems = {name: E(nc.semaphore(name)) for name in C}

        ident = E(nc.sbuf_tensor("ident", [P, P], BF16))
        epsv = E(nc.sbuf_tensor("epsv", [P, 1], F32))
        mu = E(nc.sbuf_tensor("mu", [P, 2 * Dc], F32))
        zmask = E(nc.sbuf_tensor("zmask", [M, 1], F32))
        x2bnd = E(nc.sbuf_tensor("x2bnd", [M, D], F32))
        ssum = E(nc.sbuf_tensor("ssum", [P, 1], F32))
        ssq = E(nc.sbuf_tensor("ssq", [P, 1], F32))
        varep = E(nc.sbuf_tensor("varep", [P, 1], F32))
        lnv = E(nc.sbuf_tensor("lnv", [P, 1], F32))
        negmean = E(nc.sbuf_tensor("negmean", [P, 1], F32))
        rstd = E(nc.sbuf_tensor("rstd", [P, 1], F32))
        nmr = E(nc.sbuf_tensor("nmr", [P, 1], F32))
        spcv = E(nc.sbuf_tensor("spcv", [P, 4], F32))
        spca = E(nc.sbuf_tensor("spca", [P, 1], F32))
        wup = E(nc.sbuf_tensor("wup", [P, 1], F32))
        sqscr = E(nc.sbuf_tensor("sqscr", [P, D], BF16))

        def warm_sc(sc):
            sc.wait_ge(sems["c_cst"], sched[("cst",)])
            sc.activation(sqscr[:, 0:P], ident[:, 0:P], AF.Square, accum_out=wup[:, 0:1])
            sc.activation(sqscr[:, 0:P], ident[:, 0:P], AF.Square, accum_out=wup[:, 0:1])
            sc.activation(spca[:, 0:1], wup[:, 0:1], AF.Ln)
            sc.activation(spca[:, 0:1], spca[:, 0:1], AF.Exp, scale=-0.5)
            sc.activation(spca[:, 0:1], ident[:, 0:1], AF.Identity,
                          bias=wup[:, 0:1], scale=wup[:, 0:1])

        def warm_v(v):
            v.memset(spcv[:, :], 0.0)
            v.wait_ge(sems["c_cst"], sched[("cst",)])
            v.reduce_sum(wup[:, 0:1], ident[:, 0:64], axis=AX.X)
            v.tensor_copy(spcv[:, 0:1], spcv[:, 0:1])
            v.reduce_max(wup[:, 0:1], ident[:, 0:64], axis=AX.X, negate=True)
            v.tensor_copy(spcv[:, 0:1], spcv[:, 0:1])

        RD = 1.0 / D

        def ln_act(sc, src, dst, sem=None, key=None, bufs=None, zsc=None, stats=None):
            st, qq, vp, lv, nme, rs, nr = bufs if bufs is not None else (
                ssum, ssq, varep, lnv, negmean, rstd, nmr)
            n = src.shape[0]

            def spc():
                sc.activation(spca[0:n, 0:1], epsv[0:n, 0:1], AF.Identity)

            if stats is None:
                sc.activation(sqscr[0:n, :], src, AF.Identity, accum_out=st[:, 0:1])
                sc.activation(sqscr[0:n, :], src, AF.Square, accum_out=qq[:, 0:1])
            else:
                st, qq = stats
            sc.activation(vp[:, :], st[:, :], AF.Square, scale=RD)
            sc.activation(lv[:, :], vp[:, :], AF.Identity, scale=-1.0,
                          bias=epsv[0:n, 0:1])
            sc.activation(nme[:, :], st[:, :], AF.Identity, scale=-RD)
            sc.activation(vp[:, :], qq[:, :], AF.Ln, scale=RD, bias=lv[:, 0:1])
            sc.activation(rs[:, :], vp[:, :], AF.Exp, scale=-0.5)
            if zsc is not None:
                spc()
                sc.activation(lv[:, :], rs[:, :], AF.Identity, scale=zsc)
                spc()
                rs = lv
            sc.activation(nr[:, :], nme[:, :], AF.Identity, scale=rs[:, 0:1])
            spc()
            sc.activation(dst, src, AF.Identity, bias=nr[:, 0:1],
                          scale=rs[:, 0:1])
            spc()
            ins = sc.activation(spca[0:n, 0:1], epsv[0:n, 0:1], AF.Identity)
            if sem is not None:
                ins.then_inc(sems[sem], 1)

        actx = contextlib.ExitStack()
        EA = actx.enter_context
        KT = EA(nc.sbuf_tensor("KT", [P, Dc * S], BF16))
        V = EA(nc.sbuf_tensor("V", [P, ST * D], BF16))

        with contextlib.ExitStack() as p1:
            E1 = p1.enter_context
            WKV = E1(nc.sbuf_tensor("WKV", [P, Dc * DKV], BF16))
            xbuf = E1(nc.sbuf_tensor("xbuf", [P, 3 * TPC * D], F32))
            sqv1 = E1(nc.sbuf_tensor("sqv1", [P, D], F32))
            lnS1 = E1(nc.sbuf_tensor("lnS1", [P, 8], F32))
            xnt_roll = E1(nc.sbuf_tensor("xnt_roll", [P, 2 * Dc * SCH], BF16))
            xn1q_t = E1(nc.sbuf_tensor("xn1q_t", [P, D], BF16))
            pXT = [E1(nc.psum_tensor(f"pXT{i}", [P, Dc * P], BF16)) for i in range(2)]
            pKT = [E1(nc.psum_tensor(f"pKT{i}", [P, SCH], F32)) for i in range(2)]
            pV = [E1(nc.psum_tensor(f"pV{i}", [P, D], F32)) for i in range(2)]
            blk = E1(nc.Block())
            @blk.sync
            def _(sync):
                sync.dma_start(ident[:, :], ident_dram.ap()).then_inc(sems["c_cst"], 16)
                sync.dma_start(mu[:, :], mu_in.ap()).then_inc(sems["c_cst"], 16)
                sync.dma_start(zmask[:, :], zmask_in.ap()).then_inc(sems["c_cst"], 16)
                for ch in range(NCH):
                    if ch >= 3:
                        sync.wait_ge(sems["c_xn"], sched[("p1xn", ch - 3, TPC - 1)])
                    src = x.ap()[ch * SCH:(ch + 1) * SCH, :].rearrange(
                        "(t p) d -> p t d", p=P)
                    dst = xbuf[:, (ch % 3) * TPC * D:((ch % 3) + 1) * TPC * D
                               ].rearrange("p (t d) -> p t d", t=TPC)
                    sync.dma_start(dst, src).then_inc(sems["c_x" + str(ch % 3)], 16)

            @blk.gpsimd
            def _(g):
                g.dma_start(WKV[:, :].rearrange("p (c w) -> p c w", c=Dc),
                            wkv_t.ap().rearrange("(c p) w -> p c w", p=P)
                            ).then_inc(sems["c_w1"], 16)
                g.dma_start(rb_scr.ap()[:, 0:32], WKV[:, 0:32]
                            ).then_inc(sems["c_w1"], 16)
                if dbg:
                    g.wait_ge(sems["c_xn"], sched[("p1xn", 0, 0)])
                    g.dma_start(dbg_xn1t.ap(), xn1q_t[:, :]).then_inc(sems["c_dbg"], 16)
                    g.wait_ge(sems["c_xte"], sched[("p1xte", 0, TPC - 1)])
                    g.dma_start(dbg_xnt.ap(), xnt_roll[:, 0:Dc * SCH]
                                ).then_inc(sems["c_dbg"], 16)

            @blk.vector
            def _(v):
                v.memset(epsv[:, :], 1e-5)
                warm_v(v)
                def p1_part1(ch, t):
                    idx = ch * TPC + t
                    if idx >= 4:
                        pc, pt = divmod(idx - 4, TPC)
                        v.wait_ge(sems["c_xn"], sched[("p1xn", pc, pt)])
                    v.wait_ge(sems["c_x" + str(ch % 3)], sched[("x", ch)])
                    xt = xbuf[:, (ch % 3) * TPC * D + t * D:
                              (ch % 3) * TPC * D + (t + 1) * D]
                    c0 = (idx % 4) * 2
                    v.tensor_mul(sqv1[:, :], xt, xt)
                    v.reduce_sum(lnS1[:, c0:c0 + 1], xt, axis=AX.X)
                    v.reduce_sum(lnS1[:, c0 + 1:c0 + 2], sqv1[:, :], axis=AX.X)
                    v.tensor_copy(spcv[:, 1:3], spcv[:, 1:3]
                                  ).then_inc(sems["c_lnst"], 1)

                for t in range(TPC):
                    p1_part1(0, t)
                for ch in range(NCH):
                    for t in range(TPC):
                        if t == 0 and ch >= 2:
                            v.wait_ge(sems["c_vmm"], sched[("p1vmm", ch - 2, TPC - 1)])
                        if dbg and ch == 2 and t == 0:
                            v.wait_ge(sems["c_dbg"], 32)
                        v.wait_ge(sems["c_xtp"], sched[("p1xtp", ch, t)])
                        base = (ch % 2) * Dc * SCH
                        dst = xnt_roll[:, base:base + Dc * SCH].rearrange(
                            "p (c s) -> p c s", c=Dc)[:, :, t * P:(t + 1) * P]
                        src = pXT[t % 2][:, :].rearrange("p (c q) -> p c q", c=Dc)
                        v.tensor_copy(dst, src)
                        v.tensor_copy(spcv[:, 3:4], epsv[:, 0:1]
                                      ).then_inc(sems["c_xte"], 1)
                    if ch + 1 < NCH:
                        for t in range(TPC):
                            p1_part1(ch + 1, t)
                    for t in range(TPC):
                        v.wait_ge(sems["c_vmm"], sched[("p1vmm", ch, t)])
                        stile = ch * TPC + t
                        v.tensor_copy(V[:, stile * D:(stile + 1) * D],
                                      pV[t % 2][:, :]).then_inc(sems["c_vev"], 1)

            @blk.scalar
            def _(sc):
                warm_sc(sc)
                for ch in range(NCH):
                    for t in range(TPC):
                        idx = ch * TPC + t
                        if idx > 0:
                            pch, pt = divmod(idx - 1, TPC)
                            sc.wait_ge(sems["c_xtp"], sched[("p1xtp", pch, pt)])
                        xt = xbuf[:, (ch % 3) * TPC * D + t * D:
                                  (ch % 3) * TPC * D + (t + 1) * D]
                        if dbg and ch == 0 and t == 1:
                            sc.wait_ge(sems["c_dbg"], 16)
                        sc.wait_ge(sems["c_lnst"], sched[("p1st", ch, t)])
                        c0 = ((ch * TPC + t) % 4) * 2
                        ln_act(sc, xt, xn1q_t[:, :], "c_xn", ("p1xn", ch, t),
                               stats=(lnS1[:, c0:c0 + 1], lnS1[:, c0 + 1:c0 + 2]))
                    for dkc in range(Dc):
                        sc.wait_ge(sems["c_kmm"], sched[("p1kmm", ch, dkc)])
                        sc.copy(KT[:, dkc * S + ch * SCH: dkc * S + (ch + 1) * SCH],
                                pKT[dkc % 2][:, :]).then_inc(sems["c_kev"], 1)

            @blk.tensor
            def _(te):
                te.wait_ge(sems["c_cst"], sched[("cst",)])
                te.wait_ge(sems["c_w1"], sched[("wkv",)])
                for ch in range(NCH):
                    rbase = (ch % 2) * Dc * SCH

                    def p1_xt(t):
                        idx = ch * TPC + t
                        te.wait_ge(sems["c_xn"], sched[("p1xn", ch, t)])
                        if idx >= 2:
                            pch, pt = divmod(idx - 2, TPC)
                            te.wait_ge(sems["c_xte"], sched[("p1xte", pch, pt)])
                        for cc in range(Dc):
                            ins = te.transpose(
                                pXT[t % 2][:, cc * P:(cc + 1) * P],
                                xn1q_t[:, cc * P:(cc + 1) * P], ident[:, :])
                        ins.then_inc(sems["c_xtp"], 1)

                    def p1_v(t):
                        idx = ch * TPC + t
                        te.wait_ge(sems["c_xte"], sched[("p1xte", ch, t)])
                        if idx >= 2:
                            pch, pt = divmod(idx - 2, TPC)
                            te.wait_ge(sems["c_vev"], sched[("p1vev", pch, pt)])
                        for cc in range(Dc):
                            for h in range(NH):
                                ins = te.matmul(
                                    pV[t % 2][:, h * 512:(h + 1) * 512],
                                    xnt_roll[:, rbase + cc * SCH + t * P:
                                             rbase + cc * SCH + (t + 1) * P],
                                    WKV[:, cc * DKV + D + h * 512:
                                        cc * DKV + D + (h + 1) * 512],
                                    start=(cc == 0), stop=(cc == Dc - 1))
                        ins.then_inc(sems["c_vmm"], 1)

                    p1_xt(0)
                    p1_v(0)
                    p1_xt(1)
                    te.wait_ge(sems["c_xte"], sched[("p1xte", ch, TPC - 1)])
                    for dkc in range(Dc):
                        kidx = ch * Dc + dkc
                        if kidx >= 2:
                            pch, pd = divmod(kidx - 2, Dc)
                            te.wait_ge(sems["c_kev"], sched[("p1kev", pch, pd)])
                        for cc in range(Dc):
                            ins = te.matmul(
                                pKT[dkc % 2][:, :],
                                WKV[:, cc * DKV + dkc * P: cc * DKV + (dkc + 1) * P],
                                xnt_roll[:, rbase + cc * SCH: rbase + (cc + 1) * SCH],
                                start=(cc == 0), stop=(cc == Dc - 1))
                        ins.then_inc(sems["c_kmm"], 1)
                    p1_v(1)

        NTJ = S // CW
        NTG = (ST + 7) // 8
        with contextlib.ExitStack() as p2:
            E2 = p2.enter_context
            maskb = E2(nc.sbuf_tensor("maskb", [P, 256], BF16))
            tmaskb = E2(nc.sbuf_tensor("tmaskb", [M, S], BF16))
            x1bnd = E2(nc.sbuf_tensor("x1bnd", [M, D], F32))
            xn1bnd = E2(nc.sbuf_tensor("xn1bnd", [M, D], BF16))
            Pbt = E2(nc.sbuf_tensor("Pbt", [M, S], BF16))
            negmaxt = E2(nc.sbuf_tensor("negmaxt", [M, 4], F32))
            lvect = E2(nc.sbuf_tensor("lvect", [M, 4], F32))
            cvect = E2(nc.sbuf_tensor("cvect", [M, 4], F32))
            cvect2 = E2(nc.sbuf_tensor("cvect2", [M, 4], F32))
            negMt = E2(nc.sbuf_tensor("negMt", [M, 1], F32))
            lsumt = E2(nc.sbuf_tensor("lsumt", [M, 1], F32))
            rinvt = E2(nc.sbuf_tensor("rinvt", [M, 1], F32))
            ssumt = E2(nc.sbuf_tensor("ssumt", [M, 1], F32))
            ssqt = E2(nc.sbuf_tensor("ssqt", [M, 1], F32))
            varept = E2(nc.sbuf_tensor("varept", [M, 1], F32))
            lnvt = E2(nc.sbuf_tensor("lnvt", [M, 1], F32))
            negmeant = E2(nc.sbuf_tensor("negmeant", [M, 1], F32))
            rstdt = E2(nc.sbuf_tensor("rstdt", [M, 1], F32))
            nmrt = E2(nc.sbuf_tensor("nmrt", [M, 1], F32))
            tbufs = (ssumt, ssqt, varept, lnvt, negmeant, rstdt, nmrt)

            x1q = E2(nc.sbuf_tensor("x1q", [P, 4 * D], F32))
            xn1q = E2(nc.sbuf_tensor("xn1q", [P, 3 * D], BF16))
            QT = E2(nc.sbuf_tensor("QT", [P, 2 * Dc * P], BF16))
            Pb = E2(nc.sbuf_tensor("Pb", [P, S + 2 * CW], BF16))
            PT = E2(nc.sbuf_tensor("PT", [P, 2 * 8 * P], BF16))
            negmax = E2(nc.sbuf_tensor("negmax", [P, 8], F32))
            lvec = E2(nc.sbuf_tensor("lvec", [P, 8], F32))
            cvec = E2(nc.sbuf_tensor("cvec", [P, 8], F32))
            cvec2 = E2(nc.sbuf_tensor("cvec2", [P, 4], F32))
            negM = E2(nc.sbuf_tensor("negM", [P, 1], F32))
            lsum = E2(nc.sbuf_tensor("lsum", [P, 2], F32))
            rinv = E2(nc.sbuf_tensor("rinv", [P, 2], F32))
            attb = E2(nc.sbuf_tensor("attb", [P, D], F32))
            lnb = E2(nc.sbuf_tensor("lnb", [P, D], F32))
            pS = [E2(nc.psum_tensor(f"pS{i}", [P, CW], F32)) for i in range(2)]
            pPT = [E2(nc.psum_tensor(f"pPT{i}", [P, CW], BF16)) for i in range(2)]
            pAV = E2(nc.psum_tensor("pAV", [P, D], F32))

            def pb_col(ss, off):
                return (ss % 2) * 2 * CW + off if off < 2 * CW else 2 * CW + off

            blk = E2(nc.Block())
            @blk.sync
            def _(sync):
                sync.dma_start(maskb[:, :], masks_in.ap()).then_inc(sems["c_bnd"], 16)
                sync.dma_start(tmaskb[:, :], tmask_in.ap()).then_inc(sems["c_bnd"], 16)
                sync.dma_start(x1bnd[:, :], x1bnd_in.ap()).then_inc(sems["c_bnd"], 16)
                for s in range(T_OWN):
                    if s >= 4:
                        sync.wait_ge(sems["c_x2"], sched[("x2", s - 4)])
                        sync.wait_ge(sems["c_sp" + str((s - 4) % 2)],
                                     sched[("spill", s - 4)])
                    sync.dma_start(x1q[:, (s % 4) * D:((s % 4) + 1) * D],
                                   xq_in.ap()[s * P:(s + 1) * P, :]
                                   ).then_inc(sems["c_xq" + str(s % 4)], 16)

            @blk.gpsimd
            def _(g):
                for s in range(T_OWN):
                    g.wait_ge(sems["c_x2"], sched[("x2", s)])
                    g.dma_start(x2_spill.ap()[s * P:(s + 1) * P, :],
                                x1q[:, (s % 4) * D:((s % 4) + 1) * D]
                                ).then_inc(sems["c_sp" + str(s % 2)], 16)

            @blk.vector
            def _(v):
                warm_v(v)
                v.wait_ge(sems["c_bnd"], sched[("mb",)])

                def mask_j(ss, j, off, w):
                    v.wait_ge(sems["c_qk"], sched[("qk", ss, j)])
                    if j == len(step_chunks(ss, CW)) - 1:
                        v.tensor_add(pS[j % 2][:, w - 256:w],
                                     pS[j % 2][:, w - 256:w],
                                     maskb[:, 0:256])
                    nb = (ss % 2) * 4
                    v.reduce_max(negmax[:, nb + j:nb + j + 1],
                                 pS[j % 2][:, 0:w], axis=AX.X, negate=True)
                    v.tensor_copy(spcv[:, 1:3], spcv[:, 1:3]
                                  ).then_inc(sems["c_max"], 1)

                def x2add(k):
                    v.wait_ge(sems["c_ep2"], sched[("ep2", k)])
                    v.tensor_add(x1q[:, (k % 4) * D:(k % 4 + 1) * D],
                                 x1q[:, (k % 4) * D:(k % 4 + 1) * D],
                                 lnb[:, :]).then_inc(sems["c_x2"], 1)

                def fixup(ss):
                    chunks = step_chunks(ss, CW)
                    n = len(chunks)
                    nb = (ss % 2) * 4
                    if n > 1:
                        v.wait_ge(sems["c_max"], sched[("max", ss, n - 1)])
                        v.tensor_copy(spcv[:, 3:4], epsv[:, 0:1])
                        v.tensor_reduce(negM[:, :], negmax[:, nb:nb + n],
                                        AX.X, ALU.min)
                        v.tensor_copy(spcv[:, 1:3], spcv[:, 1:3]
                                      ).then_inc(sems["c_fx1"], 1)

                def pscale(ss):
                    chunks = step_chunks(ss, CW)
                    n = len(chunks)
                    nb = (ss % 2) * 4
                    v.wait_ge(sems["c_cv"], sched[("cv", ss)])
                    if n > 1:
                        for j, (off, w) in enumerate(chunks):
                            c0 = pb_col(ss, off)
                            v.tensor_scalar_mul(Pb[:, c0:c0 + w],
                                                Pb[:, c0:c0 + w],
                                                cvec[:, nb + j:nb + j + 1])
                    v.reciprocal(rinv[:, (ss % 2):(ss % 2) + 1],
                                 lsum[:, (ss % 2):(ss % 2) + 1])
                    v.tensor_copy(spcv[:, 1:3], spcv[:, 1:3]
                                  ).then_inc(sems["c_pfx"], 1)

                for j, (off, w) in enumerate(step_chunks(0, CW)):
                    mask_j(0, j, off, w)
                fixup(0)
                pscale(0)
                for s in range(T_OWN):
                    nxt = step_chunks(s + 1, CW) if s + 1 < T_OWN else None
                    if nxt is not None:
                        for j in range(len(nxt)):
                            mask_j(s + 1, j, *nxt[j])
                        fixup(s + 1)
                    else:
                        v.wait_ge(sems["c_bnd"], sched[("bnd",)])
                        v.wait_ge(sems["c_qtp"], sched[("bqtp",)])
                        v.tensor_copy(QT[:, 0:Dc * M], pPT[0][:, 0:Dc * M])
                        v.tensor_copy(spcv[:, 1:3], spcv[:, 1:3])
                        v.tensor_copy(spcv[:, 1:3], spcv[:, 1:3]
                                      ).then_inc(sems["c_qte"], 1)
                        for j in range(NTJ):
                            v.wait_ge(sems["c_qk"], sched[("bqk", j)])
                            v.tensor_add(pS[j % 2][0:M, :], pS[j % 2][0:M, :],
                                         tmaskb[:, j * CW:(j + 1) * CW])
                            v.reduce_max(negmaxt[:, j:j + 1], pS[j % 2][0:M, :],
                                         axis=AX.X, negate=True)
                            v.tensor_copy(spcv[:, 1:3], spcv[:, 1:3]
                                          ).then_inc(sems["c_max"], 1)
                        v.wait_ge(sems["c_max"], sched[("bmax", NTJ - 1)])
                        v.tensor_copy(spcv[0:M, 3:4], epsv[0:M, 0:1])
                        v.tensor_reduce(negMt[:, :], negmaxt[:, 0:NTJ], AX.X, ALU.min)
                        v.tensor_copy(spcv[:, 1:3], spcv[:, 1:3]
                                      ).then_inc(sems["c_fx1"], 1)
                    nkb = 256 * (s + 1) // P
                    for g in range((nkb + 7) // 8):
                        kb0, kbn = g * 8, min(8, nkb - g * 8)
                        v.wait_ge(sems["c_ptp"], sched[("ptp", s, g)])
                        v.tensor_copy(PT[:, (g % 2) * 8 * P:(g % 2) * 8 * P + kbn * P],
                                      pPT[g % 2][:, 0:kbn * P])
                        v.tensor_copy(spcv[:, 1:3], spcv[:, 1:3]
                                      ).then_inc(sems["c_pte"], 1)
                    if nxt is not None:
                        pscale(s + 1)
                    else:
                        v.wait_ge(sems["c_cv"], sched[("bcv",)])
                        for j in range(NTJ):
                            v.tensor_scalar_mul(Pbt[:, j * CW:(j + 1) * CW],
                                                Pbt[:, j * CW:(j + 1) * CW],
                                                cvect[:, j:j + 1])
                        v.reciprocal(rinvt[:, :], lsumt[:, :])
                        v.tensor_copy(spcv[:, 1:3], spcv[:, 1:3]
                                      ).then_inc(sems["c_pfx"], 1)
                    if s >= 1:
                        x2add(s - 1)
                    v.wait_ge(sems["c_av"], sched[("av", s)])
                    v.scalar_tensor_tensor(attb[:, :], pAV[:, :],
                                           rinv[:, (s % 2):(s % 2) + 1],
                                           xn1q[:, (s % 3) * D:(s % 3 + 1) * D],
                                           ALU.mult, ALU.add
                                           ).then_inc(sems["c_attb"], 1)
                for g in range(NTG):
                    kb0, kbn = g * 8, min(8, ST - g * 8)
                    v.wait_ge(sems["c_ptp"], sched[("bptp", g)])
                    v.tensor_copy(PT[:, kb0 * M:(kb0 + kbn) * M],
                                  pPT[g % 2][:, 0:kbn * M])
                    v.tensor_copy(spcv[:, 1:3], spcv[:, 1:3]
                                  ).then_inc(sems["c_pte"], 1)
                x2add(T_OWN - 1)
                v.wait_ge(sems["c_av"], sched[("bav",)])
                v.scalar_tensor_tensor(attb[0:M, 0:D], pAV[0:M, :], rinvt[:, 0:1],
                                       xn1bnd[:, :], ALU.mult, ALU.add
                                       ).then_inc(sems["c_attb"], 1)
                v.wait_ge(sems["c_ep2"], sched[("bep2",)])
                v.tensor_add(x2bnd[:, :], x1bnd[:, :], lnb[0:M, 0:D]
                             ).then_inc(sems["c_x2"], 1)

            @blk.scalar
            def _(sc):
                warm_sc(sc)

                def emit_exps(ss):
                    chunks = step_chunks(ss, CW)
                    nb = (ss % 2) * 4
                    for j, (off, w) in enumerate(chunks):
                        if j == 2:
                            png = (256 * ss // P + 7) // 8
                            sc.wait_ge(sems["c_ptp"], sched[("ptp", ss - 1, png - 1)])
                        sc.wait_ge(sems["c_max"], sched[("max", ss, j)])
                        c0 = pb_col(ss, off)
                        sc.activation(Pb[:, c0:c0 + w],
                                      pS[j % 2][:, 0:w], AF.Exp,
                                      bias=negmax[:, nb + j:nb + j + 1],
                                      accum_out=lvec[:, nb + j:nb + j + 1]
                                      ).then_inc(sems["c_exp"], 1)

                def emit_cv(ss):
                    chunks = step_chunks(ss, CW)
                    n = len(chunks)
                    nb = (ss % 2) * 4
                    if n > 1:
                        sc.wait_ge(sems["c_fx1"], sched[("fx1", ss)])
                        sc.activation(cvec[:, nb:nb + n], negmax[:, nb:nb + n],
                                      AF.Exp, scale=-1.0, bias=negM[:, 0:1])
                        sc.activation(spca[:, 0:1], epsv[:, 0:1], AF.Identity)
                        for j in range(n):
                            sc.activation(cvec2[:, j:j + 1],
                                          lvec[:, nb + j:nb + j + 1],
                                          AF.Identity, scale=cvec[:, nb + j:nb + j + 1])
                        sc.activation(spca[:, 0:1], epsv[:, 0:1], AF.Identity)
                        sc.activation(spca[:, 0:1], epsv[:, 0:1], AF.Identity)
                        sc.activation(lsum[:, (ss % 2):(ss % 2) + 1], cvec2[:, 0:1],
                                      AF.Identity, bias=cvec2[:, 1:2])
                        for j in range(2, n):
                            sc.activation(lsum[:, (ss % 2):(ss % 2) + 1],
                                          lsum[:, (ss % 2):(ss % 2) + 1],
                                          AF.Identity, bias=cvec2[:, j:j + 1])
                        sc.activation(spca[:, 0:1], epsv[:, 0:1], AF.Identity
                                      ).then_inc(sems["c_cv"], 1)
                    else:
                        sc.activation(lsum[:, (ss % 2):(ss % 2) + 1],
                                      lvec[:, nb:nb + 1], AF.Identity)
                        sc.activation(spca[:, 0:1], epsv[:, 0:1], AF.Identity
                                      ).then_inc(sems["c_cv"], 1)

                def emit_qte(ss):
                    v_dst = QT[:, (ss % 2) * Dc * P:(ss % 2 + 1) * Dc * P]
                    sc.wait_ge(sems["c_qtp"], sched[("qtp", ss)])
                    sc.copy(v_dst, pPT[0][:, 0:Dc * P])
                    sc.activation(spca[:, 0:1], epsv[:, 0:1], AF.Identity)
                    sc.activation(spca[:, 0:1], epsv[:, 0:1], AF.Identity
                                  ).then_inc(sems["c_qte"], 1)

                def emit_xnq(ss):
                    if ss >= 3:
                        sc.wait_ge(sems["c_attb"], sched[("attb", ss - 3)])
                    sc.wait_ge(sems["c_xq" + str(ss % 4)], sched[("xq", ss)])
                    ln_act(sc, x1q[:, (ss % 4) * D:(ss % 4 + 1) * D],
                           xn1q[:, (ss % 3) * D:(ss % 3 + 1) * D],
                           "c_xnq", ("xnq", ss))

                emit_xnq(0)
                emit_qte(0)
                emit_exps(0)
                emit_cv(0)
                emit_xnq(1)
                for s in range(T_OWN):
                    if s + 1 < T_OWN:
                        emit_qte(s + 1)
                        emit_exps(s + 1)
                        emit_cv(s + 1)
                    else:
                        for j in range(NTJ):
                            sc.wait_ge(sems["c_max"], sched[("bmax", j)])
                            sc.activation(Pbt[:, j * CW:(j + 1) * CW],
                                          pS[j % 2][0:M, :], AF.Exp,
                                          bias=negmaxt[:, j:j + 1],
                                          accum_out=lvect[:, j:j + 1]
                                          ).then_inc(sems["c_exp"], 1)
                        sc.wait_ge(sems["c_fx1"], sched[("bfx1",)])
                        sc.activation(cvect[:, 0:NTJ], negmaxt[:, 0:NTJ], AF.Exp,
                                      scale=-1.0, bias=negMt[:, 0:1])
                        sc.activation(spca[0:M, 0:1], epsv[0:M, 0:1], AF.Identity)
                        for j in range(NTJ):
                            sc.activation(cvect2[:, j:j + 1], lvect[:, j:j + 1],
                                          AF.Identity, scale=cvect[:, j:j + 1])
                        sc.activation(spca[0:M, 0:1], epsv[0:M, 0:1], AF.Identity)
                        sc.activation(spca[0:M, 0:1], epsv[0:M, 0:1], AF.Identity)
                        sc.activation(lsumt[:, :], cvect2[:, 0:1], AF.Identity,
                                      bias=cvect2[:, 1:2])
                        for j in range(2, NTJ):
                            sc.activation(lsumt[:, :], lsumt[:, :], AF.Identity,
                                          bias=cvect2[:, j:j + 1])
                        sc.activation(spca[0:M, 0:1], epsv[0:M, 0:1], AF.Identity
                                      ).then_inc(sems["c_cv"], 1)
                    if s + 2 < T_OWN:
                        emit_xnq(s + 2)
                    elif s + 2 == T_OWN:
                        sc.wait_ge(sems["c_bnd"], sched[("bnd",)])
                        ln_act(sc, x1bnd[:, :], xn1bnd[:, :], "c_xnq", ("bxn",),
                               bufs=tbufs)
                    sc.wait_ge(sems["c_attb"], sched[("attb", s)])
                    if s >= 1:
                        sc.wait_ge(sems["c_x2"], sched[("x2", s - 1)])
                    ln_act(sc, attb[:, :], lnb[:, :], "c_ep2", ("ep2", s))
                sc.wait_ge(sems["c_attb"], sched[("battb",)])
                sc.wait_ge(sems["c_x2"], sched[("x2", T_OWN - 1)])
                ln_act(sc, attb[0:M, 0:D], lnb[0:M, 0:D], "c_ep2", ("bep2",),
                       bufs=tbufs)

            @blk.tensor
            def _(te):
                def emit_qt_t(ss):
                    te.wait_ge(sems["c_xnq"], sched[("xnq", ss)])
                    for cc in range(Dc):
                        ins = te.transpose(pPT[0][:, cc * P:(cc + 1) * P],
                                           xn1q[:, (ss % 3) * D + cc * P:
                                                (ss % 3) * D + (cc + 1) * P],
                                           ident[:, :])
                    ins.then_inc(sems["c_qtp"], 1)

                def emit_qk(ss, j, off, w):
                    pk = prev_ps_key(ss, j)
                    if pk is not None:
                        te.wait_ge(sems["c_exp"], sched[pk])
                    for cc in range(Dc):
                        for nn in range(0, w, 512):
                            nw = min(512, w - nn)
                            ins = te.matmul(
                                pS[j % 2][:, nn:nn + nw],
                                QT[:, (ss % 2) * Dc * P + cc * P:
                                   (ss % 2) * Dc * P + (cc + 1) * P],
                                KT[:, cc * S + off + nn: cc * S + off + nn + nw],
                                start=(cc == 0), stop=(cc == Dc - 1))
                    ins.then_inc(sems["c_qk"], 1)

                emit_qt_t(0)
                te.wait_ge(sems["c_qte"], sched[("qte", 0)])
                for j, (off, w) in enumerate(step_chunks(0, CW)):
                    emit_qk(0, j, off, w)
                for s in range(T_OWN):
                    if s + 1 < T_OWN:
                        emit_qt_t(s + 1)
                        te.wait_ge(sems["c_qte"], sched[("qte", s + 1)])
                        for j, (off, w) in enumerate(step_chunks(s + 1, CW)):
                            emit_qk(s + 1, j, off, w)
                    else:
                        te.wait_ge(sems["c_xnq"], sched[("bxn",)])
                        for cc in range(Dc):
                            ins = te.transpose(pPT[0][:, cc * M:(cc + 1) * M],
                                               xn1bnd[:, cc * P:(cc + 1) * P],
                                               ident[0:M, 0:M])
                        ins.then_inc(sems["c_qtp"], 1)
                        te.wait_ge(sems["c_qte"], sched[("bqte",)])
                        for j in range(NTJ):
                            if j >= 2:
                                te.wait_ge(sems["c_exp"], sched[("bexp", j - 2)])
                            else:
                                pk = prev_ps_key(T_OWN, j)
                                if pk is not None:
                                    te.wait_ge(sems["c_exp"], sched[pk])
                            for cc in range(Dc):
                                for nn in range(0, CW, 512):
                                    ins = te.matmul(
                                        pS[j % 2][0:M, nn:nn + 512],
                                        QT[:, cc * M:(cc + 1) * M],
                                        KT[:, cc * S + j * CW + nn:
                                           cc * S + j * CW + nn + 512],
                                        start=(cc == 0), stop=(cc == Dc - 1),
                                        skip_group_check=True)
                            ins.then_inc(sems["c_qk"], 1)
                    te.wait_ge(sems["c_pfx"], sched[("pfx", s)])
                    nkb = 256 * (s + 1) // P
                    ngr = (nkb + 7) // 8

                    def tp_group(g):
                        kb0, kbn = g * 8, min(8, nkb - g * 8)
                        if g >= 2:
                            te.wait_ge(sems["c_pte"], sched[("pte", s, g - 2)])
                        for b in range(kbn):
                            kb = kb0 + b
                            c0 = pb_col(s, kb * P)
                            ins = te.transpose(
                                pPT[g % 2][:, b * P:(b + 1) * P],
                                Pb[:, c0:c0 + P], ident[:, :])
                        ins.then_inc(sems["c_ptp"], 1)

                    tp_group(0)
                    if ngr > 1:
                        tp_group(1)
                    for g in range(ngr):
                        kb0, kbn = g * 8, min(8, nkb - g * 8)
                        te.wait_ge(sems["c_pte"], sched[("pte", s, g)])
                        if g == 0 and s >= 1:
                            te.wait_ge(sems["c_attb"], sched[("attb", s - 1)])
                        for b in range(kbn):
                            kb = kb0 + b
                            for h in range(NH):
                                ins = te.matmul(
                                    pAV[:, h * 512:(h + 1) * 512],
                                    PT[:, (g % 2) * 8 * P + b * P:
                                       (g % 2) * 8 * P + (b + 1) * P],
                                    V[:, kb * D + h * 512: kb * D + (h + 1) * 512],
                                    start=(kb == 0), stop=(kb == nkb - 1),
                                    skip_group_check=True)
                        if g + 2 < ngr:
                            tp_group(g + 2)
                    ins.then_inc(sems["c_av"], 1)

                te.wait_ge(sems["c_pfx"], sched[("bpfx",)])

                def ttp(g):
                    kb0, kbn = g * 8, min(8, ST - g * 8)
                    if g >= 2:
                        te.wait_ge(sems["c_pte"], sched[("bpte", g - 2)])
                    for b in range(kbn):
                        kb = kb0 + b
                        ins = te.transpose(pPT[g % 2][:, b * M:(b + 1) * M],
                                           Pbt[:, kb * P:(kb + 1) * P],
                                           ident[0:M, 0:M])
                    ins.then_inc(sems["c_ptp"], 1)

                ttp(0)
                if NTG > 1:
                    ttp(1)
                for g in range(NTG):
                    kb0, kbn = g * 8, min(8, ST - g * 8)
                    te.wait_ge(sems["c_pte"], sched[("bpte", g)])
                    if g == 0:
                        te.wait_ge(sems["c_attb"], sched[("attb", T_OWN - 1)])
                    for b in range(kbn):
                        kb = kb0 + b
                        for h in range(NH):
                            ins = te.matmul(
                                pAV[0:M, h * 512:(h + 1) * 512],
                                PT[:, kb * M:(kb + 1) * M],
                                V[:, kb * D + h * 512: kb * D + (h + 1) * 512],
                                start=(kb == 0), stop=(kb == ST - 1),
                                skip_group_check=True)
                    if g + 2 < NTG:
                        ttp(g + 2)
                ins.then_inc(sems["c_av"], 1)

            if dbg:
                @blk.gpsimd
                def _(g):
                    g.wait_ge(sems["c_x2"], sched[("x2", T_OWN - 1)])
                    g.dma_start(dbg_KT.ap(), KT[:, 0:S]).then_inc(sems["c_dbg"], 16)
                    g.dma_start(dbg_V.ap(), V[:, 0:D]).then_inc(sems["c_dbg"], 16)
                    g.dma_start(dbg_Pb.ap(), Pb[:, 0:S]).then_inc(sems["c_dbg"], 16)
                    g.dma_start(dbg_xn1q.ap(), xn1q[:, D:2 * D]).then_inc(sems["c_dbg"], 16)
                    g.dma_start(dbg_st.ap()[:, 0:4], negmax[:, 0:4]).then_inc(sems["c_dbg"], 16)
                    g.dma_start(dbg_st.ap()[:, 4:8], lvec[:, 0:4]).then_inc(sems["c_dbg"], 16)
                    g.dma_start(dbg_st.ap()[:, 8:12], cvec[:, 0:4]).then_inc(sems["c_dbg"], 16)
                    g.dma_start(dbg_st.ap()[:, 12:13], rinv[:, 0:1], allow_slow_non_contiguous=True).then_inc(sems["c_dbg"], 16)
                    g.dma_start(dbg_st.ap()[:, 13:14], lsum[:, 0:1], allow_slow_non_contiguous=True).then_inc(sems["c_dbg"], 16)
                    g.wait_ge(sems["c_dbg"], 16 * 9)

        actx.close()

        FW = F_TPC * P
        with contextlib.ExitStack() as p3:
            E4 = p3.enter_context
            WR = E4(nc.sbuf_tensor("WR", [P, Dc * D], BF16))
            WK2 = E4(nc.sbuf_tensor("WK2", [P, Dc * D], BF16))
            WV2 = E4(nc.sbuf_tensor("WV2", [P, Dc * D], BF16))
            x2r = E4(nc.sbuf_tensor("x2r", [P, 2 * F_TPC * D], F32))
            xn2t = E4(nc.sbuf_tensor("xn2t", [P, 2 * D], BF16))
            xn2bndT = E4(nc.sbuf_tensor("xn2bndT", [P, Dc * M], BF16))
            XN2T = E4(nc.sbuf_tensor("XN2T", [P, Dc * FW], BF16))
            XN2ST = E4(nc.sbuf_tensor("XN2ST", [P, Dc * FW], BF16))
            XRT = E4(nc.sbuf_tensor("XRT", [P, Dc * FW], BF16))
            XKT = E4(nc.sbuf_tensor("XKT", [P, Dc * FW], BF16))
            rbuf = E4(nc.sbuf_tensor("rbuf", [P, F_TPC * D], BF16))
            kactT = E4(nc.sbuf_tensor("kactT", [P, Dc * FW], BF16))
            ffob = E4(nc.sbuf_tensor("ffob", [P, 4 * D], F32))
            lnb3 = E4(nc.sbuf_tensor("lnb3", [P, 2 * D], F32))
            outb = E4(nc.sbuf_tensor("outb", [P, 2 * D], F32))
            xn2bnd = E4(nc.sbuf_tensor("xn2bnd", [M, D], BF16))
            ssumt2 = E4(nc.sbuf_tensor("ssumt2", [M, 1], F32))
            ssqt2 = E4(nc.sbuf_tensor("ssqt2", [M, 1], F32))
            varept2 = E4(nc.sbuf_tensor("varept2", [M, 1], F32))
            lnvt2 = E4(nc.sbuf_tensor("lnvt2", [M, 1], F32))
            negmeant2 = E4(nc.sbuf_tensor("negmeant2", [M, 1], F32))
            rstdt2 = E4(nc.sbuf_tensor("rstdt2", [M, 1], F32))
            nmrt2 = E4(nc.sbuf_tensor("nmrt2", [M, 1], F32))
            sqv = E4(nc.sbuf_tensor("sqv", [P, D], F32))
            sqg = E4(nc.sbuf_tensor("sqg", [P, D], F32))
            lnS = E4(nc.sbuf_tensor("lnS", [P, 8], F32))
            eS = E4(nc.sbuf_tensor("eS", [P, 8], F32))
            zerov = E4(nc.sbuf_tensor("zerov", [P, 1], F32))
            pscr = E4(nc.sbuf_tensor("pscr", [P, 32], F32))
            pX2T = [E4(nc.psum_tensor(f"pX2T{i}", [P, Dc * P], BF16)) for i in range(2)]
            pRV = [E4(nc.psum_tensor(f"pRV{i}", [P, D], F32)) for i in range(2)]
            pKA = [E4(nc.psum_tensor(f"pKA{i}", [P, 512], F32)) for i in range(2)]
            blk = E4(nc.Block())
            @blk.gpsimd
            def _(g):
                g.memset(pscr[:, :], 0.0)
                g.dma_start(WR[:, :].rearrange("p (c w) -> p c w", c=Dc),
                            wr_t.ap().rearrange("(c p) w -> p c w", p=P)
                            ).then_inc(sems["c_w2"], 16)
                g.dma_start(rb_scr.ap()[:, 96:128], WR[:, 0:32]
                            ).then_inc(sems["c_w2"], 16)
                g.dma_start(WK2[:, :].rearrange("p (c w) -> p c w", c=Dc),
                            wk_t.ap().rearrange("(c p) w -> p c w", p=P)
                            ).then_inc(sems["c_w2"], 16)
                g.dma_start(rb_scr.ap()[:, 64:96], WK2[:, 0:32]
                            ).then_inc(sems["c_w2"], 16)
                g.dma_start(WV2[:, :].rearrange("p (c w) -> p c w", c=Dc),
                            wv_t.ap().rearrange("(c p) w -> p c w", p=P)
                            ).then_inc(sems["c_w2"], 16)
                g.dma_start(rb_scr.ap()[:, 32:64], WV2[:, 0:32]
                            ).then_inc(sems["c_w2"], 16)
                def emit_outb(u, t):
                    idx = u * F_TPC + t
                    g.wait_ge(sems["c_oe2"], sched[("f_oe2", u, t)])
                    if idx >= 2:
                        pu, pt = divmod(idx - 2, F_TPC)
                        g.wait_ge(sems["c_out" + str(pt % 2)],
                                  sched[("f_out", pu, pt)])
                    g.tensor_add(outb[:, (t % 2) * D:((t % 2) + 1) * D],
                                 x2r[:, (u % 2) * F_TPC * D + t * D:
                                     (u % 2) * F_TPC * D + (t + 1) * D],
                                 lnb3[:, (idx % 2) * D:((idx % 2) + 1) * D]
                                 ).then_inc(sems["c_oe3"], 1)
                    g.tensor_copy(pscr[:, 0:16], pscr[:, 16:32])
                    s_idx = u * F_TPC + t
                    g.dma_start(out.ap()[s_idx * P:(s_idx + 1) * P, :],
                                outb[:, (t % 2) * D:((t % 2) + 1) * D]
                                ).then_inc(sems["c_out" + str(t % 2)], 16)

                for u in range(NFC):
                    for t in range(F_TPC):
                        if u >= 1:
                            emit_outb(u - 1, t)
                for t in range(F_TPC):
                    emit_outb(NFC - 1, t)
                g.wait_ge(sems["c_out0"], C["c_out0"])
                g.wait_ge(sems["c_out1"], C["c_out1"])

            @blk.sync
            def _(sync):
                sync.wait_ge(sems["c_sp0"], C["c_sp0"])
                sync.wait_ge(sems["c_sp1"], C["c_sp1"])
                for u in range(NFC):
                    if u >= 2:
                        sync.wait_ge(sems["c_oe3"], sched[("f_oe3", u - 2, F_TPC - 1)])
                    src = x2_spill.ap()[u * FW:(u + 1) * FW, :].rearrange(
                        "(t p) d -> p t d", p=P)
                    dst = x2r[:, (u % 2) * F_TPC * D:((u % 2) + 1) * F_TPC * D
                              ].rearrange("p (t d) -> p t d", t=F_TPC)
                    sync.dma_start(dst, src).then_inc(sems["c_x2r" + str(u % 2)], 16)

            @blk.vector
            def _(v):
                warm_v(v)
                v.memset(zerov[:, :], 0.0)
                v.wait_ge(sems["c_qtp"], sched[("b2tp",)])
                v.tensor_copy(xn2bndT[:, :], pX2T[0][:, 0:Dc * M])
                v.tensor_copy(spcv[:, 1:3], spcv[:, 1:3]).then_inc(sems["c_qte"], 1)

                def part1(u, t):
                    idx = u * F_TPC + t
                    if idx >= 4:
                        pu, pt = divmod(idx - 4, F_TPC)
                        v.wait_ge(sems["c_xn"], sched[("f_xn", pu, pt)])
                    v.wait_ge(sems["c_x2r" + str(u % 2)], sched[("x2r", u)])
                    xt = x2r[:, (u % 2) * F_TPC * D + t * D:
                             (u % 2) * F_TPC * D + (t + 1) * D]
                    c0 = (idx % 4) * 2
                    v.tensor_mul(sqv[:, :], xt, xt)
                    v.reduce_sum(lnS[:, c0:c0 + 1], xt, axis=AX.X)
                    v.reduce_sum(lnS[:, c0 + 1:c0 + 2], sqv[:, :], axis=AX.X)
                    v.tensor_copy(spcv[:, 1:3], spcv[:, 1:3]
                                  ).then_inc(sems["c_lnst"], 1)

                def xevac(u, t):
                    idx = u * F_TPC + t
                    v.wait_ge(sems["c_xtp"], sched[("f_xtp", u, t)])
                    dst = XN2T.rearrange("p (c w) -> p c w", c=Dc)[:, :, t * P:(t + 1) * P]
                    src = pX2T[t % 2][:, :].rearrange("p (c q) -> p c q", c=Dc)
                    v.tensor_copy(dst, src)
                    v.tensor_copy(spcv[:, 1:3], spcv[:, 1:3]
                                  ).then_inc(sems["c_xte"], 1)
                    if t == 0 and u >= 1:
                        v.wait_ge(sems["c_rmm"], sched[("f_rmm", u - 1, F_TPC - 1)])
                        v.wait_ge(sems["c_kamm"], sched[("f_kamm", u - 1, Dc - 1)])
                    s4 = XN2T.rearrange("p (c t q) -> p c t q", c=Dc, t=F_TPC)
                    v4 = XN2ST.rearrange("p (c t q) -> p c t q", c=Dc, t=F_TPC)
                    if t == 0:
                        b4 = xn2bndT.rearrange("p (c m o) -> p c m o", c=Dc, o=1)
                        v.tensor_copy(v4[:, :, :, 0:1],
                                      b4[:, :, u * F_TPC:(u + 1) * F_TPC, :])
                    v.tensor_copy(v4[:, :, t:t + 1, 1:P], s4[:, :, t:t + 1, 0:P - 1])
                    for cc in range(Dc):
                        sl = slice(cc * FW + t * P, cc * FW + (t + 1) * P)
                        v.tensor_sub(XRT[:, sl], XN2T[:, sl], XN2ST[:, sl])
                    for cc in range(Dc):
                        sl = slice(cc * FW + t * P, cc * FW + (t + 1) * P)
                        v.scalar_tensor_tensor(XKT[:, sl], XRT[:, sl],
                                               mu[:, Dc + cc:Dc + cc + 1],
                                               XN2ST[:, sl], ALU.mult, ALU.add)
                    for cc in range(Dc):
                        sl = slice(cc * FW + t * P, cc * FW + (t + 1) * P)
                        v.scalar_tensor_tensor(XRT[:, sl], XRT[:, sl],
                                               mu[:, cc:cc + 1],
                                               XN2ST[:, sl], ALU.mult, ALU.add)
                    v.tensor_copy(spcv[:, 1:3], spcv[:, 1:3])
                    v.tensor_copy(spcv[:, 1:3], spcv[:, 1:3]).then_inc(sems["c_mix"], 1)

                def ffob_mul(u, t):
                    idx = u * F_TPC + t
                    v.wait_ge(sems["c_omm"], sched[("f_omm", u, t)])
                    if idx >= 4:
                        pu, pt = divmod(idx - 4, F_TPC)
                        v.wait_ge(sems["c_oe2"], sched[("f_oe2", pu, pt)])
                    fsl = ffob[:, (idx % 4) * D:((idx % 4) + 1) * D]
                    v.tensor_mul(fsl, pRV[t % 2][:, :], rbuf[:, t * D:(t + 1) * D])
                    v.tensor_copy(spcv[:, 1:3], spcv[:, 1:3]
                                  ).then_inc(sems["c_ffob"], 1)

                def est(u, t):
                    idx = u * F_TPC + t
                    fsl = ffob[:, (idx % 4) * D:((idx % 4) + 1) * D]
                    v.tensor_mul(sqg[:, :], fsl, fsl)
                    v.reduce_sum(eS[:, (idx % 4) * 2:(idx % 4) * 2 + 1], fsl,
                                 axis=AX.X)
                    v.reduce_sum(eS[:, (idx % 4) * 2 + 1:(idx % 4) * 2 + 2],
                                 sqg[:, :], axis=AX.X)
                    v.tensor_copy(spcv[:, 1:3], spcv[:, 1:3])
                    v.tensor_copy(spcv[:, 1:3], spcv[:, 1:3]
                                  ).then_inc(sems["c_est"], 1)

                for t in range(F_TPC):
                    part1(0, t)
                for u in range(NFC):
                    if u >= 1:
                        ffob_mul(u - 1, 0)
                        est(u - 1, 0)
                        ffob_mul(u - 1, 1)
                        est(u - 1, 1)
                    xevac(u, 0)
                    if u >= 1:
                        ffob_mul(u - 1, 2)
                        est(u - 1, 2)
                    xevac(u, 1)
                    if u >= 1:
                        ffob_mul(u - 1, 3)
                        est(u - 1, 3)
                    xevac(u, 2)
                    xevac(u, 3)
                    if u + 1 < NFC:
                        for t in range(F_TPC):
                            part1(u + 1, t)
                for t in range(F_TPC):
                    ffob_mul(NFC - 1, t)
                    est(NFC - 1, t)

            @blk.scalar
            def _(sc):
                warm_sc(sc)
                sc.wait_ge(sems["c_x2"], sched[("bx2",)])
                ln_act(sc, x2bnd[:, :], xn2bnd[:, :], "c_xnq", ("b2xn",),
                       bufs=(ssumt2, ssqt2, varept2, lnvt2, negmeant2, rstdt2,
                             nmrt2), zsc=zmask[:, 0:1])
                for u in range(NFC):
                    for t in range(F_TPC):
                        idx = u * F_TPC + t
                        if idx > 1:
                            pu, pt = divmod(idx - 2, F_TPC)
                            sc.wait_ge(sems["c_xtp"], sched[("f_xtp", pu, pt)])
                        sc.wait_ge(sems["c_lnst"], sched[("f_st", u, t)])
                        xt = x2r[:, (u % 2) * F_TPC * D + t * D:
                                 (u % 2) * F_TPC * D + (t + 1) * D]
                        c0 = (idx % 4) * 2
                        ln_act(sc, xt, xn2t[:, (idx % 2) * D:((idx % 2) + 1) * D],
                               "c_xn", ("f_xn", u, t),
                               stats=(lnS[:, c0:c0 + 1], lnS[:, c0 + 1:c0 + 2]))
                    def emit_epi(uu):
                        for t in range(F_TPC):
                            idx = uu * F_TPC + t
                            sc.wait_ge(sems["c_est"], sched[("f_et", uu, t)])
                            if idx >= 2:
                                pu, pt = divmod(idx - 2, F_TPC)
                                sc.wait_ge(sems["c_oe3"], sched[("f_oe3", pu, pt)])
                            ln_act(sc, ffob[:, (idx % 4) * D:((idx % 4) + 1) * D],
                                   lnb3[:, (idx % 2) * D:((idx % 2) + 1) * D],
                                   "c_oe2", ("f_oe2", uu, t),
                                   stats=(eS[:, (idx % 4) * 2:(idx % 4) * 2 + 1],
                                          eS[:, (idx % 4) * 2 + 1:(idx % 4) * 2 + 2]))

                    if u >= 1:
                        emit_epi(u - 1)
                    for t in range(F_TPC):
                        sc.wait_ge(sems["c_rmm"], sched[("f_rmm", u, t)])
                        if u >= 1:
                            sc.wait_ge(sems["c_ffob"], sched[("ffob", u - 1, t)])
                        sc.activation(rbuf[:, t * D:(t + 1) * D], pRV[t % 2][:, :],
                                      AF.Sigmoid)
                        sc.activation(spca[:, 0:1], epsv[:, 0:1], AF.Identity
                                      ).then_inc(sems["c_rev"], 1)
                    for dkc in range(Dc):
                        sc.wait_ge(sems["c_kamm"], sched[("f_kamm", u, dkc)])
                        sc.activation(pKA[dkc % 2][:, :], pKA[dkc % 2][:, :], AF.Relu)
                        sc.activation(kactT[:, dkc * FW:(dkc + 1) * FW],
                                      pKA[dkc % 2][:, :], AF.Square)
                        sc.activation(spca[:, 0:1], epsv[:, 0:1], AF.Identity)
                        sc.activation(spca[:, 0:1], epsv[:, 0:1], AF.Identity
                                      ).then_inc(sems["c_kaev"], 1)
                    if u == NFC - 1:
                        emit_epi(u)

            @blk.tensor
            def _(te):
                te.wait_ge(sems["c_xnq"], sched[("b2xn",)])
                for cc in range(Dc):
                    ins = te.transpose(pX2T[0][:, cc * M:(cc + 1) * M],
                                       xn2bnd[:, cc * P:(cc + 1) * P], ident[0:M, 0:M])
                ins.then_inc(sems["c_qtp"], 1)
                te.wait_ge(sems["c_qte"], sched[("b2te",)])
                for u in range(NFC):

                    def tp_tile(t):
                        idx = u * F_TPC + t
                        te.wait_ge(sems["c_xn"], sched[("f_xn", u, t)])
                        if idx >= 2:
                            pu, pt = divmod(idx - 2, F_TPC)
                            te.wait_ge(sems["c_xte"], sched[("f_xte", pu, pt)])
                        for cc in range(Dc):
                            ins = te.transpose(
                                pX2T[t % 2][:, cc * P:(cc + 1) * P],
                                xn2t[:, (idx % 2) * D + cc * P:
                                     (idx % 2) * D + (cc + 1) * P], ident[:, :])
                        ins.then_inc(sems["c_xtp"], 1)

                    def rmm(t):
                        seq = u * 8 + t
                        if u == 0 and t == 0:
                            te.wait_ge(sems["c_w2"], sched[("wr",)])
                        te.wait_ge(sems["c_mix"], sched[("f_mix", u, t)])
                        if seq >= 2:
                            pu2, sl = divmod(seq - 2, 8)
                            if sl < F_TPC:
                                te.wait_ge(sems["c_rev"], sched[("f_rev", pu2, sl)])
                            else:
                                te.wait_ge(sems["c_ffob"],
                                           sched[("ffob", pu2, sl - F_TPC)])
                        for h in range(NH):
                            for cc in range(Dc):
                                ins = te.matmul(
                                    pRV[t % 2][:, h * 512:(h + 1) * 512],
                                    XRT[:, cc * FW + t * P: cc * FW + (t + 1) * P],
                                    WR[:, cc * D + h * 512: cc * D + (h + 1) * 512],
                                    start=(cc == 0), stop=(cc == Dc - 1))
                        ins.then_inc(sems["c_rmm"], 1)

                    tp_tile(0)
                    tp_tile(1)
                    rmm(0)
                    tp_tile(2)
                    rmm(1)
                    tp_tile(3)
                    rmm(2)
                    rmm(3)
                    te.wait_ge(sems["c_mix"], sched[("f_mix", u, F_TPC - 1)])
                    if u == 0:
                        te.wait_ge(sems["c_w2"], sched[("wk2",)])
                    for dkc in range(Dc):
                        idx = u * Dc + dkc
                        if idx >= 2:
                            pu, pd = divmod(idx - 2, Dc)
                            te.wait_ge(sems["c_kaev"], sched[("f_kaev", pu, pd)])
                        for cc in range(Dc):
                            ins = te.matmul(
                                pKA[dkc % 2][:, :],
                                WK2[:, cc * D + dkc * P: cc * D + (dkc + 1) * P],
                                XKT[:, cc * FW:(cc + 1) * FW],
                                start=(cc == 0), stop=(cc == Dc - 1))
                        ins.then_inc(sems["c_kamm"], 1)
                    te.wait_ge(sems["c_kaev"], sched[("f_kaev", u, Dc - 1)])
                    if u == 0:
                        te.wait_ge(sems["c_w2"], sched[("wv2",)])
                    for t in range(F_TPC):
                        te.wait_ge(sems["c_rev"], sched[("f_rev", u, t)])
                        pu2, sl = divmod(u * 8 + 4 + t - 2, 8)
                        if sl < F_TPC:
                            te.wait_ge(sems["c_rev"], sched[("f_rev", pu2, sl)])
                        else:
                            te.wait_ge(sems["c_ffob"], sched[("ffob", pu2, sl - F_TPC)])
                        for h in range(NH):
                            for dkc in range(Dc):
                                ins = te.matmul(
                                    pRV[t % 2][:, h * 512:(h + 1) * 512],
                                    kactT[:, dkc * FW + t * P: dkc * FW + (t + 1) * P],
                                    WV2[:, dkc * D + h * 512: dkc * D + (h + 1) * 512],
                                    start=(dkc == 0), stop=(dkc == Dc - 1))
                        ins.then_inc(sems["c_omm"], 1)

    nc.finalize()
    return nc


def make_in_maps(x_full, W, Wr, Wk, Wv, mur, muk, S=4096, D=1024, n_cores=8):
    T_OWN = S // 256
    wkv_t = np.ascontiguousarray(W.T)
    wr_t = np.ascontiguousarray(Wr.T)
    wk_t = np.ascontiguousarray(Wk.T)
    wv_t = np.ascontiguousarray(Wv.T)
    Dc = D // P
    mu = np.ascontiguousarray(
        np.concatenate([mur.reshape(Dc, P).T, muk.reshape(Dc, P).T], axis=1))

    tri = np.where(np.arange(P)[:, None] >= np.arange(P)[None, :], 0.0, NEG
                   ).astype(ml_dtypes.bfloat16)

    in_maps = []
    for c in range(n_cores):
        b, r = divmod(c, 2)
        xb = np.ascontiguousarray(x_full[b])
        masks = np.zeros((P, 256), ml_dtypes.bfloat16)
        if r == 0:
            masks[:, 0:128] = tri
            masks[:, 128:256] = ml_dtypes.bfloat16(NEG)
        else:
            masks[:, 128:256] = tri
        xq = np.empty((T_OWN * P, D), np.float32)
        x1bnd = np.zeros((T_OWN, D), np.float32)
        zmask = np.zeros((T_OWN, 1), np.float32)
        tmask = np.full((T_OWN, S), NEG, ml_dtypes.bfloat16)
        for s in range(T_OWN):
            p = 2 * s + r
            xq[s * P:(s + 1) * P] = xb[p * P:(p + 1) * P]
            bnd = p * P - 1
            if bnd >= 0:
                x1bnd[s] = xb[bnd]
                zmask[s] = 1.0
                tmask[s, :bnd + 1] = 0.0
            else:
                tmask[s, 0] = 0.0
        in_maps.append({
            "x": xb, "xq_in": xq,
            "wkv_t": wkv_t, "wr_t": wr_t, "wk_t": wk_t, "wv_t": wv_t,
            "mu_in": mu, "masks_in": masks, "tmask_in": tmask,
            "x1bnd_in": x1bnd, "zmask_in": zmask,
        })
    return in_maps


def assemble_out(results, S=4096, D=1024, n_cores=8):
    T_OWN = S // 256
    out_full = np.empty((n_cores // 2, S, D), np.float32)
    for c in range(n_cores):
        b, r = divmod(c, 2)
        o = results[c]["out"]
        for s in range(T_OWN):
            p = 2 * s + r
            out_full[b, p * P:(p + 1) * P, :] = o[s * P:(s + 1) * P, :]
    return out_full


def kernel(L0_attn_x, L0_attn_W, L0_ffn_W_r, L0_ffn_W_k, L0_ffn_W_v,
           L0_ffn_mu_r, L0_ffn_mu_k):
    S, D = 4096, 1024
    x_full = np.ascontiguousarray(np.asarray(L0_attn_x, dtype=np.float32))
    in_maps = make_in_maps(
        x_full,
        np.asarray(L0_attn_W, np.float32),
        np.asarray(L0_ffn_W_r, np.float32),
        np.asarray(L0_ffn_W_k, np.float32),
        np.asarray(L0_ffn_W_v, np.float32),
        np.asarray(L0_ffn_mu_r, np.float32),
        np.asarray(L0_ffn_mu_k, np.float32),
        S=S, D=D)
    nc = build_program(S, D)
    res = run_bass_kernel_spmd(nc, in_maps, core_ids=list(range(8)))
    return assemble_out(res.results, S=S, D=D)


# version 25
# speedup vs baseline: 1.0999x; 1.0694x over previous
import contextlib
from collections import defaultdict

import numpy as np
import ml_dtypes
import concourse.bass as bass
import concourse.mybir as mybir
from concourse.bass_utils import run_bass_kernel_spmd

F32 = mybir.dt.float32
BF16 = mybir.dt.bfloat16
AF = mybir.ActivationFunctionType
ALU = mybir.AluOpType
AX = mybir.AxisListType

NEG = -1.0e30
P = 128


def step_chunks(s, CW=1024):
    keys = 256 * (s + 1)
    out = []
    off = 0
    while off < keys:
        w = min(CW, keys - off)
        out.append((off, w))
        off += w
    return out


def build_program(S=4096, D=1024, dbg=False):
    Dc = D // P
    DKV = 2 * D
    T_OWN = S // 256
    SCH = 256
    NCH = S // SCH
    TPC = SCH // P
    CW = 1024
    ST = S // P
    F_TPC = 4
    NFC = T_OWN // F_TPC
    NH = D // 512
    M = T_OWN

    nc = bass.Bass("TRN2", target_bir_lowering=False, detect_race_conditions=False)

    x = nc.dram_tensor("x", [S, D], F32, kind="ExternalInput")
    xq_in = nc.dram_tensor("xq_in", [T_OWN * P, D], F32, kind="ExternalInput")
    wkv_t = nc.dram_tensor("wkv_t", [D, DKV], F32, kind="ExternalInput")
    wr_t = nc.dram_tensor("wr_t", [D, D], F32, kind="ExternalInput")
    wk_t = nc.dram_tensor("wk_t", [D, D], F32, kind="ExternalInput")
    wv_t = nc.dram_tensor("wv_t", [D, D], F32, kind="ExternalInput")
    mu_in = nc.dram_tensor("mu_in", [P, 2 * Dc], F32, kind="ExternalInput")
    masks_in = nc.dram_tensor("masks_in", [P, 256], BF16, kind="ExternalInput")
    tmask_in = nc.dram_tensor("tmask_in", [M, S], BF16, kind="ExternalInput")
    x1bnd_in = nc.dram_tensor("x1bnd_in", [M, D], F32, kind="ExternalInput")
    zmask_in = nc.dram_tensor("zmask_in", [M, 1], F32, kind="ExternalInput")
    out = nc.dram_tensor("out", [T_OWN * P, D], F32, kind="ExternalOutput")
    rb_scr = nc.dram_tensor("rb_scr", [P, 128], BF16, kind="Internal")
    x2_spill = nc.dram_tensor("x2_spill", [T_OWN * P, D], F32,
                              kind="ExternalOutput" if dbg else "Internal")
    if dbg:
        dbg_xnt = nc.dram_tensor("dbg_xnt", [P, 2048], BF16, kind="ExternalOutput")
        dbg_xn1t = nc.dram_tensor("dbg_xn1t", [P, D], BF16, kind="ExternalOutput")
        dbg_KT = nc.dram_tensor("dbg_KT", [P, S], BF16, kind="ExternalOutput")
        dbg_V = nc.dram_tensor("dbg_V", [P, D], BF16, kind="ExternalOutput")
        dbg_Pb = nc.dram_tensor("dbg_Pb", [P, S], BF16, kind="ExternalOutput")
        dbg_st = nc.dram_tensor("dbg_st", [P, 16], F32, kind="ExternalOutput")
        dbg_x2bnd = nc.dram_tensor("dbg_x2bnd", [M, D], F32, kind="ExternalOutput")
        dbg_xn1q = nc.dram_tensor("dbg_xn1q", [P, D], BF16, kind="ExternalOutput")
    ident_dram = nc.inline_tensor(np.eye(P, dtype=ml_dtypes.bfloat16), "ident_const")

    cnt = defaultdict(int)
    sched = {}

    def bump(sem, key, n=1):
        cnt[sem] += n
        if key is not None:
            sched[key] = cnt[sem]
        return cnt[sem]

    bump("c_cst", ("cst",), 16 * 3)
    bump("c_dbg", ("dbg",), 16 * 999)
    bump("c_w1", ("wkv",), 32)

    for ch in range(NCH):
        bump("c_x" + str(ch % 3), ("x", ch), 16)
        for t in range(TPC):
            bump("c_lnst", ("p1st", ch, t))
            bump("c_xn", ("p1xn", ch, t))
            bump("c_xtp", ("p1xtp", ch, t))
            bump("c_xte", ("p1xte", ch, t))
        for dkc in range(Dc):
            bump("c_kmm", ("p1kmm", ch, dkc))
            bump("c_kev", ("p1kev", ch, dkc))
        for t in range(TPC):
            bump("c_vmm", ("p1vmm", ch, t))
            bump("c_vev", ("p1vev", ch, t))

    bump("c_bnd", ("mb",), 16)
    for s in range(T_OWN):
        bump("c_xq" + str(s % 4), ("xq", s), 16)
        bump("c_xnq", ("xnq", s))
        bump("c_qtp", ("qtp", s))
        bump("c_qte", ("qte", s))
        chunks = step_chunks(s, CW)
        for j in range(len(chunks)):
            bump("c_qk", ("qk", s, j))
            bump("c_max", ("max", s, j))
            bump("c_exp", ("exp", s, j))
        if len(chunks) > 1:
            bump("c_fx1", ("fx1", s))
        bump("c_cv", ("cv", s))
        bump("c_pfx", ("pfx", s))
        nkb = 256 * (s + 1) // P
        for g in range((nkb + 7) // 8):
            bump("c_ptp", ("ptp", s, g))
            bump("c_pte", ("pte", s, g))
        bump("c_av", ("av", s))
        bump("c_attb", ("attb", s))
        bump("c_ep2", ("ep2", s))
        bump("c_x2", ("x2", s))
        bump("c_sp" + str(s % 2), ("spill", s), 16)

    NTJ = S // CW
    bump("c_bnd", ("bnd",), 32)
    bump("c_xnq", ("bxn",))
    bump("c_qtp", ("bqtp",))
    bump("c_qte", ("bqte",))
    for j in range(NTJ):
        bump("c_qk", ("bqk", j))
        bump("c_max", ("bmax", j))
        bump("c_exp", ("bexp", j))
    bump("c_fx1", ("bfx1",))
    bump("c_cv", ("bcv",))
    bump("c_pfx", ("bpfx",))
    NTG = (ST + 7) // 8
    for g in range(NTG):
        bump("c_ptp", ("bptp", g))
        bump("c_pte", ("bpte", g))
    bump("c_av", ("bav",))
    bump("c_attb", ("battb",))
    bump("c_ep2", ("bep2",))
    bump("c_x2", ("bx2",))

    bump("c_w2", ("wr",), 32)
    bump("c_w2", ("wk2",), 32)
    bump("c_w2", ("wv2",), 32)
    bump("c_xnq", ("b2xn",))
    bump("c_qtp", ("b2tp",))
    bump("c_qte", ("b2te",))
    for u in range(NFC):
        bump("c_x2r" + str(u % 2), ("x2r", u), 16)
        for t in range(F_TPC):
            bump("c_lnst", ("f_st", u, t))
            bump("c_xn", ("f_xn", u, t))
            bump("c_xtp", ("f_xtp", u, t))
            bump("c_xte", ("f_xte", u, t))
            bump("c_mix", ("f_mix", u, t))
        for t in range(F_TPC):
            bump("c_rmm", ("f_rmm", u, t))
            bump("c_rev", ("f_rev", u, t))
        for dkc in range(Dc):
            bump("c_kamm", ("f_kamm", u, dkc))
            bump("c_kaev", ("f_kaev", u, dkc))
        for t in range(F_TPC):
            bump("c_omm", ("f_omm", u, t))
            bump("c_ffob", ("ffob", u, t))
            bump("c_est", ("f_et", u, t))
            bump("c_oe2", ("f_oe2", u, t))
            bump("c_oe3", ("f_oe3", u, t))
            bump("c_out" + str(t % 2), ("f_out", u, t), 16)

    C = dict(cnt)

    def prev_ps_key(s, j):
        if j >= 2:
            return ("exp", s, j - 2)
        ss = s - 1
        while ss >= 0:
            cand = [jj for jj in range(len(step_chunks(ss, CW))) if jj % 2 == j % 2]
            if cand:
                return ("exp", ss, cand[-1])
            ss -= 1
        return None

    with contextlib.ExitStack() as ctx:
        E = ctx.enter_context
        sems = {name: E(nc.semaphore(name)) for name in C}

        ident = E(nc.sbuf_tensor("ident", [P, P], BF16))
        epsv = E(nc.sbuf_tensor("epsv", [P, 1], F32))
        mu = E(nc.sbuf_tensor("mu", [P, 2 * Dc], F32))
        zmask = E(nc.sbuf_tensor("zmask", [M, 1], F32))
        x2bnd = E(nc.sbuf_tensor("x2bnd", [M, D], F32))
        ssum = E(nc.sbuf_tensor("ssum", [P, 1], F32))
        ssq = E(nc.sbuf_tensor("ssq", [P, 1], F32))
        varep = E(nc.sbuf_tensor("varep", [P, 1], F32))
        lnv = E(nc.sbuf_tensor("lnv", [P, 1], F32))
        negmean = E(nc.sbuf_tensor("negmean", [P, 1], F32))
        rstd = E(nc.sbuf_tensor("rstd", [P, 1], F32))
        nmr = E(nc.sbuf_tensor("nmr", [P, 1], F32))
        spcv = E(nc.sbuf_tensor("spcv", [P, 4], F32))
        spca = E(nc.sbuf_tensor("spca", [P, 1], F32))
        wup = E(nc.sbuf_tensor("wup", [P, 1], F32))
        sqscr = E(nc.sbuf_tensor("sqscr", [P, D], BF16))

        def warm_sc(sc):
            sc.wait_ge(sems["c_cst"], sched[("cst",)])
            sc.activation(sqscr[:, 0:P], ident[:, 0:P], AF.Square, accum_out=wup[:, 0:1])
            sc.activation(sqscr[:, 0:P], ident[:, 0:P], AF.Square, accum_out=wup[:, 0:1])
            sc.activation(spca[:, 0:1], wup[:, 0:1], AF.Ln)
            sc.activation(spca[:, 0:1], spca[:, 0:1], AF.Exp, scale=-0.5)
            sc.activation(spca[:, 0:1], ident[:, 0:1], AF.Identity,
                          bias=wup[:, 0:1], scale=wup[:, 0:1])

        def warm_v(v):
            v.memset(spcv[:, :], 0.0)
            v.wait_ge(sems["c_cst"], sched[("cst",)])
            v.reduce_sum(wup[:, 0:1], ident[:, 0:64], axis=AX.X)
            v.tensor_copy(spcv[:, 0:1], spcv[:, 0:1])
            v.reduce_max(wup[:, 0:1], ident[:, 0:64], axis=AX.X, negate=True)
            v.tensor_copy(spcv[:, 0:1], spcv[:, 0:1])

        RD = 1.0 / D

        def ln_act(sc, src, dst, sem=None, key=None, bufs=None, zsc=None, stats=None):
            st, qq, vp, lv, nme, rs, nr = bufs if bufs is not None else (
                ssum, ssq, varep, lnv, negmean, rstd, nmr)
            n = src.shape[0]

            def spc():
                sc.activation(spca[0:n, 0:1], epsv[0:n, 0:1], AF.Identity)

            if stats is None:
                sc.activation(sqscr[0:n, :], src, AF.Identity, accum_out=st[:, 0:1])
                sc.activation(sqscr[0:n, :], src, AF.Square, accum_out=qq[:, 0:1])
            else:
                st, qq = stats
            sc.activation(vp[:, :], st[:, :], AF.Square, scale=RD)
            sc.activation(lv[:, :], vp[:, :], AF.Identity, scale=-1.0,
                          bias=epsv[0:n, 0:1])
            sc.activation(nme[:, :], st[:, :], AF.Identity, scale=-RD)
            sc.activation(vp[:, :], qq[:, :], AF.Ln, scale=RD, bias=lv[:, 0:1])
            sc.activation(rs[:, :], vp[:, :], AF.Exp, scale=-0.5)
            if zsc is not None:
                spc()
                sc.activation(lv[:, :], rs[:, :], AF.Identity, scale=zsc)
                spc()
                rs = lv
            sc.activation(nr[:, :], nme[:, :], AF.Identity, scale=rs[:, 0:1])
            spc()
            sc.activation(dst, src, AF.Identity, bias=nr[:, 0:1],
                          scale=rs[:, 0:1])
            spc()
            ins = sc.activation(spca[0:n, 0:1], epsv[0:n, 0:1], AF.Identity)
            if sem is not None:
                ins.then_inc(sems[sem], 1)

        actx = contextlib.ExitStack()
        EA = actx.enter_context
        KT = EA(nc.sbuf_tensor("KT", [P, Dc * S], BF16))
        V = EA(nc.sbuf_tensor("V", [P, ST * D], BF16))

        with contextlib.ExitStack() as p1:
            E1 = p1.enter_context
            WKV = E1(nc.sbuf_tensor("WKV", [P, Dc * DKV], BF16))
            xbuf = E1(nc.sbuf_tensor("xbuf", [P, 3 * TPC * D], F32))
            sqv1 = E1(nc.sbuf_tensor("sqv1", [P, D], F32))
            lnS1 = E1(nc.sbuf_tensor("lnS1", [P, 8], F32))
            xnt_roll = E1(nc.sbuf_tensor("xnt_roll", [P, 2 * Dc * SCH], BF16))
            xn1q_t = E1(nc.sbuf_tensor("xn1q_t", [P, D], BF16))
            pXT = [E1(nc.psum_tensor(f"pXT{i}", [P, Dc * P], BF16)) for i in range(2)]
            pKT = [E1(nc.psum_tensor(f"pKT{i}", [P, SCH], F32)) for i in range(2)]
            pV = [E1(nc.psum_tensor(f"pV{i}", [P, D], F32)) for i in range(2)]
            blk = E1(nc.Block())
            @blk.sync
            def _(sync):
                sync.dma_start(ident[:, :], ident_dram.ap()).then_inc(sems["c_cst"], 16)
                sync.dma_start(mu[:, :], mu_in.ap()).then_inc(sems["c_cst"], 16)
                sync.dma_start(zmask[:, :], zmask_in.ap()).then_inc(sems["c_cst"], 16)
                for ch in range(NCH):
                    if ch >= 3:
                        sync.wait_ge(sems["c_xn"], sched[("p1xn", ch - 3, TPC - 1)])
                    src = x.ap()[ch * SCH:(ch + 1) * SCH, :].rearrange(
                        "(t p) d -> p t d", p=P)
                    dst = xbuf[:, (ch % 3) * TPC * D:((ch % 3) + 1) * TPC * D
                               ].rearrange("p (t d) -> p t d", t=TPC)
                    sync.dma_start(dst, src).then_inc(sems["c_x" + str(ch % 3)], 16)

            @blk.gpsimd
            def _(g):
                g.dma_start(WKV[:, :].rearrange("p (c w) -> p c w", c=Dc),
                            wkv_t.ap().rearrange("(c p) w -> p c w", p=P)
                            ).then_inc(sems["c_w1"], 16)
                g.dma_start(rb_scr.ap()[:, 0:32], WKV[:, 0:32]
                            ).then_inc(sems["c_w1"], 16)
                if dbg:
                    g.wait_ge(sems["c_xn"], sched[("p1xn", 0, 0)])
                    g.dma_start(dbg_xn1t.ap(), xn1q_t[:, :]).then_inc(sems["c_dbg"], 16)
                    g.wait_ge(sems["c_xte"], sched[("p1xte", 0, TPC - 1)])
                    g.dma_start(dbg_xnt.ap(), xnt_roll[:, 0:Dc * SCH]
                                ).then_inc(sems["c_dbg"], 16)

            @blk.vector
            def _(v):
                v.memset(epsv[:, :], 1e-5)
                warm_v(v)
                def p1_part1(ch, t):
                    idx = ch * TPC + t
                    if idx >= 4:
                        pc, pt = divmod(idx - 4, TPC)
                        v.wait_ge(sems["c_xn"], sched[("p1xn", pc, pt)])
                    v.wait_ge(sems["c_x" + str(ch % 3)], sched[("x", ch)])
                    xt = xbuf[:, (ch % 3) * TPC * D + t * D:
                              (ch % 3) * TPC * D + (t + 1) * D]
                    c0 = (idx % 4) * 2
                    v.tensor_mul(sqv1[:, :], xt, xt)
                    v.reduce_sum(lnS1[:, c0:c0 + 1], xt, axis=AX.X)
                    v.reduce_sum(lnS1[:, c0 + 1:c0 + 2], sqv1[:, :], axis=AX.X)
                    v.tensor_copy(spcv[:, 1:3], spcv[:, 1:3]
                                  ).then_inc(sems["c_lnst"], 1)

                for t in range(TPC):
                    p1_part1(0, t)
                for ch in range(NCH):
                    for t in range(TPC):
                        if t == 0 and ch >= 2:
                            v.wait_ge(sems["c_vmm"], sched[("p1vmm", ch - 2, TPC - 1)])
                        if dbg and ch == 2 and t == 0:
                            v.wait_ge(sems["c_dbg"], 32)
                        v.wait_ge(sems["c_xtp"], sched[("p1xtp", ch, t)])
                        base = (ch % 2) * Dc * SCH
                        dst = xnt_roll[:, base:base + Dc * SCH].rearrange(
                            "p (c s) -> p c s", c=Dc)[:, :, t * P:(t + 1) * P]
                        src = pXT[t % 2][:, :].rearrange("p (c q) -> p c q", c=Dc)
                        v.tensor_copy(dst, src)
                        v.tensor_copy(spcv[:, 3:4], epsv[:, 0:1]
                                      ).then_inc(sems["c_xte"], 1)
                    if ch + 1 < NCH:
                        for t in range(TPC):
                            p1_part1(ch + 1, t)
                    for t in range(TPC):
                        v.wait_ge(sems["c_vmm"], sched[("p1vmm", ch, t)])
                        stile = ch * TPC + t
                        v.tensor_copy(V[:, stile * D:(stile + 1) * D],
                                      pV[t % 2][:, :]).then_inc(sems["c_vev"], 1)

            @blk.scalar
            def _(sc):
                warm_sc(sc)
                for ch in range(NCH):
                    for t in range(TPC):
                        idx = ch * TPC + t
                        if idx > 0:
                            pch, pt = divmod(idx - 1, TPC)
                            sc.wait_ge(sems["c_xtp"], sched[("p1xtp", pch, pt)])
                        xt = xbuf[:, (ch % 3) * TPC * D + t * D:
                                  (ch % 3) * TPC * D + (t + 1) * D]
                        if dbg and ch == 0 and t == 1:
                            sc.wait_ge(sems["c_dbg"], 16)
                        sc.wait_ge(sems["c_lnst"], sched[("p1st", ch, t)])
                        c0 = ((ch * TPC + t) % 4) * 2
                        ln_act(sc, xt, xn1q_t[:, :], "c_xn", ("p1xn", ch, t),
                               stats=(lnS1[:, c0:c0 + 1], lnS1[:, c0 + 1:c0 + 2]))
                    for dkc in range(Dc):
                        sc.wait_ge(sems["c_kmm"], sched[("p1kmm", ch, dkc)])
                        sc.copy(KT[:, dkc * S + ch * SCH: dkc * S + (ch + 1) * SCH],
                                pKT[dkc % 2][:, :]).then_inc(sems["c_kev"], 1)

            @blk.tensor
            def _(te):
                te.wait_ge(sems["c_cst"], sched[("cst",)])
                te.wait_ge(sems["c_w1"], sched[("wkv",)])
                for ch in range(NCH):
                    rbase = (ch % 2) * Dc * SCH

                    def p1_xt(t):
                        idx = ch * TPC + t
                        te.wait_ge(sems["c_xn"], sched[("p1xn", ch, t)])
                        if idx >= 2:
                            pch, pt = divmod(idx - 2, TPC)
                            te.wait_ge(sems["c_xte"], sched[("p1xte", pch, pt)])
                        for cc in range(Dc):
                            ins = te.transpose(
                                pXT[t % 2][:, cc * P:(cc + 1) * P],
                                xn1q_t[:, cc * P:(cc + 1) * P], ident[:, :])
                        ins.then_inc(sems["c_xtp"], 1)

                    def p1_v(t):
                        idx = ch * TPC + t
                        te.wait_ge(sems["c_xte"], sched[("p1xte", ch, t)])
                        if idx >= 2:
                            pch, pt = divmod(idx - 2, TPC)
                            te.wait_ge(sems["c_vev"], sched[("p1vev", pch, pt)])
                        for cc in range(Dc):
                            for h in range(NH):
                                ins = te.matmul(
                                    pV[t % 2][:, h * 512:(h + 1) * 512],
                                    xnt_roll[:, rbase + cc * SCH + t * P:
                                             rbase + cc * SCH + (t + 1) * P],
                                    WKV[:, cc * DKV + D + h * 512:
                                        cc * DKV + D + (h + 1) * 512],
                                    start=(cc == 0), stop=(cc == Dc - 1))
                        ins.then_inc(sems["c_vmm"], 1)

                    p1_xt(0)
                    p1_v(0)
                    p1_xt(1)
                    te.wait_ge(sems["c_xte"], sched[("p1xte", ch, TPC - 1)])
                    for dkc in range(Dc):
                        kidx = ch * Dc + dkc
                        if kidx >= 2:
                            pch, pd = divmod(kidx - 2, Dc)
                            te.wait_ge(sems["c_kev"], sched[("p1kev", pch, pd)])
                        for cc in range(Dc):
                            ins = te.matmul(
                                pKT[dkc % 2][:, :],
                                WKV[:, cc * DKV + dkc * P: cc * DKV + (dkc + 1) * P],
                                xnt_roll[:, rbase + cc * SCH: rbase + (cc + 1) * SCH],
                                start=(cc == 0), stop=(cc == Dc - 1))
                        ins.then_inc(sems["c_kmm"], 1)
                    p1_v(1)

        NTJ = S // CW
        NTG = (ST + 7) // 8
        with contextlib.ExitStack() as p2:
            E2 = p2.enter_context
            maskb = E2(nc.sbuf_tensor("maskb", [P, 256], BF16))
            tmaskb = E2(nc.sbuf_tensor("tmaskb", [M, S], BF16))
            x1bnd = E2(nc.sbuf_tensor("x1bnd", [M, D], F32))
            xn1bnd = E2(nc.sbuf_tensor("xn1bnd", [M, D], BF16))
            Pbt = E2(nc.sbuf_tensor("Pbt", [M, S], BF16))
            negmaxt = E2(nc.sbuf_tensor("negmaxt", [M, 4], F32))
            lvect = E2(nc.sbuf_tensor("lvect", [M, 4], F32))
            cvect = E2(nc.sbuf_tensor("cvect", [M, 4], F32))
            cvect2 = E2(nc.sbuf_tensor("cvect2", [M, 4], F32))
            negMt = E2(nc.sbuf_tensor("negMt", [M, 1], F32))
            lsumt = E2(nc.sbuf_tensor("lsumt", [M, 1], F32))
            rinvt = E2(nc.sbuf_tensor("rinvt", [M, 1], F32))
            ssumt = E2(nc.sbuf_tensor("ssumt", [M, 1], F32))
            ssqt = E2(nc.sbuf_tensor("ssqt", [M, 1], F32))
            varept = E2(nc.sbuf_tensor("varept", [M, 1], F32))
            lnvt = E2(nc.sbuf_tensor("lnvt", [M, 1], F32))
            negmeant = E2(nc.sbuf_tensor("negmeant", [M, 1], F32))
            rstdt = E2(nc.sbuf_tensor("rstdt", [M, 1], F32))
            nmrt = E2(nc.sbuf_tensor("nmrt", [M, 1], F32))
            tbufs = (ssumt, ssqt, varept, lnvt, negmeant, rstdt, nmrt)

            x1q = E2(nc.sbuf_tensor("x1q", [P, 4 * D], F32))
            xn1q = E2(nc.sbuf_tensor("xn1q", [P, 3 * D], BF16))
            QT = E2(nc.sbuf_tensor("QT", [P, 2 * Dc * P], BF16))
            Pb = E2(nc.sbuf_tensor("Pb", [P, S + 2 * CW], BF16))
            PT = E2(nc.sbuf_tensor("PT", [P, 2 * 8 * P], BF16))
            negmax = E2(nc.sbuf_tensor("negmax", [P, 8], F32))
            lvec = E2(nc.sbuf_tensor("lvec", [P, 8], F32))
            cvec = E2(nc.sbuf_tensor("cvec", [P, 8], F32))
            cvec2 = E2(nc.sbuf_tensor("cvec2", [P, 4], F32))
            negM = E2(nc.sbuf_tensor("negM", [P, 1], F32))
            lsum = E2(nc.sbuf_tensor("lsum", [P, 2], F32))
            rinv = E2(nc.sbuf_tensor("rinv", [P, 2], F32))
            attb = E2(nc.sbuf_tensor("attb", [P, D], F32))
            lnb = E2(nc.sbuf_tensor("lnb", [P, D], F32))
            pS = [E2(nc.psum_tensor(f"pS{i}", [P, CW], F32)) for i in range(2)]
            pPT = [E2(nc.psum_tensor(f"pPT{i}", [P, CW], BF16)) for i in range(2)]
            pAV = E2(nc.psum_tensor("pAV", [P, D], F32))

            def pb_col(ss, off):
                return (ss % 2) * 2 * CW + off if off < 2 * CW else 2 * CW + off

            blk = E2(nc.Block())
            @blk.sync
            def _(sync):
                sync.dma_start(maskb[:, :], masks_in.ap()).then_inc(sems["c_bnd"], 16)
                sync.dma_start(tmaskb[:, :], tmask_in.ap()).then_inc(sems["c_bnd"], 16)
                sync.dma_start(x1bnd[:, :], x1bnd_in.ap()).then_inc(sems["c_bnd"], 16)
                for s in range(T_OWN):
                    if s >= 4:
                        sync.wait_ge(sems["c_x2"], sched[("x2", s - 4)])
                        sync.wait_ge(sems["c_sp" + str((s - 4) % 2)],
                                     sched[("spill", s - 4)])
                    sync.dma_start(x1q[:, (s % 4) * D:((s % 4) + 1) * D],
                                   xq_in.ap()[s * P:(s + 1) * P, :]
                                   ).then_inc(sems["c_xq" + str(s % 4)], 16)

            @blk.gpsimd
            def _(g):
                for s in range(T_OWN):
                    g.wait_ge(sems["c_x2"], sched[("x2", s)])
                    g.dma_start(x2_spill.ap()[s * P:(s + 1) * P, :],
                                x1q[:, (s % 4) * D:((s % 4) + 1) * D]
                                ).then_inc(sems["c_sp" + str(s % 2)], 16)

            @blk.vector
            def _(v):
                warm_v(v)
                v.wait_ge(sems["c_bnd"], sched[("mb",)])

                def qt_evac(ss):
                    v.wait_ge(sems["c_qtp"], sched[("qtp", ss)])
                    v.tensor_copy(QT[:, (ss % 2) * Dc * P:(ss % 2 + 1) * Dc * P],
                                  pPT[0][:, 0:Dc * P])
                    v.tensor_copy(spcv[:, 1:3], spcv[:, 1:3])
                    v.tensor_copy(spcv[:, 1:3], spcv[:, 1:3]
                                  ).then_inc(sems["c_qte"], 1)

                def mask_j(ss, j, off, w):
                    v.wait_ge(sems["c_qk"], sched[("qk", ss, j)])
                    if j == len(step_chunks(ss, CW)) - 1:
                        v.tensor_add(pS[j % 2][:, w - 256:w],
                                     pS[j % 2][:, w - 256:w],
                                     maskb[:, 0:256])
                    nb = (ss % 2) * 4
                    v.reduce_max(negmax[:, nb + j:nb + j + 1],
                                 pS[j % 2][:, 0:w], axis=AX.X, negate=True)
                    v.tensor_copy(spcv[:, 1:3], spcv[:, 1:3]
                                  ).then_inc(sems["c_max"], 1)

                def x2add(k):
                    v.wait_ge(sems["c_ep2"], sched[("ep2", k)])
                    v.tensor_add(x1q[:, (k % 4) * D:(k % 4 + 1) * D],
                                 x1q[:, (k % 4) * D:(k % 4 + 1) * D],
                                 lnb[:, :]).then_inc(sems["c_x2"], 1)

                def fixup(ss):
                    chunks = step_chunks(ss, CW)
                    n = len(chunks)
                    nb = (ss % 2) * 4
                    if n > 1:
                        v.wait_ge(sems["c_max"], sched[("max", ss, n - 1)])
                        v.tensor_copy(spcv[:, 3:4], epsv[:, 0:1])
                        v.tensor_reduce(negM[:, :], negmax[:, nb:nb + n],
                                        AX.X, ALU.min)
                        v.tensor_copy(spcv[:, 1:3], spcv[:, 1:3]
                                      ).then_inc(sems["c_fx1"], 1)

                def pscale(ss):
                    chunks = step_chunks(ss, CW)
                    n = len(chunks)
                    nb = (ss % 2) * 4
                    v.wait_ge(sems["c_cv"], sched[("cv", ss)])
                    if n > 1:
                        for j, (off, w) in enumerate(chunks):
                            c0 = pb_col(ss, off)
                            v.tensor_scalar_mul(Pb[:, c0:c0 + w],
                                                Pb[:, c0:c0 + w],
                                                cvec[:, nb + j:nb + j + 1])
                    v.reciprocal(rinv[:, (ss % 2):(ss % 2) + 1],
                                 lsum[:, (ss % 2):(ss % 2) + 1])
                    v.tensor_copy(spcv[:, 1:3], spcv[:, 1:3]
                                  ).then_inc(sems["c_pfx"], 1)

                qt_evac(0)
                for j, (off, w) in enumerate(step_chunks(0, CW)):
                    mask_j(0, j, off, w)
                fixup(0)
                pscale(0)
                for s in range(T_OWN):
                    nxt = step_chunks(s + 1, CW) if s + 1 < T_OWN else None
                    if nxt is not None:
                        qt_evac(s + 1)
                        for j in range(len(nxt)):
                            mask_j(s + 1, j, *nxt[j])
                        fixup(s + 1)
                    else:
                        v.wait_ge(sems["c_bnd"], sched[("bnd",)])
                        v.wait_ge(sems["c_qtp"], sched[("bqtp",)])
                        v.tensor_copy(QT[:, 0:Dc * M], pPT[0][:, 0:Dc * M])
                        v.tensor_copy(spcv[:, 1:3], spcv[:, 1:3])
                        v.tensor_copy(spcv[:, 1:3], spcv[:, 1:3]
                                      ).then_inc(sems["c_qte"], 1)
                        for j in range(NTJ):
                            v.wait_ge(sems["c_qk"], sched[("bqk", j)])
                            v.tensor_add(pS[j % 2][0:M, :], pS[j % 2][0:M, :],
                                         tmaskb[:, j * CW:(j + 1) * CW])
                            v.reduce_max(negmaxt[:, j:j + 1], pS[j % 2][0:M, :],
                                         axis=AX.X, negate=True)
                            v.tensor_copy(spcv[:, 1:3], spcv[:, 1:3]
                                          ).then_inc(sems["c_max"], 1)
                        v.wait_ge(sems["c_max"], sched[("bmax", NTJ - 1)])
                        v.tensor_copy(spcv[0:M, 3:4], epsv[0:M, 0:1])
                        v.tensor_reduce(negMt[:, :], negmaxt[:, 0:NTJ], AX.X, ALU.min)
                        v.tensor_copy(spcv[:, 1:3], spcv[:, 1:3]
                                      ).then_inc(sems["c_fx1"], 1)
                    nkb = 256 * (s + 1) // P
                    for g in range((nkb + 7) // 8):
                        kb0, kbn = g * 8, min(8, nkb - g * 8)
                        v.wait_ge(sems["c_ptp"], sched[("ptp", s, g)])
                        v.tensor_copy(PT[:, (g % 2) * 8 * P:(g % 2) * 8 * P + kbn * P],
                                      pPT[g % 2][:, 0:kbn * P])
                        v.tensor_copy(spcv[:, 1:3], spcv[:, 1:3]
                                      ).then_inc(sems["c_pte"], 1)
                    if nxt is not None:
                        pscale(s + 1)
                    else:
                        v.wait_ge(sems["c_cv"], sched[("bcv",)])
                        for j in range(NTJ):
                            v.tensor_scalar_mul(Pbt[:, j * CW:(j + 1) * CW],
                                                Pbt[:, j * CW:(j + 1) * CW],
                                                cvect[:, j:j + 1])
                        v.reciprocal(rinvt[:, :], lsumt[:, :])
                        v.tensor_copy(spcv[:, 1:3], spcv[:, 1:3]
                                      ).then_inc(sems["c_pfx"], 1)
                    if s >= 1:
                        x2add(s - 1)
                    v.wait_ge(sems["c_av"], sched[("av", s)])
                    v.scalar_tensor_tensor(attb[:, :], pAV[:, :],
                                           rinv[:, (s % 2):(s % 2) + 1],
                                           xn1q[:, (s % 3) * D:(s % 3 + 1) * D],
                                           ALU.mult, ALU.add
                                           ).then_inc(sems["c_attb"], 1)
                for g in range(NTG):
                    kb0, kbn = g * 8, min(8, ST - g * 8)
                    v.wait_ge(sems["c_ptp"], sched[("bptp", g)])
                    v.tensor_copy(PT[:, kb0 * M:(kb0 + kbn) * M],
                                  pPT[g % 2][:, 0:kbn * M])
                    v.tensor_copy(spcv[:, 1:3], spcv[:, 1:3]
                                  ).then_inc(sems["c_pte"], 1)
                x2add(T_OWN - 1)
                v.wait_ge(sems["c_av"], sched[("bav",)])
                v.scalar_tensor_tensor(attb[0:M, 0:D], pAV[0:M, :], rinvt[:, 0:1],
                                       xn1bnd[:, :], ALU.mult, ALU.add
                                       ).then_inc(sems["c_attb"], 1)
                v.wait_ge(sems["c_ep2"], sched[("bep2",)])
                v.tensor_add(x2bnd[:, :], x1bnd[:, :], lnb[0:M, 0:D]
                             ).then_inc(sems["c_x2"], 1)

            @blk.scalar
            def _(sc):
                warm_sc(sc)

                def emit_exps(ss):
                    chunks = step_chunks(ss, CW)
                    nb = (ss % 2) * 4
                    for j, (off, w) in enumerate(chunks):
                        if j == 2:
                            png = (256 * ss // P + 7) // 8
                            sc.wait_ge(sems["c_ptp"], sched[("ptp", ss - 1, png - 1)])
                        sc.wait_ge(sems["c_max"], sched[("max", ss, j)])
                        c0 = pb_col(ss, off)
                        sc.activation(Pb[:, c0:c0 + w],
                                      pS[j % 2][:, 0:w], AF.Exp,
                                      bias=negmax[:, nb + j:nb + j + 1],
                                      accum_out=lvec[:, nb + j:nb + j + 1]
                                      ).then_inc(sems["c_exp"], 1)

                def emit_cv(ss):
                    chunks = step_chunks(ss, CW)
                    n = len(chunks)
                    nb = (ss % 2) * 4
                    if n > 1:
                        sc.wait_ge(sems["c_fx1"], sched[("fx1", ss)])
                        sc.activation(cvec[:, nb:nb + n], negmax[:, nb:nb + n],
                                      AF.Exp, scale=-1.0, bias=negM[:, 0:1])
                        sc.activation(spca[:, 0:1], epsv[:, 0:1], AF.Identity)
                        for j in range(n):
                            sc.activation(cvec2[:, j:j + 1],
                                          lvec[:, nb + j:nb + j + 1],
                                          AF.Identity, scale=cvec[:, nb + j:nb + j + 1])
                        sc.activation(spca[:, 0:1], epsv[:, 0:1], AF.Identity)
                        sc.activation(spca[:, 0:1], epsv[:, 0:1], AF.Identity)
                        sc.activation(lsum[:, (ss % 2):(ss % 2) + 1], cvec2[:, 0:1],
                                      AF.Identity, bias=cvec2[:, 1:2])
                        for j in range(2, n):
                            sc.activation(lsum[:, (ss % 2):(ss % 2) + 1],
                                          lsum[:, (ss % 2):(ss % 2) + 1],
                                          AF.Identity, bias=cvec2[:, j:j + 1])
                        sc.activation(spca[:, 0:1], epsv[:, 0:1], AF.Identity
                                      ).then_inc(sems["c_cv"], 1)
                    else:
                        sc.activation(lsum[:, (ss % 2):(ss % 2) + 1],
                                      lvec[:, nb:nb + 1], AF.Identity)
                        sc.activation(spca[:, 0:1], epsv[:, 0:1], AF.Identity
                                      ).then_inc(sems["c_cv"], 1)

                def emit_xnq(ss):
                    if ss >= 3:
                        sc.wait_ge(sems["c_attb"], sched[("attb", ss - 3)])
                    sc.wait_ge(sems["c_xq" + str(ss % 4)], sched[("xq", ss)])
                    ln_act(sc, x1q[:, (ss % 4) * D:(ss % 4 + 1) * D],
                           xn1q[:, (ss % 3) * D:(ss % 3 + 1) * D],
                           "c_xnq", ("xnq", ss))

                emit_xnq(0)
                emit_exps(0)
                emit_cv(0)
                emit_xnq(1)
                for s in range(T_OWN):
                    if s + 1 < T_OWN:
                        emit_exps(s + 1)
                        emit_cv(s + 1)
                    else:
                        for j in range(NTJ):
                            sc.wait_ge(sems["c_max"], sched[("bmax", j)])
                            sc.activation(Pbt[:, j * CW:(j + 1) * CW],
                                          pS[j % 2][0:M, :], AF.Exp,
                                          bias=negmaxt[:, j:j + 1],
                                          accum_out=lvect[:, j:j + 1]
                                          ).then_inc(sems["c_exp"], 1)
                        sc.wait_ge(sems["c_fx1"], sched[("bfx1",)])
                        sc.activation(cvect[:, 0:NTJ], negmaxt[:, 0:NTJ], AF.Exp,
                                      scale=-1.0, bias=negMt[:, 0:1])
                        sc.activation(spca[0:M, 0:1], epsv[0:M, 0:1], AF.Identity)
                        for j in range(NTJ):
                            sc.activation(cvect2[:, j:j + 1], lvect[:, j:j + 1],
                                          AF.Identity, scale=cvect[:, j:j + 1])
                        sc.activation(spca[0:M, 0:1], epsv[0:M, 0:1], AF.Identity)
                        sc.activation(spca[0:M, 0:1], epsv[0:M, 0:1], AF.Identity)
                        sc.activation(lsumt[:, :], cvect2[:, 0:1], AF.Identity,
                                      bias=cvect2[:, 1:2])
                        for j in range(2, NTJ):
                            sc.activation(lsumt[:, :], lsumt[:, :], AF.Identity,
                                          bias=cvect2[:, j:j + 1])
                        sc.activation(spca[0:M, 0:1], epsv[0:M, 0:1], AF.Identity
                                      ).then_inc(sems["c_cv"], 1)
                    if s + 2 < T_OWN:
                        emit_xnq(s + 2)
                    elif s + 2 == T_OWN:
                        sc.wait_ge(sems["c_bnd"], sched[("bnd",)])
                        ln_act(sc, x1bnd[:, :], xn1bnd[:, :], "c_xnq", ("bxn",),
                               bufs=tbufs)
                    sc.wait_ge(sems["c_attb"], sched[("attb", s)])
                    if s >= 1:
                        sc.wait_ge(sems["c_x2"], sched[("x2", s - 1)])
                    ln_act(sc, attb[:, :], lnb[:, :], "c_ep2", ("ep2", s))
                sc.wait_ge(sems["c_attb"], sched[("battb",)])
                sc.wait_ge(sems["c_x2"], sched[("x2", T_OWN - 1)])
                ln_act(sc, attb[0:M, 0:D], lnb[0:M, 0:D], "c_ep2", ("bep2",),
                       bufs=tbufs)

            @blk.tensor
            def _(te):
                def emit_qt_t(ss):
                    te.wait_ge(sems["c_xnq"], sched[("xnq", ss)])
                    for cc in range(Dc):
                        ins = te.transpose(pPT[0][:, cc * P:(cc + 1) * P],
                                           xn1q[:, (ss % 3) * D + cc * P:
                                                (ss % 3) * D + (cc + 1) * P],
                                           ident[:, :])
                    ins.then_inc(sems["c_qtp"], 1)

                def emit_qk(ss, j, off, w):
                    pk = prev_ps_key(ss, j)
                    if pk is not None:
                        te.wait_ge(sems["c_exp"], sched[pk])
                    for cc in range(Dc):
                        for nn in range(0, w, 512):
                            nw = min(512, w - nn)
                            ins = te.matmul(
                                pS[j % 2][:, nn:nn + nw],
                                QT[:, (ss % 2) * Dc * P + cc * P:
                                   (ss % 2) * Dc * P + (cc + 1) * P],
                                KT[:, cc * S + off + nn: cc * S + off + nn + nw],
                                start=(cc == 0), stop=(cc == Dc - 1))
                    ins.then_inc(sems["c_qk"], 1)

                emit_qt_t(0)
                te.wait_ge(sems["c_qte"], sched[("qte", 0)])
                for j, (off, w) in enumerate(step_chunks(0, CW)):
                    emit_qk(0, j, off, w)
                for s in range(T_OWN):
                    if s + 1 < T_OWN:
                        emit_qt_t(s + 1)
                        te.wait_ge(sems["c_qte"], sched[("qte", s + 1)])
                        for j, (off, w) in enumerate(step_chunks(s + 1, CW)):
                            emit_qk(s + 1, j, off, w)
                    else:
                        te.wait_ge(sems["c_xnq"], sched[("bxn",)])
                        for cc in range(Dc):
                            ins = te.transpose(pPT[0][:, cc * M:(cc + 1) * M],
                                               xn1bnd[:, cc * P:(cc + 1) * P],
                                               ident[0:M, 0:M])
                        ins.then_inc(sems["c_qtp"], 1)
                        te.wait_ge(sems["c_qte"], sched[("bqte",)])
                        for j in range(NTJ):
                            if j >= 2:
                                te.wait_ge(sems["c_exp"], sched[("bexp", j - 2)])
                            else:
                                pk = prev_ps_key(T_OWN, j)
                                if pk is not None:
                                    te.wait_ge(sems["c_exp"], sched[pk])
                            for cc in range(Dc):
                                for nn in range(0, CW, 512):
                                    ins = te.matmul(
                                        pS[j % 2][0:M, nn:nn + 512],
                                        QT[:, cc * M:(cc + 1) * M],
                                        KT[:, cc * S + j * CW + nn:
                                           cc * S + j * CW + nn + 512],
                                        start=(cc == 0), stop=(cc == Dc - 1),
                                        skip_group_check=True)
                            ins.then_inc(sems["c_qk"], 1)
                    te.wait_ge(sems["c_pfx"], sched[("pfx", s)])
                    nkb = 256 * (s + 1) // P
                    ngr = (nkb + 7) // 8

                    def tp_group(g):
                        kb0, kbn = g * 8, min(8, nkb - g * 8)
                        if g >= 2:
                            te.wait_ge(sems["c_pte"], sched[("pte", s, g - 2)])
                        for b in range(kbn):
                            kb = kb0 + b
                            c0 = pb_col(s, kb * P)
                            ins = te.transpose(
                                pPT[g % 2][:, b * P:(b + 1) * P],
                                Pb[:, c0:c0 + P], ident[:, :])
                        ins.then_inc(sems["c_ptp"], 1)

                    tp_group(0)
                    if ngr > 1:
                        tp_group(1)
                    for g in range(ngr):
                        kb0, kbn = g * 8, min(8, nkb - g * 8)
                        te.wait_ge(sems["c_pte"], sched[("pte", s, g)])
                        if g == 0 and s >= 1:
                            te.wait_ge(sems["c_attb"], sched[("attb", s - 1)])
                        for b in range(kbn):
                            kb = kb0 + b
                            for h in range(NH):
                                ins = te.matmul(
                                    pAV[:, h * 512:(h + 1) * 512],
                                    PT[:, (g % 2) * 8 * P + b * P:
                                       (g % 2) * 8 * P + (b + 1) * P],
                                    V[:, kb * D + h * 512: kb * D + (h + 1) * 512],
                                    start=(kb == 0), stop=(kb == nkb - 1),
                                    skip_group_check=True)
                        if g + 2 < ngr:
                            tp_group(g + 2)
                    ins.then_inc(sems["c_av"], 1)

                te.wait_ge(sems["c_pfx"], sched[("bpfx",)])

                def ttp(g):
                    kb0, kbn = g * 8, min(8, ST - g * 8)
                    if g >= 2:
                        te.wait_ge(sems["c_pte"], sched[("bpte", g - 2)])
                    for b in range(kbn):
                        kb = kb0 + b
                        ins = te.transpose(pPT[g % 2][:, b * M:(b + 1) * M],
                                           Pbt[:, kb * P:(kb + 1) * P],
                                           ident[0:M, 0:M])
                    ins.then_inc(sems["c_ptp"], 1)

                ttp(0)
                if NTG > 1:
                    ttp(1)
                for g in range(NTG):
                    kb0, kbn = g * 8, min(8, ST - g * 8)
                    te.wait_ge(sems["c_pte"], sched[("bpte", g)])
                    if g == 0:
                        te.wait_ge(sems["c_attb"], sched[("attb", T_OWN - 1)])
                    for b in range(kbn):
                        kb = kb0 + b
                        for h in range(NH):
                            ins = te.matmul(
                                pAV[0:M, h * 512:(h + 1) * 512],
                                PT[:, kb * M:(kb + 1) * M],
                                V[:, kb * D + h * 512: kb * D + (h + 1) * 512],
                                start=(kb == 0), stop=(kb == ST - 1),
                                skip_group_check=True)
                    if g + 2 < NTG:
                        ttp(g + 2)
                ins.then_inc(sems["c_av"], 1)

            if dbg:
                @blk.gpsimd
                def _(g):
                    g.wait_ge(sems["c_x2"], sched[("x2", T_OWN - 1)])
                    g.dma_start(dbg_KT.ap(), KT[:, 0:S]).then_inc(sems["c_dbg"], 16)
                    g.dma_start(dbg_V.ap(), V[:, 0:D]).then_inc(sems["c_dbg"], 16)
                    g.dma_start(dbg_Pb.ap(), Pb[:, 0:S]).then_inc(sems["c_dbg"], 16)
                    g.dma_start(dbg_xn1q.ap(), xn1q[:, D:2 * D]).then_inc(sems["c_dbg"], 16)
                    g.dma_start(dbg_st.ap()[:, 0:4], negmax[:, 0:4]).then_inc(sems["c_dbg"], 16)
                    g.dma_start(dbg_st.ap()[:, 4:8], lvec[:, 0:4]).then_inc(sems["c_dbg"], 16)
                    g.dma_start(dbg_st.ap()[:, 8:12], cvec[:, 0:4]).then_inc(sems["c_dbg"], 16)
                    g.dma_start(dbg_st.ap()[:, 12:13], rinv[:, 0:1], allow_slow_non_contiguous=True).then_inc(sems["c_dbg"], 16)
                    g.dma_start(dbg_st.ap()[:, 13:14], lsum[:, 0:1], allow_slow_non_contiguous=True).then_inc(sems["c_dbg"], 16)
                    g.wait_ge(sems["c_dbg"], 16 * 9)

        actx.close()

        FW = F_TPC * P
        with contextlib.ExitStack() as p3:
            E4 = p3.enter_context
            WR = E4(nc.sbuf_tensor("WR", [P, Dc * D], BF16))
            WK2 = E4(nc.sbuf_tensor("WK2", [P, Dc * D], BF16))
            WV2 = E4(nc.sbuf_tensor("WV2", [P, Dc * D], BF16))
            x2r = E4(nc.sbuf_tensor("x2r", [P, 2 * F_TPC * D], F32))
            xn2t = E4(nc.sbuf_tensor("xn2t", [P, 2 * D], BF16))
            xn2bndT = E4(nc.sbuf_tensor("xn2bndT", [P, Dc * M], BF16))
            XN2T = E4(nc.sbuf_tensor("XN2T", [P, Dc * FW], BF16))
            XN2ST = E4(nc.sbuf_tensor("XN2ST", [P, Dc * FW], BF16))
            XRT = E4(nc.sbuf_tensor("XRT", [P, Dc * FW], BF16))
            XKT = E4(nc.sbuf_tensor("XKT", [P, Dc * FW], BF16))
            rbuf = E4(nc.sbuf_tensor("rbuf", [P, F_TPC * D], BF16))
            kactT = E4(nc.sbuf_tensor("kactT", [P, Dc * FW], BF16))
            ffob = E4(nc.sbuf_tensor("ffob", [P, 4 * D], F32))
            lnb3 = E4(nc.sbuf_tensor("lnb3", [P, 2 * D], F32))
            outb = E4(nc.sbuf_tensor("outb", [P, 2 * D], F32))
            xn2bnd = E4(nc.sbuf_tensor("xn2bnd", [M, D], BF16))
            ssumt2 = E4(nc.sbuf_tensor("ssumt2", [M, 1], F32))
            ssqt2 = E4(nc.sbuf_tensor("ssqt2", [M, 1], F32))
            varept2 = E4(nc.sbuf_tensor("varept2", [M, 1], F32))
            lnvt2 = E4(nc.sbuf_tensor("lnvt2", [M, 1], F32))
            negmeant2 = E4(nc.sbuf_tensor("negmeant2", [M, 1], F32))
            rstdt2 = E4(nc.sbuf_tensor("rstdt2", [M, 1], F32))
            nmrt2 = E4(nc.sbuf_tensor("nmrt2", [M, 1], F32))
            sqv = E4(nc.sbuf_tensor("sqv", [P, D], F32))
            sqg = E4(nc.sbuf_tensor("sqg", [P, D], F32))
            lnS = E4(nc.sbuf_tensor("lnS", [P, 8], F32))
            eS = E4(nc.sbuf_tensor("eS", [P, 8], F32))
            zerov = E4(nc.sbuf_tensor("zerov", [P, 1], F32))
            pscr = E4(nc.sbuf_tensor("pscr", [P, 32], F32))
            pX2T = [E4(nc.psum_tensor(f"pX2T{i}", [P, Dc * P], BF16)) for i in range(2)]
            pRV = [E4(nc.psum_tensor(f"pRV{i}", [P, D], F32)) for i in range(2)]
            pKA = [E4(nc.psum_tensor(f"pKA{i}", [P, 512], F32)) for i in range(2)]
            blk = E4(nc.Block())
            @blk.gpsimd
            def _(g):
                g.memset(pscr[:, :], 0.0)
                g.dma_start(WR[:, :].rearrange("p (c w) -> p c w", c=Dc),
                            wr_t.ap().rearrange("(c p) w -> p c w", p=P)
                            ).then_inc(sems["c_w2"], 16)
                g.dma_start(rb_scr.ap()[:, 96:128], WR[:, 0:32]
                            ).then_inc(sems["c_w2"], 16)
                g.dma_start(WK2[:, :].rearrange("p (c w) -> p c w", c=Dc),
                            wk_t.ap().rearrange("(c p) w -> p c w", p=P)
                            ).then_inc(sems["c_w2"], 16)
                g.dma_start(rb_scr.ap()[:, 64:96], WK2[:, 0:32]
                            ).then_inc(sems["c_w2"], 16)
                g.dma_start(WV2[:, :].rearrange("p (c w) -> p c w", c=Dc),
                            wv_t.ap().rearrange("(c p) w -> p c w", p=P)
                            ).then_inc(sems["c_w2"], 16)
                g.dma_start(rb_scr.ap()[:, 32:64], WV2[:, 0:32]
                            ).then_inc(sems["c_w2"], 16)
                def emit_outb(u, t):
                    idx = u * F_TPC + t
                    g.wait_ge(sems["c_oe2"], sched[("f_oe2", u, t)])
                    if idx >= 2:
                        pu, pt = divmod(idx - 2, F_TPC)
                        g.wait_ge(sems["c_out" + str(pt % 2)],
                                  sched[("f_out", pu, pt)])
                    g.tensor_add(outb[:, (t % 2) * D:((t % 2) + 1) * D],
                                 x2r[:, (u % 2) * F_TPC * D + t * D:
                                     (u % 2) * F_TPC * D + (t + 1) * D],
                                 lnb3[:, (idx % 2) * D:((idx % 2) + 1) * D]
                                 ).then_inc(sems["c_oe3"], 1)
                    g.tensor_copy(pscr[:, 0:16], pscr[:, 16:32])
                    s_idx = u * F_TPC + t
                    g.dma_start(out.ap()[s_idx * P:(s_idx + 1) * P, :],
                                outb[:, (t % 2) * D:((t % 2) + 1) * D]
                                ).then_inc(sems["c_out" + str(t % 2)], 16)

                for u in range(NFC):
                    for t in range(F_TPC):
                        if u >= 1:
                            emit_outb(u - 1, t)
                for t in range(F_TPC):
                    emit_outb(NFC - 1, t)
                g.wait_ge(sems["c_out0"], C["c_out0"])
                g.wait_ge(sems["c_out1"], C["c_out1"])

            @blk.sync
            def _(sync):
                sync.wait_ge(sems["c_sp0"], C["c_sp0"])
                sync.wait_ge(sems["c_sp1"], C["c_sp1"])
                for u in range(NFC):
                    if u >= 2:
                        sync.wait_ge(sems["c_oe3"], sched[("f_oe3", u - 2, F_TPC - 1)])
                    src = x2_spill.ap()[u * FW:(u + 1) * FW, :].rearrange(
                        "(t p) d -> p t d", p=P)
                    dst = x2r[:, (u % 2) * F_TPC * D:((u % 2) + 1) * F_TPC * D
                              ].rearrange("p (t d) -> p t d", t=F_TPC)
                    sync.dma_start(dst, src).then_inc(sems["c_x2r" + str(u % 2)], 16)

            @blk.vector
            def _(v):
                warm_v(v)
                v.memset(zerov[:, :], 0.0)
                v.wait_ge(sems["c_qtp"], sched[("b2tp",)])
                v.tensor_copy(xn2bndT[:, :], pX2T[0][:, 0:Dc * M])
                v.tensor_copy(spcv[:, 1:3], spcv[:, 1:3]).then_inc(sems["c_qte"], 1)

                def part1(u, t):
                    idx = u * F_TPC + t
                    if idx >= 4:
                        pu, pt = divmod(idx - 4, F_TPC)
                        v.wait_ge(sems["c_xn"], sched[("f_xn", pu, pt)])
                    v.wait_ge(sems["c_x2r" + str(u % 2)], sched[("x2r", u)])
                    xt = x2r[:, (u % 2) * F_TPC * D + t * D:
                             (u % 2) * F_TPC * D + (t + 1) * D]
                    c0 = (idx % 4) * 2
                    v.tensor_mul(sqv[:, :], xt, xt)
                    v.reduce_sum(lnS[:, c0:c0 + 1], xt, axis=AX.X)
                    v.reduce_sum(lnS[:, c0 + 1:c0 + 2], sqv[:, :], axis=AX.X)
                    v.tensor_copy(spcv[:, 1:3], spcv[:, 1:3]
                                  ).then_inc(sems["c_lnst"], 1)

                def xevac(u, t):
                    idx = u * F_TPC + t
                    v.wait_ge(sems["c_xtp"], sched[("f_xtp", u, t)])
                    dst = XN2T.rearrange("p (c w) -> p c w", c=Dc)[:, :, t * P:(t + 1) * P]
                    src = pX2T[t % 2][:, :].rearrange("p (c q) -> p c q", c=Dc)
                    v.tensor_copy(dst, src)
                    v.tensor_copy(spcv[:, 1:3], spcv[:, 1:3]
                                  ).then_inc(sems["c_xte"], 1)
                    if t == 0 and u >= 1:
                        v.wait_ge(sems["c_rmm"], sched[("f_rmm", u - 1, F_TPC - 1)])
                        v.wait_ge(sems["c_kamm"], sched[("f_kamm", u - 1, Dc - 1)])
                    s4 = XN2T.rearrange("p (c t q) -> p c t q", c=Dc, t=F_TPC)
                    v4 = XN2ST.rearrange("p (c t q) -> p c t q", c=Dc, t=F_TPC)
                    if t == 0:
                        b4 = xn2bndT.rearrange("p (c m o) -> p c m o", c=Dc, o=1)
                        v.tensor_copy(v4[:, :, :, 0:1],
                                      b4[:, :, u * F_TPC:(u + 1) * F_TPC, :])
                    v.tensor_copy(v4[:, :, t:t + 1, 1:P], s4[:, :, t:t + 1, 0:P - 1])
                    for cc in range(Dc):
                        sl = slice(cc * FW + t * P, cc * FW + (t + 1) * P)
                        v.tensor_sub(XRT[:, sl], XN2T[:, sl], XN2ST[:, sl])
                    for cc in range(Dc):
                        sl = slice(cc * FW + t * P, cc * FW + (t + 1) * P)
                        v.scalar_tensor_tensor(XKT[:, sl], XRT[:, sl],
                                               mu[:, Dc + cc:Dc + cc + 1],
                                               XN2ST[:, sl], ALU.mult, ALU.add)
                    for cc in range(Dc):
                        sl = slice(cc * FW + t * P, cc * FW + (t + 1) * P)
                        v.scalar_tensor_tensor(XRT[:, sl], XRT[:, sl],
                                               mu[:, cc:cc + 1],
                                               XN2ST[:, sl], ALU.mult, ALU.add)
                    v.tensor_copy(spcv[:, 1:3], spcv[:, 1:3])
                    v.tensor_copy(spcv[:, 1:3], spcv[:, 1:3]).then_inc(sems["c_mix"], 1)

                def ffob_mul(u, t):
                    idx = u * F_TPC + t
                    v.wait_ge(sems["c_omm"], sched[("f_omm", u, t)])
                    if idx >= 4:
                        pu, pt = divmod(idx - 4, F_TPC)
                        v.wait_ge(sems["c_oe2"], sched[("f_oe2", pu, pt)])
                    fsl = ffob[:, (idx % 4) * D:((idx % 4) + 1) * D]
                    v.tensor_mul(fsl, pRV[t % 2][:, :], rbuf[:, t * D:(t + 1) * D])
                    v.tensor_copy(spcv[:, 1:3], spcv[:, 1:3]
                                  ).then_inc(sems["c_ffob"], 1)

                def est(u, t):
                    idx = u * F_TPC + t
                    fsl = ffob[:, (idx % 4) * D:((idx % 4) + 1) * D]
                    v.tensor_mul(sqg[:, :], fsl, fsl)
                    v.reduce_sum(eS[:, (idx % 4) * 2:(idx % 4) * 2 + 1], fsl,
                                 axis=AX.X)
                    v.reduce_sum(eS[:, (idx % 4) * 2 + 1:(idx % 4) * 2 + 2],
                                 sqg[:, :], axis=AX.X)
                    v.tensor_copy(spcv[:, 1:3], spcv[:, 1:3])
                    v.tensor_copy(spcv[:, 1:3], spcv[:, 1:3]
                                  ).then_inc(sems["c_est"], 1)

                for t in range(F_TPC):
                    part1(0, t)
                for u in range(NFC):
                    if u >= 1:
                        ffob_mul(u - 1, 0)
                        est(u - 1, 0)
                        ffob_mul(u - 1, 1)
                        est(u - 1, 1)
                    xevac(u, 0)
                    if u >= 1:
                        ffob_mul(u - 1, 2)
                        est(u - 1, 2)
                    xevac(u, 1)
                    if u >= 1:
                        ffob_mul(u - 1, 3)
                        est(u - 1, 3)
                    xevac(u, 2)
                    xevac(u, 3)
                    if u + 1 < NFC:
                        for t in range(F_TPC):
                            part1(u + 1, t)
                for t in range(F_TPC):
                    ffob_mul(NFC - 1, t)
                    est(NFC - 1, t)

            @blk.scalar
            def _(sc):
                warm_sc(sc)
                sc.wait_ge(sems["c_x2"], sched[("bx2",)])
                ln_act(sc, x2bnd[:, :], xn2bnd[:, :], "c_xnq", ("b2xn",),
                       bufs=(ssumt2, ssqt2, varept2, lnvt2, negmeant2, rstdt2,
                             nmrt2), zsc=zmask[:, 0:1])
                for u in range(NFC):
                    for t in range(F_TPC):
                        idx = u * F_TPC + t
                        if idx > 1:
                            pu, pt = divmod(idx - 2, F_TPC)
                            sc.wait_ge(sems["c_xtp"], sched[("f_xtp", pu, pt)])
                        sc.wait_ge(sems["c_lnst"], sched[("f_st", u, t)])
                        xt = x2r[:, (u % 2) * F_TPC * D + t * D:
                                 (u % 2) * F_TPC * D + (t + 1) * D]
                        c0 = (idx % 4) * 2
                        ln_act(sc, xt, xn2t[:, (idx % 2) * D:((idx % 2) + 1) * D],
                               "c_xn", ("f_xn", u, t),
                               stats=(lnS[:, c0:c0 + 1], lnS[:, c0 + 1:c0 + 2]))
                    def emit_epi(uu):
                        for t in range(F_TPC):
                            idx = uu * F_TPC + t
                            sc.wait_ge(sems["c_est"], sched[("f_et", uu, t)])
                            if idx >= 2:
                                pu, pt = divmod(idx - 2, F_TPC)
                                sc.wait_ge(sems["c_oe3"], sched[("f_oe3", pu, pt)])
                            ln_act(sc, ffob[:, (idx % 4) * D:((idx % 4) + 1) * D],
                                   lnb3[:, (idx % 2) * D:((idx % 2) + 1) * D],
                                   "c_oe2", ("f_oe2", uu, t),
                                   stats=(eS[:, (idx % 4) * 2:(idx % 4) * 2 + 1],
                                          eS[:, (idx % 4) * 2 + 1:(idx % 4) * 2 + 2]))

                    if u >= 1:
                        emit_epi(u - 1)
                    for t in range(F_TPC):
                        sc.wait_ge(sems["c_rmm"], sched[("f_rmm", u, t)])
                        if u >= 1:
                            sc.wait_ge(sems["c_ffob"], sched[("ffob", u - 1, t)])
                        sc.activation(rbuf[:, t * D:(t + 1) * D], pRV[t % 2][:, :],
                                      AF.Sigmoid)
                        sc.activation(spca[:, 0:1], epsv[:, 0:1], AF.Identity
                                      ).then_inc(sems["c_rev"], 1)
                    for dkc in range(Dc):
                        sc.wait_ge(sems["c_kamm"], sched[("f_kamm", u, dkc)])
                        sc.activation(pKA[dkc % 2][:, :], pKA[dkc % 2][:, :], AF.Relu)
                        sc.activation(kactT[:, dkc * FW:(dkc + 1) * FW],
                                      pKA[dkc % 2][:, :], AF.Square)
                        sc.activation(spca[:, 0:1], epsv[:, 0:1], AF.Identity)
                        sc.activation(spca[:, 0:1], epsv[:, 0:1], AF.Identity
                                      ).then_inc(sems["c_kaev"], 1)
                    if u == NFC - 1:
                        emit_epi(u)

            @blk.tensor
            def _(te):
                te.wait_ge(sems["c_xnq"], sched[("b2xn",)])
                for cc in range(Dc):
                    ins = te.transpose(pX2T[0][:, cc * M:(cc + 1) * M],
                                       xn2bnd[:, cc * P:(cc + 1) * P], ident[0:M, 0:M])
                ins.then_inc(sems["c_qtp"], 1)
                te.wait_ge(sems["c_qte"], sched[("b2te",)])
                for u in range(NFC):

                    def tp_tile(t):
                        idx = u * F_TPC + t
                        te.wait_ge(sems["c_xn"], sched[("f_xn", u, t)])
                        if idx >= 2:
                            pu, pt = divmod(idx - 2, F_TPC)
                            te.wait_ge(sems["c_xte"], sched[("f_xte", pu, pt)])
                        for cc in range(Dc):
                            ins = te.transpose(
                                pX2T[t % 2][:, cc * P:(cc + 1) * P],
                                xn2t[:, (idx % 2) * D + cc * P:
                                     (idx % 2) * D + (cc + 1) * P], ident[:, :])
                        ins.then_inc(sems["c_xtp"], 1)

                    def rmm(t):
                        seq = u * 8 + t
                        if u == 0 and t == 0:
                            te.wait_ge(sems["c_w2"], sched[("wr",)])
                        te.wait_ge(sems["c_mix"], sched[("f_mix", u, t)])
                        if seq >= 2:
                            pu2, sl = divmod(seq - 2, 8)
                            if sl < F_TPC:
                                te.wait_ge(sems["c_rev"], sched[("f_rev", pu2, sl)])
                            else:
                                te.wait_ge(sems["c_ffob"],
                                           sched[("ffob", pu2, sl - F_TPC)])
                        for h in range(NH):
                            for cc in range(Dc):
                                ins = te.matmul(
                                    pRV[t % 2][:, h * 512:(h + 1) * 512],
                                    XRT[:, cc * FW + t * P: cc * FW + (t + 1) * P],
                                    WR[:, cc * D + h * 512: cc * D + (h + 1) * 512],
                                    start=(cc == 0), stop=(cc == Dc - 1))
                        ins.then_inc(sems["c_rmm"], 1)

                    tp_tile(0)
                    tp_tile(1)
                    rmm(0)
                    tp_tile(2)
                    rmm(1)
                    tp_tile(3)
                    rmm(2)
                    rmm(3)
                    te.wait_ge(sems["c_mix"], sched[("f_mix", u, F_TPC - 1)])
                    if u == 0:
                        te.wait_ge(sems["c_w2"], sched[("wk2",)])
                    for dkc in range(Dc):
                        idx = u * Dc + dkc
                        if idx >= 2:
                            pu, pd = divmod(idx - 2, Dc)
                            te.wait_ge(sems["c_kaev"], sched[("f_kaev", pu, pd)])
                        for cc in range(Dc):
                            ins = te.matmul(
                                pKA[dkc % 2][:, :],
                                WK2[:, cc * D + dkc * P: cc * D + (dkc + 1) * P],
                                XKT[:, cc * FW:(cc + 1) * FW],
                                start=(cc == 0), stop=(cc == Dc - 1))
                        ins.then_inc(sems["c_kamm"], 1)
                    te.wait_ge(sems["c_kaev"], sched[("f_kaev", u, Dc - 1)])
                    if u == 0:
                        te.wait_ge(sems["c_w2"], sched[("wv2",)])
                    for t in range(F_TPC):
                        te.wait_ge(sems["c_rev"], sched[("f_rev", u, t)])
                        pu2, sl = divmod(u * 8 + 4 + t - 2, 8)
                        if sl < F_TPC:
                            te.wait_ge(sems["c_rev"], sched[("f_rev", pu2, sl)])
                        else:
                            te.wait_ge(sems["c_ffob"], sched[("ffob", pu2, sl - F_TPC)])
                        for h in range(NH):
                            for dkc in range(Dc):
                                ins = te.matmul(
                                    pRV[t % 2][:, h * 512:(h + 1) * 512],
                                    kactT[:, dkc * FW + t * P: dkc * FW + (t + 1) * P],
                                    WV2[:, dkc * D + h * 512: dkc * D + (h + 1) * 512],
                                    start=(dkc == 0), stop=(dkc == Dc - 1))
                        ins.then_inc(sems["c_omm"], 1)

    nc.finalize()
    return nc


def make_in_maps(x_full, W, Wr, Wk, Wv, mur, muk, S=4096, D=1024, n_cores=8):
    T_OWN = S // 256
    wkv_t = np.ascontiguousarray(W.T)
    wr_t = np.ascontiguousarray(Wr.T)
    wk_t = np.ascontiguousarray(Wk.T)
    wv_t = np.ascontiguousarray(Wv.T)
    Dc = D // P
    mu = np.ascontiguousarray(
        np.concatenate([mur.reshape(Dc, P).T, muk.reshape(Dc, P).T], axis=1))

    tri = np.where(np.arange(P)[:, None] >= np.arange(P)[None, :], 0.0, NEG
                   ).astype(ml_dtypes.bfloat16)

    in_maps = []
    for c in range(n_cores):
        b, r = divmod(c, 2)
        xb = np.ascontiguousarray(x_full[b])
        masks = np.zeros((P, 256), ml_dtypes.bfloat16)
        if r == 0:
            masks[:, 0:128] = tri
            masks[:, 128:256] = ml_dtypes.bfloat16(NEG)
        else:
            masks[:, 128:256] = tri
        xq = np.empty((T_OWN * P, D), np.float32)
        x1bnd = np.zeros((T_OWN, D), np.float32)
        zmask = np.zeros((T_OWN, 1), np.float32)
        tmask = np.full((T_OWN, S), NEG, ml_dtypes.bfloat16)
        for s in range(T_OWN):
            p = 2 * s + r
            xq[s * P:(s + 1) * P] = xb[p * P:(p + 1) * P]
            bnd = p * P - 1
            if bnd >= 0:
                x1bnd[s] = xb[bnd]
                zmask[s] = 1.0
                tmask[s, :bnd + 1] = 0.0
            else:
                tmask[s, 0] = 0.0
        in_maps.append({
            "x": xb, "xq_in": xq,
            "wkv_t": wkv_t, "wr_t": wr_t, "wk_t": wk_t, "wv_t": wv_t,
            "mu_in": mu, "masks_in": masks, "tmask_in": tmask,
            "x1bnd_in": x1bnd, "zmask_in": zmask,
        })
    return in_maps


def assemble_out(results, S=4096, D=1024, n_cores=8):
    T_OWN = S // 256
    out_full = np.empty((n_cores // 2, S, D), np.float32)
    for c in range(n_cores):
        b, r = divmod(c, 2)
        o = results[c]["out"]
        for s in range(T_OWN):
            p = 2 * s + r
            out_full[b, p * P:(p + 1) * P, :] = o[s * P:(s + 1) * P, :]
    return out_full


def kernel(L0_attn_x, L0_attn_W, L0_ffn_W_r, L0_ffn_W_k, L0_ffn_W_v,
           L0_ffn_mu_r, L0_ffn_mu_k):
    S, D = 4096, 1024
    x_full = np.ascontiguousarray(np.asarray(L0_attn_x, dtype=np.float32))
    in_maps = make_in_maps(
        x_full,
        np.asarray(L0_attn_W, np.float32),
        np.asarray(L0_ffn_W_r, np.float32),
        np.asarray(L0_ffn_W_k, np.float32),
        np.asarray(L0_ffn_W_v, np.float32),
        np.asarray(L0_ffn_mu_r, np.float32),
        np.asarray(L0_ffn_mu_k, np.float32),
        S=S, D=D)
    nc = build_program(S, D)
    res = run_bass_kernel_spmd(nc, in_maps, core_ids=list(range(8)))
    return assemble_out(res.results, S=S, D=D)
